# version 10
# baseline (speedup 1.0000x reference)
import numpy as np
import ml_dtypes
import concourse.bass as bass
import concourse.mybir as mybir
from concourse.bass_utils import run_bass_kernel_spmd

F32 = mybir.dt.float32
BF16 = mybir.dt.bfloat16
AF = mybir.ActivationFunctionType
OP = mybir.AluOpType
AX = mybir.AxisListType.X

L = 8192
DM = 1024
DI = 2048
DQ = 512
NDC = 4
DS = 16
RK = 64
SC = 1024
NSC = 8
TPC = 2048
QT = 256
FF = 4096
MAGIC = float(np.float32(12582912.0))
WCLIP = float(np.float32(1.49999988))
EPS = 1e-5
RG = [[0, 1, 2, 3], [4, 5, 6, 7]]

DEBUG_TAPS = {}
DEBUG_DUMP = []
POST_LIMIT = None
DEBUG_SWEEP = []

LD_SEMS = ("lw", "l0", "l1", "lb0", "lb1", "lc0", "lc1", "ldt", "lxp", "lry",
           "lg0", "lg1", "lu0", "lu1", "lgu", "lwd0", "lwd1")
ST_SEMS = ("sa", "sp0", "sp1", "sg", "so", "sq", "sd")
DMA_SEMS = frozenset(LD_SEMS) | frozenset(ST_SEMS)


class Walk:

    def __init__(self, name, eng, sems):
        self.name = name
        self.eng = eng
        self.sems = sems
        self.c = {k: 0 for k in sems}

    def E(self, engname, fn, inc=None, waits=()):
        if engname == self.name:
            for s, v in waits:
                if s in DMA_SEMS and v > 0:
                    v = self.c[s]
                if v > 0:
                    self.eng.wait_ge(self.sems[s], v)
            inst = fn(self.eng)
        else:
            inst = None
        if inc is not None:
            s, a = inc
            self.c[s] += a
            if inst is not None:
                inst.then_inc(self.sems[s], a)
            return self.c[s]
        return None


class Arena:
    def __init__(self, ap):
        self.base = ap
        self.n = ap.shape[1]
        self.off = 0
        self.marks = []
        self.peak = 0

    def alloc(self, cols, tag=""):
        cols = (cols + 15) // 16 * 16
        a = self.base[:, self.off:self.off + cols]
        self.off += cols
        self.peak = max(self.peak, self.off)
        assert self.off <= self.n, f"arena overflow at {tag}: {self.off}>{self.n}"
        return a

    def alloc_bf16(self, cols_bf16, tag=""):
        return self.alloc((cols_bf16 + 1) // 2, tag).bitcast(BF16)

    def push(self):
        self.marks.append(self.off)

    def pop(self):
        self.off = self.marks.pop()


class MK:
    def __init__(self):
        nc = bass.Bass("TRN2", target_bir_lowering=False)
        self.nc = nc
        d = self.d = {}
        d['xT'] = nc.dram_tensor("xT", [DM, L], F32, kind="ExternalInput").ap()
        d['xTf'] = nc.dram_tensor("xTf", [DM, L], F32, kind="ExternalInput").ap()
        for i, dn in enumerate("fb"):
            d[f'win{i}'] = nc.dram_tensor(f"win_{dn}", [DM, 2 * DQ], F32, kind="ExternalInput").ap()
            d[f'wx{i}'] = nc.dram_tensor(f"wx_{dn}", [DQ, 96], F32, kind="ExternalInput").ap()
            d[f'wdt{i}'] = nc.dram_tensor(f"wdt_{dn}", [RK, DQ], F32, kind="ExternalInput").ap()
            d[f'wout{i}'] = nc.dram_tensor(f"wout_{dn}", [DQ, DM], F32, kind="ExternalInput").ap()
            d[f'vec{i}'] = nc.dram_tensor(f"vec_{dn}", [128, 92], F32, kind="ExternalInput").ap()
        d['ident'] = nc.dram_tensor("ident", [128, 128], F32, kind="ExternalInput").ap()
        d['ffw1'] = nc.dram_tensor("ffw1", [DM, DI], F32, kind="ExternalInput").ap()
        d['ffw2'] = nc.dram_tensor("ffw2", [DI, DM], F32, kind="ExternalInput").ap()
        d['pvec'] = nc.dram_tensor("pvec", [128, 64], F32, kind="ExternalInput").ap()
        d['wgT'] = nc.dram_tensor("wgT", [DM, FF], F32, kind="ExternalInput").ap()
        d['wuT'] = nc.dram_tensor("wuT", [DM, FF], F32, kind="ExternalInput").ap()
        d['wdT'] = nc.dram_tensor("wdT", [FF, DM], F32, kind="ExternalInput").ap()
        d['xpost'] = nc.dram_tensor("xpost", [DM, TPC], F32, kind="ExternalInput").ap()
        d['out'] = nc.dram_tensor("out", [TPC, DM], F32, kind="ExternalOutput").ap()
        d['arin'] = nc.dram_tensor("arin", [2, NSC, 96, SC], F32, kind="Internal").ap()
        d['arout'] = nc.dram_tensor("arout", [2, NSC, 96, SC], F32, kind="Internal").ap()
        d['rsin'] = nc.dram_tensor("rsin", [NSC, 4, 2, 8, 128, QT], F32, kind="Internal").ap()
        d['rsout'] = nc.dram_tensor("rsout", [NSC, 2, 8, 128, QT], F32, kind="Internal").ap()
        d['wgq'] = nc.dram_tensor("wgq", [8, 128, FF], BF16, kind="Internal").ap()
        d['wuq'] = nc.dram_tensor("wuq", [8, 128, FF], BF16, kind="Internal").ap()
        d['wdq'] = nc.dram_tensor("wdq", [32, 128, DM], BF16, kind="Internal").ap()
        d['gud'] = nc.dram_tensor("gud", [4, 4, 8, 128, 512], F32, kind="Internal").ap()
        d['x1d'] = nc.dram_tensor("x1d", [4, 8, 128, 512], F32, kind="Internal").ap()
        d['attnd'] = nc.dram_tensor("attnd", [4, 8, 128, 512], F32, kind="Internal").ap()
        d['yd'] = nc.dram_tensor("yd", [4, 8, 128, 512], F32, kind="Internal").ap()
        d['qtmd'] = nc.dram_tensor("qtmd", [4, 4, 128, 1024], BF16, kind="Internal").ap()
        d['xqd'] = nc.dram_tensor("xqd", [4, 8, 128, 512], BF16, kind="Internal").ap()
        d['mxd'] = nc.dram_tensor("mxd", [4, 128, 64], F32, kind="Internal").ap()
        d['srow'] = nc.dram_tensor("srow", [128, 8], F32, kind="Internal").ap()
        if 'u' in DEBUG_SWEEP:
            d['ud'] = nc.dram_tensor("ud", [2, NSC, 128, 4096], F32,
                                     kind="ExternalOutput").ap()
        if 'xib' in DEBUG_SWEEP:
            d['xibd'] = nc.dram_tensor("xibd", [2, NSC, 128, 4160], F32,
                                       kind="ExternalOutput").ap()
        if 'xbt' in DEBUG_SWEEP:
            d['xbtd'] = nc.dram_tensor("xbtd", [2, NSC, 2, 128, 4096], BF16,
                                       kind="ExternalOutput").ap()
        self.taps = {}
        for name, shape in DEBUG_TAPS.items():
            self.taps[name] = nc.dram_tensor("tap_" + name, list(shape), F32,
                                             kind="ExternalOutput").ap()
        self.dumps = {}
        for name, key, idx in DEBUG_DUMP:
            src = d[key] if idx is None else d[key][idx]
            dt_ = src.dtype
            self.dumps[name] = (nc.dram_tensor("dump_" + name, list(src.shape), dt_,
                                               kind="ExternalOutput").ap(), src)

        ARN = 53184
        sbuf = nc.alloc_sbuf_tensor("arena", [128, ARN], F32).ap()
        A = self.A = Arena(sbuf)
        y = self.l = {}
        y['identf'] = A.alloc(128, 'identf')
        y['identb'] = A.alloc_bf16(128, 'identb')
        mis = A.alloc(16, 'mis')
        y['onescol'] = mis[:, 0:1]
        y['c127'] = mis[:, 1:2]
        orow = A.alloc(128, 'onesrow')
        y['onesrow128'] = orow[0:1, 0:128]
        y['vecs'] = A.alloc(2 * 96, 'vecs')
        y['pvec'] = A.alloc(64, 'pvec')
        y['carry'] = A.alloc(64, 'carry')
        y['qsc'] = A.alloc(16, 'qsc')
        y['rowt'] = A.alloc(16, 'rowt')

        A.push()
        y['stage'] = A.alloc(2 * 1024, 'stage')
        y['wzb'] = A.alloc_bf16(8 * 1024, 'wzb')
        y['woutb'] = A.alloc_bf16(4 * 1024, 'woutb')
        y['wxf'] = A.alloc(4 * 96, 'wxf')
        y['wdtf'] = A.alloc(4 * 128, 'wdtf')
        y['xbt'] = A.alloc_bf16(2 * 8 * 512, 'xbt')
        y['xib'] = A.alloc(NDC * 1040, 'xib')
        y['ubuf'] = A.alloc(NDC * 1024, 'ubuf')
        y['szbuf'] = A.alloc(NDC * 1024, 'szbuf')
        y['dtbuf'] = A.alloc(NDC * 1024, 'dtbuf')
        y['dtubuf'] = A.alloc(NDC * 1024, 'dtubuf')
        y['cvtmp'] = A.alloc(1024, 'cvtmp')
        y['dttmp'] = A.alloc(512, 'dttmp')
        y['abuf'] = A.alloc(2 * 1024, 'abuf')
        y['bbuf'] = A.alloc(2 * 1024, 'bbuf')
        y['hbuf'] = A.alloc(2 * 1024, 'hbuf')
        y['brep'] = A.alloc(2 * 1024, 'brep')
        y['crep'] = A.alloc(2 * 1024, 'crep')
        y['zbuf'] = A.alloc_bf16(2 * 1024, 'zbuf')
        y['y1buf'] = y['cvtmp']
        y['yszb'] = A.alloc_bf16(NDC * 1024, 'yszb')
        y['dtin'] = A.alloc(1024, 'dtin')
        y['xdbcv'] = A.alloc(1024, 'xdbcv')
        y['pstage'] = A.alloc(2 * 512, 'pstage')
        self.MB_PEAK = A.off
        A.pop()

        A.push()
        y['p_rsy'] = A.alloc(8 * 512, 'p_rsy')
        y['p_xp'] = A.alloc(8 * 512, 'p_xp')
        y['p_B'] = A.alloc(8 * 512, 'p_B')
        y['p_C'] = A.alloc(8 * 512, 'p_C')
        y['p_D'] = A.alloc(8 * 512, 'p_D')
        yb_base = A.alloc(2048, 'p_ybf')
        y['p_ybf'] = yb_base.bitcast(BF16)
        y['p_qtm'] = yb_base[:, 0:512].bitcast(BF16)
        y['p_s2rep'] = yb_base[:, 512:1024]
        y['p_outst'] = yb_base[:, 1024:2048]
        y['p_mid'] = A.alloc_bf16(16 * 512, 'p_mid')
        y['p_xq'] = A.alloc_bf16(8 * 512, 'p_xq')
        y['p_wt'] = A.alloc_bf16(2 * 1024, 'p_wt')
        y['p_ws'] = A.alloc_bf16(2 * 8 * 512, 'p_ws')
        y['p_ws2'] = A.alloc_bf16(2 * 8 * 512, 'p_ws2')
        y['p_stg'] = A.alloc(2 * 1024, 'p_stg')
        y['p_gu'] = A.alloc(2048, 'p_gu')
        y['p_gqt'] = A.alloc_bf16(32 * 128, 'p_gqt')
        y['p_wd'] = A.alloc_bf16(2 * 32 * 128, 'p_wd')
        y['p_chunk'] = A.alloc(512, 'p_chunk')
        y['p_chunk2'] = A.alloc(512, 'p_chunk2')
        y['p_rows'] = A.alloc(3 * 512, 'p_rows')
        y['p_mx'] = A.alloc(64, 'p_mx')
        self.PO_PEAK = A.off
        A.pop()

        A.push()
        y['q_st'] = A.alloc(2 * 4096, 'q_st')
        y['q_bf'] = A.alloc_bf16(2 * 4096, 'q_bf')
        y['q_acc'] = A.alloc(48, 'q_acc')
        self.QU_PEAK = A.off
        A.pop()

        self.pb = [nc.alloc_psum_tensor(f"pb{i}", [128, 512], F32).ap() for i in range(8)]
        self.pbh = [p.bitcast(BF16) for p in self.pb]

    def dma_ld(self, out, in_, sem, waits=()):
        return self.w.E("sync", lambda e: e.dma_start(out=out, in_=in_),
                        inc=(sem, 16), waits=waits)

    def dma_st(self, out, in_, sem, waits=()):
        return self.w.E("gpsimd", lambda e: e.dma_start(out=out, in_=in_),
                        inc=(sem, 16), waits=waits)

    def vgap(self):
        self.w.E("vector", lambda e: e.nop(cycle_cnt=640))

    def bank_wait(self, i):
        return [self.bankfree[i]] if self.bankfree[i][1] > 0 else []

    def barrier(self):
        w = self.w
        marks = dict(w.c)
        dma_wt = [(sn, marks[sn]) for sn in LD_SEMS + ST_SEMS if marks[sn] > 0]
        for eng in ("vector", "scalar", "tensor", "gpsimd", "sync"):
            wt = [(sn, marks[sn]) for sn in ("v", "s", "t", "g")]
            sem = {"vector": "v", "scalar": "s", "tensor": "t",
                   "gpsimd": "g", "sync": "g"}[eng]
            wt = [x for x in wt if x[0] != sem or eng in ("sync",)]
            w.E(eng, lambda e: e.nop(),
                inc=({"vector": "v", "scalar": "s", "tensor": "t"}.get(eng, "g"), 1)
                if eng in ("vector", "scalar", "tensor") else None,
                waits=wt + dma_wt)

    def tap(self, name, src):
        if name in self.taps:
            tgt = self.taps[name]
            w = self.w
            self.dma_st(tgt[0:src.shape[0], :], src, "sd",
                        waits=[("v", w.c["v"]), ("s", w.c["s"]), ("t", w.c["t"])])

    def ph_consts(self):
        w, l, d = self.w, self.l, self.d
        E = w.E
        t_id = self.dma_ld(l['identf'][:, :], d['ident'][:, :], "lw")
        t_v0 = self.dma_ld(l['vecs'][:, 0:92], d['vec0'][:, :], "lw")
        t_v1 = self.dma_ld(l['vecs'][:, 96:188], d['vec1'][:, :], "lw")
        self.t_pv = self.dma_ld(l['pvec'][:, :], d['pvec'][:, :], "lw")
        E("vector", lambda e: e.memset(l['onescol'], 1.0))
        E("vector", lambda e: e.memset(l['c127'], 127.0))
        E("vector", lambda e: e.memset(l['onesrow128'], 1.0))
        E("vector", lambda e: e.tensor_copy(l['identb'][:, :], l['identf'][:, :]),
          waits=[("lw", t_id)])
        for dirn in range(2):
            vv = l['vecs'][:, 96 * dirn + 28:96 * dirn + 92]
            t_e = E("scalar", lambda e: e.activation(out=vv, in_=vv, func=AF.Exp),
                    inc=("s", 1), waits=[("lw", max(t_v0, t_v1))])
            E("vector", lambda e: e.tensor_scalar(vv, vv, -1.0, None, op0=OP.mult),
              waits=[("s", t_e)])
        E("vector", lambda e: e.nop(), inc=("v", 1))

    def ph_wquant(self, dsrc, ddst, nchunk, width, slot):
        w, l = self.w, self.l
        E = w.E
        qst, qbf, qacc = l['q_st'], l['q_bf'], l['q_acc']
        abs_marks = []
        for c in range(nchunk):
            stg = qst[:, (c % 2) * 4096:(c % 2) * 4096 + width]
            wt = [("s", abs_marks[c - 2])] if c >= 2 else []
            if self.qst_free[c % 2]:
                wt = wt + [("v", self.qst_free[c % 2])]
            tl = self.dma_ld(stg, dsrc[c * 128:(c + 1) * 128, :], "l%d" % (c % 2),
                             waits=wt)
            m = E("scalar", lambda e: e.activation(
                out=stg, in_=stg, func=AF.Abs, accum_out=qacc[:, c:c + 1]),
                inc=("s", 1), waits=[("l%d" % (c % 2), tl)])
            abs_marks.append(m)
        t_sum = E("vector", lambda e: e.tensor_reduce(
            qacc[:, 36:37], qacc[:, 0:nchunk], axis=AX, op=OP.add),
            inc=("v", 1), waits=[("s", abs_marks[-1])])
        t_ps = E("tensor", lambda e: e.matmul(self.pb[6][0:1, 0:1], l['onescol'],
                                              qacc[:, 36:37], start=True, stop=True),
                 inc=("t", 1), waits=[("v", t_sum)] + self.bank_wait(6))
        t_mn = E("vector", lambda e: e.tensor_scalar(
            qacc[0:1, 37:38], self.pb[6][0:1, 0:1], 1.0 / (nchunk * 128 * width), 1e-5,
            op0=OP.mult, op1=OP.max), inc=("v", 1), waits=[("t", t_ps)])
        t_rep = E("tensor", lambda e: e.matmul(self.pb[6][:, 1:2], l['onesrow128'],
                                               qacc[0:1, 37:38], start=True, stop=True),
                  inc=("t", 1), waits=[("v", t_mn)])
        t_sinv = E("vector", lambda e: e.tensor_copy(l['qsc'][:, slot:slot + 1],
                                                     self.pb[6][:, 1:2]),
                   inc=("v", 1), waits=[("t", t_rep)])
        self.bankfree[6] = ("v", t_sinv)
        self.vgap()
        t_s = E("vector", lambda e: e.reciprocal(
            l['qsc'][:, 3 + slot:4 + slot], l['qsc'][:, slot:slot + 1]),
            inc=("v", 1), waits=[("v", t_sinv)])
        self.vgap()
        st_marks = []
        tq_marks = [0, 0]
        for c in range(nchunk):
            stg = qst[:, (c % 2) * 4096:(c % 2) * 4096 + width]
            qb = qbf[:, (c % 2) * 4096:(c % 2) * 4096 + width]
            wt = [("s", abs_marks[-1])]
            if c >= 2:
                wt = [("sq", st_marks[c - 2]), ("v", tq_marks[c % 2])]
            tl = self.dma_ld(stg, dsrc[c * 128:(c + 1) * 128, :], "l%d" % (c % 2),
                             waits=wt)
            f1 = E("vector", lambda e: e.tensor_scalar(
                stg, stg, l['qsc'][:, 3 + slot:4 + slot], WCLIP, op0=OP.mult, op1=OP.min),
                inc=("v", 1), waits=[("l%d" % (c % 2), tl), ("v", t_s)])
            f2 = E("vector", lambda e: e.tensor_scalar(
                stg, stg, -WCLIP, MAGIC, op0=OP.max, op1=OP.add), inc=("v", 1),
                waits=[("v", f1)])
            t_q = E("vector", lambda e: e.tensor_scalar(
                qb, stg, MAGIC, None, op0=OP.subtract), inc=("v", 1), waits=[("v", f2)])
            tq_marks[c % 2] = t_q
            sm = self.dma_st(ddst[c, :, :], qb, "sq", waits=[("v", t_q)])
            st_marks.append(sm)
        self.qst_free = list(tq_marks)

    def ph_sweep(self, dirn):
        w, l, d = self.w, self.l, self.d
        E = w.E
        xT = d['xT'] if dirn == 0 else d['xTf']
        stage = l['stage']

        cast_marks = []

        def ldcast(dst_bf, src_ap):
            buf = len(cast_marks) % 2
            stg = stage[:, buf * 1024:buf * 1024 + 1024]
            wt = [("v", self.stg_free[buf])] if self.stg_free[buf] else []
            tl = self.dma_ld(stg, src_ap, "l%d" % buf, waits=wt)
            m = E("vector", lambda e: e.tensor_copy(dst_bf, stg), inc=("v", 1),
                  waits=[("l%d" % buf, tl)])
            self.stg_free[buf] = m
            cast_marks.append(m)

        for kc in range(8):
            ldcast(l['wzb'][:, kc * 1024:(kc + 1) * 1024],
                   d[f'win{dirn}'][kc * 128:(kc + 1) * 128, :])
        for dc in range(NDC):
            ldcast(l['woutb'][:, dc * 1024:(dc + 1) * 1024],
                   d[f'wout{dirn}'][dc * 128:(dc + 1) * 128, :])
        t_wx = self.dma_ld(l['wxf'][:, 0:4 * 96],
                           d[f'wx{dirn}'][:, :].rearrange("(dc p) r -> p dc r", p=128),
                           "lw")
        t_wdt = self.dma_ld(l['wdtf'][0:RK, :], d[f'wdt{dirn}'][:, :], "lw")
        self.st_sweep = dict(t_wcast=cast_marks[-1], t_wx=t_wx, t_wdt=t_wdt)

        E("vector", lambda e: e.memset(l['carry'][:, 0:64], 0.0))
        for dc in range(NDC):
            E("vector", lambda e: e.memset(
                l['xib'][:, dc * 1040 + 13:dc * 1040 + 16], 0.0))
        E("vector", lambda e: e.nop(), inc=("v", 1))

        for j in range(NSC):
            self.ph_A(dirn, j, xT)
            self.ph_B(dirn, j)

    def ph_A(self, dirn, j, xT):
        w, l, d, pb = self.w, self.l, self.d, self.pb
        E = w.E
        vo = 96 * dirn
        t0 = j * SC
        st = self.st_sweep
        u_marks = []
        sz_marks = []
        for ts in range(2):
            tbuf = ts % 2
            xc_marks = []
            for kc in range(8):
                stg = l['stage'][:, (kc % 2) * 1024:(kc % 2) * 1024 + 512]
                wt = [self.xbt_free[tbuf]] if kc < 2 and self.xbt_free[tbuf][1] > 0 else []
                if kc < 2 and self.dbg_xbt_st[tbuf]:
                    wt = wt + [("sd", self.dbg_xbt_st[tbuf])]
                if self.stg_free[kc % 2]:
                    wt = wt + [("v", self.stg_free[kc % 2])]
                tl = self.dma_ld(stg, xT[kc * 128:(kc + 1) * 128,
                                         t0 + ts * 512:t0 + (ts + 1) * 512],
                                 "l%d" % (kc % 2), waits=wt)
                m = E("vector", lambda e: e.tensor_copy(
                    l['xbt'][:, (tbuf * 8 + kc) * 512:(tbuf * 8 + kc + 1) * 512], stg),
                    inc=("v", 1), waits=[("l%d" % (kc % 2), tl)])
                self.stg_free[kc % 2] = m
                xc_marks.append(m)
            t_xcast = xc_marks[-1]
            if 'xbt' in DEBUG_SWEEP:
                self.dbg_xbt_st[tbuf] = self.dma_st(
                    d['xbtd'][dirn, j, ts],
                    l['xbt'][:, tbuf * 4096:(tbuf + 1) * 4096], "sd",
                    waits=[("v", t_xcast)])

            mm_last = 0
            for mchunk in range(8):
                bk = mchunk % 2
                for kc in range(8):
                    wt = []
                    if kc == 0:
                        wt = [("v", t_xcast), ("v", st['t_wcast'])] + self.bank_wait(bk)
                    mm_last = E("tensor", lambda e: e.matmul(
                        pb[bk][:, :],
                        l['wzb'][:, kc * 1024 + mchunk * 128:kc * 1024 + (mchunk + 1) * 128],
                        l['xbt'][:, (tbuf * 8 + kc) * 512:(tbuf * 8 + kc + 1) * 512],
                        start=(kc == 0), stop=(kc == 7)), inc=("t", 1), waits=wt)
                if mchunk < 4:
                    dc = mchunk
                    wtx = [("t", mm_last)]
                    if ts == 0 and mchunk == 0 and self.dbg_xib_st:
                        wtx.append(("sd", self.dbg_xib_st))
                    m = E("vector", lambda e: e.tensor_copy(
                        l['xib'][:, dc * 1040 + 16 + ts * 512:dc * 1040 + 16 + (ts + 1) * 512],
                        pb[bk][:, :]), inc=("v", 1), waits=wtx)
                    self.bankfree[bk] = ("v", m)
                else:
                    dc = mchunk - 4
                    wt = [("t", mm_last)]
                    if self.sz_free[1] > 0:
                        wt.append(self.sz_free)
                    m = E("scalar", lambda e: e.activation(
                        out=l['szbuf'][:, dc * 1024 + ts * 512:dc * 1024 + (ts + 1) * 512],
                        in_=pb[bk][:, :], func=AF.Silu), inc=("s", 1), waits=wt)
                    sz_marks.append(m)
                    self.bankfree[bk] = ("s", m)
            self.xbt_free[tbuf] = ("t", mm_last)

        cw = l['vecs']
        for dc in range(NDC):
            base = dc * 1040 + 16
            wt0 = []
            if dc > 0:
                wt0 = [("s", u_marks[dc - 1])]
            elif self.cv_free[1] > 0:
                wt0 = [self.cv_free]
            t_cv = E("vector", lambda e: e.tensor_scalar(
                l['cvtmp'][:, 0:1024], l['xib'][:, base - 3:base - 3 + 1024],
                cw[:, vo + dc * 4 + 0:vo + dc * 4 + 1],
                cw[:, vo + 16 + dc:vo + 16 + dc + 1], op0=OP.mult, op1=OP.add),
                inc=("v", 1), waits=wt0)
            for k in range(1, 4):
                t_cv = E("vector", lambda e: e.scalar_tensor_tensor(
                    l['cvtmp'][:, 0:1024], l['xib'][:, base - (3 - k):base - (3 - k) + 1024],
                    cw[:, vo + dc * 4 + k:vo + dc * 4 + k + 1],
                    l['cvtmp'][:, 0:1024], op0=OP.mult, op1=OP.add), inc=("v", 1))
            wtu = [("v", t_cv)]
            if self.u_free[1] > 0 and dc == 0:
                wtu.append(self.u_free)
            if dc == 0 and self.dbg_u_st:
                wtu.append(("sd", self.dbg_u_st))
            m = E("scalar", lambda e: e.activation(
                out=l['ubuf'][:, dc * 1024:(dc + 1) * 1024], in_=l['cvtmp'][:, 0:1024],
                func=AF.Silu), inc=("s", 1), waits=wtu)
            u_marks.append(m)
            E("vector", lambda e: e.tensor_copy(
                l['xib'][:, dc * 1040 + 13:dc * 1040 + 16],
                l['xib'][:, base + 1021:base + 1024]))
        self.cv_free = ("s", u_marks[-1])
        self.u_ready = u_marks
        self.sz_ready = sz_marks[-1]
        if 'u' in DEBUG_SWEEP:
            self.dbg_u_st = self.dma_st(d['ud'][dirn, j], l['ubuf'][:, 0:4096],
                                        "sd", waits=[("s", u_marks[-1])])
        if 'xib' in DEBUG_SWEEP:
            self.dbg_xib_st = self.dma_st(d['xibd'][dirn, j], l['xib'][:, 0:4160],
                                          "sd", waits=[("s", u_marks[-1])])

        ev_marks = []
        for ts in range(2):
            mm = 0
            for dc in range(NDC):
                wt = [("s", u_marks[dc])]
                if dc == 0:
                    wt += self.bank_wait(2) + [("lw", st['t_wx'])]
                mm = E("tensor", lambda e: e.matmul(
                    pb[2][0:96, :], l['wxf'][:, dc * 96:(dc + 1) * 96],
                    l['ubuf'][:, dc * 1024 + ts * 512:dc * 1024 + (ts + 1) * 512],
                    start=(dc == 0), stop=(dc == 3)), inc=("t", 1), waits=wt)
            wt = [("t", mm)]
            if self.xdbc_free[1] > 0 and ts == 0:
                wt.append(self.xdbc_free)
            m = E("vector", lambda e: e.tensor_copy(
                l['xdbcv'][0:96, ts * 512:(ts + 1) * 512], pb[2][0:96, :]),
                inc=("v", 1), waits=wt)
            self.bankfree[2] = ("v", m)
            ev_marks.append(m)
        t_st = self.dma_st(d['arin'][dirn, j], l['xdbcv'][0:96, 0:1024], "sa",
                           waits=[("v", ev_marks[-1])])
        self.xdbc_free = ("sa", t_st)
        t_ar = E("gpsimd", lambda e: e.collective_compute(
            "AllReduce", OP.add, replica_groups=RG,
            ins=[d['arin'][dirn, j]], outs=[d['arout'][dirn, j]]),
            inc=("g", 1), waits=[("sa", t_st)])
        self.ar_done = t_ar

    def ph_B(self, dirn, j):
        w, l, d, pb = self.w, self.l, self.d, self.pb
        E = w.E
        vo = 96 * dirn
        t_ar = self.ar_done
        st = self.st_sweep

        wt = [("g", t_ar)]
        if self.dtin_free[1] > 0:
            wt.append(self.dtin_free)
        t_dtin = self.dma_ld(l['dtin'][0:RK, 0:1024], d['arout'][dirn, j][0:RK, :],
                             "ldt", waits=wt)
        dt_marks = {}
        last_mm = 0
        for dc in range(NDC):
            for ts in range(2):
                wt = [("ldt", t_dtin), ("lw", st['t_wdt'])] + self.bank_wait(3)
                if self.dt_free[1] > 0 and dc == 0 and ts == 0:
                    wt.append(self.dt_free)
                last_mm = E("tensor", lambda e: e.matmul(
                    pb[3][:, :], l['wdtf'][0:RK, dc * 128:(dc + 1) * 128],
                    l['dtin'][0:RK, ts * 512:(ts + 1) * 512],
                    start=True, stop=True), inc=("t", 1), waits=wt)
                t_e = E("scalar", lambda e: e.activation(
                    out=l['dttmp'][:, 0:512], in_=pb[3][:, :], func=AF.Exp,
                    bias=l['vecs'][:, vo + 20 + dc:vo + 20 + dc + 1]),
                    inc=("s", 1), waits=[("t", last_mm)])
                self.bankfree[3] = ("s", t_e)
                m = E("scalar", lambda e: e.activation(
                    out=l['dtbuf'][:, dc * 1024 + ts * 512:dc * 1024 + (ts + 1) * 512],
                    in_=l['dttmp'][:, 0:512], func=AF.Ln, bias=1.0), inc=("s", 1))
            dt_marks[dc] = m
        self.dtin_free = ("t", last_mm)

        dtu_marks = {}
        for dc in range(NDC):
            m = E("vector", lambda e: e.tensor_tensor(
                l['dtubuf'][:, dc * 1024:(dc + 1) * 1024],
                l['dtbuf'][:, dc * 1024:(dc + 1) * 1024],
                l['ubuf'][:, dc * 1024:(dc + 1) * 1024], op=OP.mult),
                inc=("v", 1), waits=[("s", dt_marks[dc]), ("s", self.u_ready[dc])])
            dtu_marks[dc] = m

        scan_slots = {}
        bfree = {}
        cfree = {}
        last_y1 = 0
        ysz_marks = {}
        for dcp in range(2):
            dcs = [dcp * 2, dcp * 2 + 1]
            for s in range(DS):
                slot = s % 2
                wtb = [("g", t_ar)]
                if (dcp, s - 2) in bfree:
                    wtb.append(("v", bfree[(dcp, s - 2)]))
                elif dcp > 0 and (0, s + DS - 2) in bfree:
                    wtb.append(("v", bfree[(0, s + DS - 2)]))
                t_bb = self.dma_ld(l['brep'][:, slot * 1024:(slot + 1) * 1024],
                                   d['arout'][dirn, j][RK + s:RK + s + 1, :]
                                   .partition_broadcast(128).squeeze(1),
                                   "lb%d" % slot, waits=wtb)
                wtc = []
                if (dcp, s - 2) in cfree:
                    wtc.append(("v", cfree[(dcp, s - 2)]))
                elif dcp > 0 and (0, s + DS - 2) in cfree:
                    wtc.append(("v", cfree[(0, s + DS - 2)]))
                t_cb = self.dma_ld(l['crep'][:, slot * 1024:(slot + 1) * 1024],
                                   d['arout'][dirn, j][RK + DS + s:RK + DS + s + 1, :]
                                   .partition_broadcast(128).squeeze(1),
                                   "lc%d" % slot, waits=wtc)
                for di, dc in enumerate(dcs):
                    aslot = (s * 2 + di) % 2
                    cidx = s * 4 + dc
                    prev = scan_slots.get(aslot, (0, 0))
                    t_a = E("scalar", lambda e: e.activation(
                        out=l['abuf'][:, aslot * 1024:(aslot + 1) * 1024],
                        in_=l['dtbuf'][:, dc * 1024:(dc + 1) * 1024], func=AF.Exp,
                        scale=l['vecs'][:, vo + 28 + dc * 16 + s:vo + 28 + dc * 16 + s + 1]),
                        inc=("s", 1),
                        waits=[("s", dt_marks[dc])] + ([("v", prev[0])] if prev[0] else []))
                    t_b = E("vector", lambda e: e.tensor_tensor(
                        l['bbuf'][:, aslot * 1024:(aslot + 1) * 1024],
                        l['dtubuf'][:, dc * 1024:(dc + 1) * 1024],
                        l['brep'][:, slot * 1024:(slot + 1) * 1024], op=OP.mult),
                        inc=("v", 1), waits=[("lb%d" % slot, t_bb), ("v", dtu_marks[dc])])
                    t_h = E("vector", lambda e: e.tensor_tensor_scan(
                        l['hbuf'][:, aslot * 1024:(aslot + 1) * 1024],
                        l['abuf'][:, aslot * 1024:(aslot + 1) * 1024],
                        l['bbuf'][:, aslot * 1024:(aslot + 1) * 1024],
                        l['carry'][:, cidx:cidx + 1], op0=OP.mult, op1=OP.add),
                        inc=("v", 1), waits=[("s", t_a)])
                    t_z = E("vector", lambda e: e.tensor_tensor(
                        l['zbuf'][:, aslot * 1024:(aslot + 1) * 1024],
                        l['hbuf'][:, aslot * 1024:(aslot + 1) * 1024],
                        l['crep'][:, slot * 1024:(slot + 1) * 1024], op=OP.mult),
                        inc=("v", 1),
                        waits=[("lc%d" % slot, t_cb)] + ([("t", prev[1])] if prev[1] else []))
                    E("vector", lambda e: e.tensor_copy(
                        l['carry'][:, cidx:cidx + 1],
                        l['hbuf'][:, aslot * 1024 + 1023:aslot * 1024 + 1024]))
                    if di == 1:
                        bfree[(dcp, s)] = t_b
                        cfree[(dcp, s)] = t_z
                    t_id = 0
                    for half in range(2):
                        bk = 4 + di * 2 + half
                        wt3 = [("v", t_z)]
                        if s == 0:
                            wt3 += self.bank_wait(bk)
                        t_id = E("tensor", lambda e: e.matmul(
                            pb[bk][:, :], l['identb'][:, :],
                            l['zbuf'][:, aslot * 1024 + half * 512:aslot * 1024 + (half + 1) * 512],
                            start=(s == 0), stop=(s == DS - 1),
                            skip_group_check=True), inc=("t", 1), waits=wt3)
                    scan_slots[aslot] = (t_z, t_id)
            t_idm = w.c["t"]
            for di, dc in enumerate(dcs):
                for half in range(2):
                    bk = 4 + di * 2 + half
                    wty = [("t", t_idm)]
                    if dcp == 0 and di == 0 and half == 0 and self.cv_free[1] > 0:
                        wty.append(self.cv_free)
                    last_y1 = E("vector", lambda e: e.scalar_tensor_tensor(
                        l['y1buf'][:, half * 512:(half + 1) * 512],
                        l['ubuf'][:, dc * 1024 + half * 512:dc * 1024 + (half + 1) * 512],
                        l['vecs'][:, vo + 24 + dc:vo + 24 + dc + 1],
                        pb[bk][:, :], op0=OP.mult, op1=OP.add),
                        inc=("v", 1), waits=wty)
                    self.bankfree[bk] = ("v", last_y1)
                if dirn == 0:
                    outap = l['yszb'][:, dc * 1024:(dc + 1) * 1024]
                else:
                    outap = l['yszb'][:, dc * 1024:(dc + 1) * 1024][:, ::-1]
                wt = [("s", self.sz_ready)]
                if self.ysz_free[1] > 0 and dcp == 0 and di == 0:
                    wt.append(self.ysz_free)
                ysz_marks[dc] = E("vector", lambda e: e.tensor_tensor(
                    outap, l['y1buf'][:, 0:1024],
                    l['szbuf'][:, dc * 1024:(dc + 1) * 1024], op=OP.mult),
                    inc=("v", 1), waits=wt)
        self.u_free = ("v", last_y1)
        self.sz_free = ("v", max(ysz_marks.values()))
        self.dt_free = ("v", max(dtu_marks.values()))
        self.cv_free = ("v", max(ysz_marks.values()))
        t_scan_done = max(ysz_marks.values())

        jj = j if dirn == 0 else NSC - 1 - j
        self.rs_ready.setdefault(jj, [])
        ps_marks = list(self.ps_marks)
        last_wout = 0
        for half in range(2):
            for mc in range(8):
                bk = mc % 2
                mm = 0
                for dc in range(NDC):
                    wt = []
                    if dc == 0:
                        wt = [("v", t_scan_done)] + self.bank_wait(bk)
                    mm = E("tensor", lambda e: e.matmul(
                        pb[bk][:, :],
                        l['woutb'][:, dc * 1024 + mc * 128:dc * 1024 + (mc + 1) * 128],
                        l['yszb'][:, dc * 1024 + half * 512:dc * 1024 + (half + 1) * 512],
                        start=(dc == 0), stop=(dc == 3)), inc=("t", 1), waits=wt)
                last_wout = mm
                pslot = mc % 2
                wtp = [("sp%d" % pslot, ps_marks[pslot])] if ps_marks[pslot] > 0 else []
                t_ev = E("scalar", lambda e: e.copy(
                    out=l['pstage'][:, pslot * 512:(pslot + 1) * 512], in_=pb[bk][:, :]),
                    inc=("s", 1), waits=[("t", mm)] + wtp)
                self.bankfree[bk] = ("s", t_ev)
                t_ps = self.dma_st(
                    d['rsin'][jj, 2 * half:2 * half + 2, dirn, mc, :, :]
                    .rearrange("q p i -> p q i"),
                    l['pstage'][:, pslot * 512:(pslot + 1) * 512]
                    .rearrange("p (q i) -> p q i", q=2),
                    "sp%d" % pslot, waits=[("s", t_ev)])
                ps_marks[pslot] = t_ps
        self.ps_marks = ps_marks
        self.ysz_free = ("t", last_wout)

        if dirn == 1:
            t_rs = E("gpsimd", lambda e: e.collective_compute(
                "ReduceScatter", OP.add, replica_groups=RG,
                ins=[d['rsin'][jj]], outs=[d['rsout'][jj]]),
                inc=("g", 1),
                waits=[("sp0", w.c["sp0"]), ("sp1", w.c["sp1"])])
            self.rs_done[jj] = t_rs

    def ln(self, src, dst, wcol=None, bcol=None, ready=()):
        w, l, pb = self.w, self.l, self.pb
        E = w.E
        mm_q = 0
        for mc in range(8):
            wt = list(ready) if mc == 0 else [("t", mm_q)]
            t_sq = E("scalar", lambda e: e.activation(
                out=l['p_chunk'][:, :], in_=src(mc), func=AF.Square), inc=("s", 1),
                waits=wt)
            wts = (list(ready) + self.bank_wait(2) + self.bank_wait(3)) if mc == 0 else []
            mm_s = E("tensor", lambda e: e.matmul(
                pb[2][0:1, :], l['onescol'], src(mc), start=(mc == 0), stop=(mc == 7),
                skip_group_check=True), inc=("t", 1), waits=wts)
            mm_q = E("tensor", lambda e: e.matmul(
                pb[3][0:1, :], l['onescol'], l['p_chunk'][:, :], start=(mc == 0),
                stop=(mc == 7), skip_group_check=True), inc=("t", 1), waits=[("s", t_sq)])
        r = l['p_rows']
        r_mu, r_m2, r_t = (r[0:1, i * 512:(i + 1) * 512] for i in range(3))
        r_r = r_m2
        t1 = E("vector", lambda e: e.tensor_scalar(r_mu, pb[2][0:1, :], 1.0 / DM, None,
                                                   op0=OP.mult), inc=("v", 1),
               waits=[("t", mm_q)])
        E("vector", lambda e: e.tensor_scalar(r_m2, pb[3][0:1, :], 1.0 / DM, None,
                                              op0=OP.mult))
        E("vector", lambda e: e.tensor_tensor(r_t, r_mu, r_mu, op=OP.mult))
        E("vector", lambda e: e.tensor_tensor(r_m2, r_m2, r_t, op=OP.subtract))
        t3 = E("vector", lambda e: e.tensor_scalar(r_m2, r_m2, EPS, None, op0=OP.add),
               inc=("v", 1))
        E("scalar", lambda e: e.activation(out=r_t, in_=r_m2, func=AF.Ln),
          waits=[("v", t3)])
        t5 = E("scalar", lambda e: e.activation(out=r_r, in_=r_t, func=AF.Exp, scale=-0.5),
               inc=("s", 1))
        mrep = E("tensor", lambda e: e.matmul(pb[2][:, :], l['onesrow128'], r_mu,
                                              start=True, stop=True), inc=("t", 1),
                 waits=[("v", t1)])
        rrep = E("tensor", lambda e: e.matmul(pb[3][:, :], l['onesrow128'], r_r,
                                              start=True, stop=True), inc=("t", 1),
                 waits=[("s", t5)])
        last = 0
        for mc in range(8):
            E("vector", lambda e: e.tensor_tensor(
                l['p_chunk'][:, :], src(mc), pb[2][:, :], op=OP.subtract),
              waits=([("t", rrep)] if mc == 0 else []))
            E("vector", lambda e: e.tensor_tensor(
                l['p_chunk'][:, :], l['p_chunk'][:, :], pb[3][:, :], op=OP.mult))
            if wcol is not None:
                last = E("vector", lambda e: e.tensor_scalar(
                    dst(mc), l['p_chunk'][:, :], wcol(mc), bcol(mc),
                    op0=OP.mult, op1=OP.add), inc=("v", 1))
            else:
                last = E("vector", lambda e: e.tensor_copy(dst(mc), l['p_chunk'][:, :]),
                         inc=("v", 1))
        self.bankfree[2] = ("v", last)
        self.bankfree[3] = ("v", last)
        return last

    def ph_post(self):
        for ts in range(4):
            self.post_ts(ts)

    def post_ts(self, ts):
        w, l, d, pb = self.w, self.l, self.d, self.pb
        E = w.E
        pv = l['pvec']
        CH = lambda buf, mc: l[buf][:, mc * 512:(mc + 1) * 512]
        j0, j1 = 2 * ts, 2 * ts + 1
        t_xp = self.dma_ld(l['p_xp'][:, :],
                           d['xpost'][:, ts * 512:(ts + 1) * 512]
                           .rearrange("(kc p) t -> p kc t", p=128),
                           "lxp", waits=[("v", self.xp_free)] if self.xp_free else [])
        t_sum = 0
        for dirn in range(2):
            ld_m = 0
            for mc in range(8):
                for jb, j in enumerate((j0, j1)):
                    wt = []
                    if mc == 0 and jb == 0:
                        wt = [("g", self.rs_done[j0]), ("g", self.rs_done[j1])]
                        if self.rsy_free:
                            wt.append(("v", self.rsy_free))
                    ld_m = self.dma_ld(
                        l['p_rsy'][:, mc * 512 + jb * 256:mc * 512 + (jb + 1) * 256],
                        d['rsout'][j, dirn, mc, :, :], "lry", waits=wt)
            add_m = 0
            for mc in range(8):
                wt = [("lry", ld_m), ("lxp", t_xp)] if mc == 0 else []
                add_m = E("vector", lambda e: e.tensor_tensor(
                    CH('p_B', mc), CH('p_rsy', mc), CH('p_xp', mc), op=OP.add),
                    inc=("v", 1), waits=wt)
            self.rsy_free = add_m
            dstbuf = 'p_C' if dirn == 0 else 'p_B'
            self.ln(lambda mc: CH('p_B', mc), lambda mc: CH(dstbuf, mc),
                    wcol=lambda mc: pv[:, 24 + mc:25 + mc],
                    bcol=lambda mc: pv[:, 32 + mc:33 + mc],
                    ready=[("v", add_m), ("lw", self.t_pv)])
            if dirn == 1:
                for mc in range(8):
                    t_sum = E("vector", lambda e: e.tensor_tensor(
                        CH('p_C', mc), CH('p_C', mc), CH('p_B', mc), op=OP.add),
                        inc=("v", 1))
                if DEBUG_DUMP:
                    for mc in range(8):
                        self.dma_st(d['yd'][ts, mc], CH('p_C', mc), "sd",
                                    waits=[("v", t_sum)])

        cast_m = 0
        for mc in range(8):
            wt = []
            if mc == 0:
                wt = [("v", t_sum)]
                if self.out_st:
                    wt.append(("so", self.out_st))
                if self.mid_q_free:
                    wt.append(("t", self.mid_q_free))
            cast_m = E("vector", lambda e: e.tensor_copy(
                l['p_ybf'][:, mc * 512:(mc + 1) * 512], CH('p_C', mc)), inc=("v", 1),
                waits=wt)
        mid_m = 0
        for mc in range(16):
            buf = mc % 2
            stg = l['p_stg'][:, buf * 1024:(buf + 1) * 1024]
            wt = [("v", self.pstg_free[buf])] if self.pstg_free[buf] else []
            tl = self.dma_ld(stg, d['ffw1'][:, mc * 128:(mc + 1) * 128]
                             .rearrange("(kc p) m -> p kc m", p=128),
                             "l%d" % buf, waits=wt)
            cst = E("vector", lambda e: e.tensor_copy(
                l['p_wt'][:, buf * 1024:(buf + 1) * 1024], stg), inc=("v", 1),
                waits=[("l%d" % buf, tl)])
            self.pstg_free[buf] = cst
            bk = mc % 2
            mm = 0
            for kc in range(8):
                wt = []
                if kc == 0:
                    wt = [("v", cast_m), ("v", cst)] + self.bank_wait(bk)
                mm = E("tensor", lambda e: e.matmul(
                    pb[bk][:, :],
                    l['p_wt'][:, buf * 1024 + kc * 128:buf * 1024 + (kc + 1) * 128],
                    l['p_ybf'][:, kc * 512:(kc + 1) * 512],
                    start=(kc == 0), stop=(kc == 7)), inc=("t", 1), waits=wt)
            wt = [("t", mm), ("lw", self.t_pv)]
            if self.mid_free and mc < 2:
                wt.append(("t", self.mid_free))
            mid_m = E("scalar", lambda e: e.activation(
                out=l['p_mid'][:, mc * 512:(mc + 1) * 512], in_=pb[bk][:, :],
                func=AF.Relu, bias=pv[:, mc:mc + 1]), inc=("s", 1), waits=wt)
            self.bankfree[bk] = ("s", mid_m)
        ff2_last = 0
        for mc in range(8):
            buf = mc % 2
            cst = 0
            for hf in range(2):
                stg = l['p_stg'][:, buf * 1024:(buf + 1) * 1024]
                wt = [("v", self.pstg_free[buf])] if self.pstg_free[buf] else []
                tl = self.dma_ld(stg, d['ffw2'][hf * 1024:(hf + 1) * 1024,
                                                mc * 128:(mc + 1) * 128]
                                 .rearrange("(kc p) m -> p kc m", p=128),
                                 "l%d" % buf, waits=wt)
                cst = E("vector", lambda e: e.tensor_copy(
                    l['p_ws2'][:, buf * 4096 + hf * 1024:buf * 4096 + (hf + 1) * 1024],
                    stg), inc=("v", 1), waits=[("l%d" % buf, tl)])
                self.pstg_free[buf] = cst
                self.ffw2_cast[buf] = cst
            bk = 2 + mc % 2
            mm = 0
            for kc in range(16):
                wt = []
                if kc == 0:
                    wt = [("v", cst), ("s", mid_m)] + self.bank_wait(bk)
                mm = E("tensor", lambda e: e.matmul(
                    pb[bk][:, :],
                    l['p_ws2'][:, buf * 4096 + kc * 128:buf * 4096 + (kc + 1) * 128],
                    l['p_mid'][:, kc * 512:(kc + 1) * 512],
                    start=(kc == 0), stop=(kc == 15)), inc=("t", 1), waits=wt)
            ff2_last = E("vector", lambda e: e.scalar_tensor_tensor(
                CH('p_B', mc), pb[bk][:, :], pv[:, 16 + mc:17 + mc], CH('p_C', mc),
                op0=OP.add, op1=OP.add), inc=("v", 1), waits=[("t", mm)])
            self.bankfree[bk] = ("v", ff2_last)
        self.mid_free = w.c["t"]
        t_fin = self.ln(lambda mc: CH('p_B', mc), lambda mc: CH('p_B', mc),
                        wcol=lambda mc: pv[:, 40 + mc:41 + mc],
                        bcol=lambda mc: pv[:, 48 + mc:49 + mc],
                        ready=[("v", ff2_last)])
        if DEBUG_DUMP:
            for mc in range(8):
                self.dma_st(d['attnd'][ts, mc], CH('p_B', mc), "sd",
                            waits=[("v", t_fin)])
        t_add = 0
        for mc in range(8):
            t_add = E("vector", lambda e: e.tensor_tensor(
                CH('p_D', mc), CH('p_xp', mc), CH('p_B', mc), op=OP.add), inc=("v", 1))
        self.xp_free = t_add
        self.x1_ready = self.ln(lambda mc: CH('p_D', mc), lambda mc: CH('p_D', mc),
                                ready=[("v", t_add)])
        if DEBUG_DUMP:
            for mc in range(8):
                self.dma_st(d['x1d'][ts, mc], CH('p_D', mc), "sd",
                            waits=[("v", self.x1_ready)])
            self.barrier()

        if POST_LIMIT == 'ff':
            for mc in range(8):
                E("vector", lambda e: e.tensor_copy(CH('p_B', mc), CH('p_D', mc)))
            E("vector", lambda e: e.nop(), inc=("v", 1))
        else:
            self.post_ffn(ts)

        t_fln = self.ln(lambda mc: CH('p_B', mc), lambda mc: CH('p_B', mc),
                        ready=[("v", w.c["v"])])
        for tt in range(4):
            cp = 0
            for g in range(2):
                bk = 4 + g
                tr = 0
                for i in range(4):
                    mc = 4 * g + i
                    wt = self.bank_wait(bk) if i == 0 else []
                    if tt == 0 and g == 0 and i == 0:
                        wt = wt + [("v", t_fln)]
                    tr = E("tensor", lambda e: e.transpose(
                        pb[bk][:, i * 128:(i + 1) * 128],
                        l['p_B'][:, mc * 512 + tt * 128:mc * 512 + tt * 128 + 128],
                        l['identf'][:, :]), inc=("t", 1), waits=wt)
                wtc = [("t", tr)]
                if g == 0 and self.out_st:
                    wtc.append(("so", self.out_st))
                cp = E("vector", lambda e: e.tensor_copy(
                    l['p_outst'][:, g * 512:(g + 1) * 512], pb[bk][:, :]),
                    inc=("v", 1), waits=wtc)
                self.bankfree[bk] = ("v", cp)
            self.out_st = self.dma_st(
                d['out'][ts * 512 + tt * 128:ts * 512 + (tt + 1) * 128, :],
                l['p_outst'][:, :], "so", waits=[("v", cp)])

    def post_ffn(self, ts):
        w, l, d, pb = self.w, self.l, self.d, self.pb
        E = w.E
        mx = l['p_mx']
        xq_m = 0
        for tt in range(4):
            trm = 0
            for mc in range(8):
                bk = 4 + mc // 4
                wt = self.bank_wait(bk) if mc % 4 == 0 else []
                if tt == 0 and mc == 0:
                    wt = wt + [("v", self.x1_ready)]
                trm = E("tensor", lambda e: e.transpose(
                    pb[bk][:, (mc % 4) * 128:(mc % 4 + 1) * 128],
                    l['p_D'][:, mc * 512 + tt * 128:mc * 512 + tt * 128 + 128],
                    l['identf'][:, :]), inc=("t", 1), waits=wt)
            c1, c2 = mx[:, 20:21], mx[:, 21:22]
            mxc, rc, c2t, sc = mx[:, 16:17], mx[:, 17:18], mx[:, 18:19], mx[:, 19:20]
            E("vector", lambda e: e.tensor_reduce(c1, pb[4][:, :], axis=AX, op=OP.max,
                                                  apply_absolute_value=True),
              waits=[("t", trm)])
            r2 = E("vector", lambda e: e.tensor_reduce(c2, pb[5][:, :], axis=AX,
                                                       op=OP.max,
                                                       apply_absolute_value=True),
                   inc=("v", 1))
            m1 = E("vector", lambda e: e.scalar_tensor_tensor(
                mxc, c1, 1e-5, c2, op0=OP.max, op1=OP.max), inc=("v", 1),
                waits=[("v", r2)])
            rc1 = E("vector", lambda e: e.reciprocal(rc, mxc), inc=("v", 1),
                    waits=[("v", m1)])
            ct1 = E("vector", lambda e: e.tensor_scalar(
                c2t, mxc, float(np.float32(1.0 / 127.0)), None, op0=OP.mult),
                inc=("v", 1))
            sc1 = E("vector", lambda e: e.tensor_scalar(sc, rc, 127.0, None,
                                                        op0=OP.mult), inc=("v", 1),
                    waits=[("v", rc1)])
            E("vector", lambda e: e.tensor_tensor(
                mx[:, tt:tt + 1], c2t, l['qsc'][:, 0:1], op=OP.mult),
              waits=[("v", ct1)])
            t_sc = E("vector", lambda e: e.tensor_tensor(
                mx[:, 4 + tt:5 + tt], c2t, l['qsc'][:, 1:2], op=OP.mult), inc=("v", 1))
            qm = 0
            for hb, bk in enumerate((4, 5)):
                E("vector", lambda e: e.tensor_scalar(
                    l['p_chunk'][:, :], pb[bk][:, :], sc, MAGIC, op0=OP.mult, op1=OP.add),
                  waits=[("v", t_sc), ("v", sc1)])
                qm = E("vector", lambda e: e.tensor_scalar(
                    l['p_qtm'][:, hb * 512:(hb + 1) * 512], l['p_chunk'][:, :],
                    MAGIC, None, op0=OP.subtract), inc=("v", 1))
            self.bankfree[4] = ("v", qm)
            self.bankfree[5] = ("v", qm)
            if DEBUG_DUMP:
                dm_ = self.dma_st(d['qtmd'][ts, tt], l['p_qtm'][:, :], waits=[("v", qm)])
                self.barrier()
            for g in range(2):
                bk = 6 + g
                tr2 = 0
                for i in range(4):
                    mc = 4 * g + i
                    wt = [("v", qm)] + (self.bank_wait(bk) if i == 0 else [])
                    tr2 = E("tensor", lambda e: e.transpose(
                        self.pbh[bk][:, i * 128:(i + 1) * 128],
                        l['p_qtm'][:, mc * 128:(mc + 1) * 128], l['identb'][:, :]),
                        inc=("t", 1), waits=wt)
                for i in range(4):
                    mc = 4 * g + i
                    xq_m = E("scalar", lambda e: e.copy(
                        out=l['p_xq'][:, mc * 512 + tt * 128:mc * 512 + tt * 128 + 128],
                        in_=self.pbh[bk][:, i * 128:(i + 1) * 128]), inc=("s", 1),
                        waits=[("t", tr2)] if i == 0 else [])
                self.bankfree[bk] = ("s", xq_m)

        self.sc_ready = w.c["v"]
        if POST_LIMIT == 'quant':
            for mc in range(8):
                E("vector", lambda e: e.tensor_copy(
                    l['p_B'][:, mc * 512:(mc + 1) * 512],
                    l['p_D'][:, mc * 512:(mc + 1) * 512]), inc=("v", 1),
                    waits=[("s", xq_m)])
            return
        if DEBUG_DUMP:
            for mc in range(8):
                self.dma_st(d['xqd'][ts, mc],
                            l['p_xq'][:, mc * 512:(mc + 1) * 512], "sd",
                            waits=[("s", xq_m)])
            self.dma_st(d['mxd'][ts], mx[:, :], "sd", waits=[("s", xq_m)])
            self.barrier()
        gu_st = self.gu_st
        gu_prev = 0
        red_m = 0
        for mcg in range(8):
            buf = mcg % 2
            wt = [("sq", self.wq_done)]
            if gu_st[buf]:
                wt.append(("sg", gu_st[buf]))
            tg = self.dma_ld(l['p_ws'][:, buf * 4096:buf * 4096 + 4096],
                             d['wgq'][:, :, mcg * 512:(mcg + 1) * 512]
                             .rearrange("k p m -> p k m"), "lg%d" % buf, waits=wt)
            wtu = [("v", self.ffw2_cast[buf])] if self.ffw2_cast[buf] else []
            if gu_st[buf]:
                wtu.append(("sg", gu_st[buf]))
            tu = self.dma_ld(l['p_ws2'][:, buf * 4096:buf * 4096 + 4096],
                             d['wuq'][:, :, mcg * 512:(mcg + 1) * 512]
                             .rearrange("k p m -> p k m"), "lu%d" % buf, waits=wtu)
            if mcg < 2:
                self.ffw2_cast[buf] = 0
            for tt in range(4):
                mmg = 0
                for kc in range(8):
                    wt = []
                    if kc == 0:
                        wt = [("lg%d" % buf, tg), ("s", xq_m)] + self.bank_wait(0)
                    mmg = E("tensor", lambda e: e.matmul(
                        pb[0][:, :],
                        l['p_xq'][:, kc * 512 + tt * 128:kc * 512 + tt * 128 + 128],
                        l['p_ws'][:, buf * 4096 + kc * 512:buf * 4096 + (kc + 1) * 512],
                        start=(kc == 0), stop=(kc == 7)), inc=("t", 1), waits=wt)
                wtg = [("t", mmg)]
                if gu_prev:
                    wtg.append(("v", gu_prev))
                wtg.append(("v", self.sc_ready))
                t_g = E("scalar", lambda e: e.activation(
                    out=l['p_chunk'][:, :], in_=pb[0][:, :], func=AF.Silu,
                    scale=mx[:, tt:tt + 1]), inc=("s", 1), waits=wtg)
                self.bankfree[0] = ("s", t_g)
                mmu = 0
                for kc in range(8):
                    wt = []
                    if kc == 0:
                        wt = [("lu%d" % buf, tu)] + self.bank_wait(1)
                    mmu = E("tensor", lambda e: e.matmul(
                        pb[1][:, :],
                        l['p_xq'][:, kc * 512 + tt * 128:kc * 512 + tt * 128 + 128],
                        l['p_ws2'][:, buf * 4096 + kc * 512:buf * 4096 + (kc + 1) * 512],
                        start=(kc == 0), stop=(kc == 7)), inc=("t", 1), waits=wt)
                t_u = E("scalar", lambda e: e.activation(
                    out=l['p_chunk2'][:, :], in_=pb[1][:, :], func=AF.Identity,
                    scale=mx[:, 4 + tt:5 + tt]), inc=("s", 1),
                    waits=[("t", mmu), ("v", self.sc_ready)])
                self.bankfree[1] = ("s", t_u)
                wt = [("s", t_g), ("s", t_u)]
                if self.sg_last:
                    wt.append(("sg", self.sg_last))
                t_gu = E("vector", lambda e: e.tensor_tensor(
                    l['p_gu'][:, 0:512], l['p_chunk'][:, :], l['p_chunk2'][:, :],
                    op=OP.mult), inc=("v", 1), waits=wt)
                gu_prev = t_gu
                gu_st[buf] = self.dma_st(d['gud'][ts, tt, mcg, :, :],
                                         l['p_gu'][:, 0:512], "sg",
                                         waits=[("v", t_gu)])
                self.sg_last = gu_st[buf]
                red_m = E("vector", lambda e: e.tensor_reduce(
                    mx[:, 32 + tt * 8 + mcg:33 + tt * 8 + mcg], l['p_gu'][:, 0:512],
                    axis=AX, op=OP.max, apply_absolute_value=True), inc=("v", 1))
        self.red_last = red_m
        cm = 0
        for tt in range(4):
            cm = E("vector", lambda e: e.tensor_reduce(
                mx[:, 8 + tt:9 + tt], mx[:, 32 + tt * 8:32 + tt * 8 + 8],
                axis=AX, op=OP.max), inc=("v", 1),
                waits=[("v", red_m)] if tt == 0 else [])
        km = 0
        for tt in range(4):
            km = E("vector", lambda e: e.tensor_scalar(
                mx[:, 24 + tt:25 + tt], mx[:, 8 + tt:9 + tt], 1e-5, None, op0=OP.max),
                inc=("v", 1), waits=[("v", cm)] if tt == 0 else [])
        rm = 0
        for tt in range(4):
            rm = E("vector", lambda e: e.reciprocal(mx[:, 28 + tt:29 + tt],
                                                    mx[:, 24 + tt:25 + tt]),
                   inc=("v", 1), waits=[("v", km)] if tt == 0 else [])
        gm = 0
        for tt in range(4):
            gm = E("vector", lambda e: e.tensor_scalar(
                mx[:, 16 + tt:17 + tt], mx[:, 24 + tt:25 + tt],
                float(np.float32(1.0 / 127.0)), None, op0=OP.mult), inc=("v", 1))
        for tt in range(4):
            E("vector", lambda e: e.tensor_scalar(
                mx[:, 12 + tt:13 + tt], mx[:, 28 + tt:29 + tt], 127.0, None,
                op0=OP.mult), waits=[("v", rm)] if tt == 0 else [])
        for tt in range(4):
            srow_m = E("vector", lambda e: e.tensor_tensor(
                mx[:, 20 + tt:21 + tt], mx[:, 16 + tt:17 + tt], l['qsc'][:, 2:3],
                op=OP.mult), inc=("v", 1), waits=[("v", gm)] if tt == 0 else [])
        if POST_LIMIT == 'gateup':
            for mc in range(8):
                E("vector", lambda e: e.tensor_copy(
                    l['p_B'][:, mc * 512:(mc + 1) * 512],
                    l['p_D'][:, mc * 512:(mc + 1) * 512]), inc=("v", 1),
                    waits=[("v", red_m), ("sg", self.sg_last)])
            return
        t_tr = 0
        for tt in range(4):
            wt = [("v", srow_m)] + (self.bank_wait(6) if tt == 0 else [])
            t_tr = E("tensor", lambda e: e.transpose(
                pb[6][0:1, tt * 128:(tt + 1) * 128], mx[:, 20 + tt:21 + tt],
                l['identf'][:, :]), inc=("t", 1), waits=wt)
        rr4 = l['p_rows'][0:1, 1024:1536]
        t_cp = E("vector", lambda e: e.tensor_copy(rr4, pb[6][0:1, :]),
                 inc=("v", 1), waits=[("t", t_tr)])
        self.bankfree[6] = ("v", t_cp)
        t_bc = 0
        for tt in range(4):
            rr = l['p_rows'][0:1, 1024 + tt * 128:1024 + (tt + 1) * 128]
            wt2 = [("v", t_cp)] + (self.bank_wait(7) if tt == 0 else [])
            t_bc = E("tensor", lambda e: e.matmul(
                pb[7][:, tt * 128:(tt + 1) * 128], l['onesrow128'], rr,
                start=True, stop=True, skip_group_check=True),
                inc=("t", 1), waits=wt2)
        t_srep = E("vector", lambda e: e.tensor_copy(l['p_s2rep'][:, :], pb[7][:, :]),
                   inc=("v", 1), waits=[("t", t_bc)])
        self.bankfree[7] = ("v", t_srep)
        self.s2rep_ready = t_srep

        t_q = 0
        dn_prev = 0
        tr_prev = 0
        for tt in range(4):
            gq_done = 0
            for sub in range(2):
                wt = ([("sg", self.sg_last), ("v", self.red_last)]
                      if tt == 0 and sub == 0 else [])
                if t_q:
                    wt.append(("v", t_q))
                tl = self.dma_ld(l['p_gu'][:, :],
                                 d['gud'][ts, tt, 4 * sub:4 * sub + 4]
                                 .rearrange("m p i -> p m i"), "lgu", waits=wt)
                wt2 = [("lgu", tl), ("v", srow_m)]
                if tt == 0 and sub == 0 and self.mid_q_free:
                    wt2.append(("t", self.mid_q_free))
                if tr_prev:
                    wt2.append(("t", tr_prev))
                E("vector", lambda e: e.tensor_scalar(
                    l['p_gu'][:, :], l['p_gu'][:, :], mx[:, 12 + tt:13 + tt], MAGIC,
                    op0=OP.mult, op1=OP.add), waits=wt2)
                t_q = E("vector", lambda e: e.tensor_scalar(
                    l['p_mid'][:, sub * 2048:sub * 2048 + 2048], l['p_gu'][:, :],
                    MAGIC, None, op0=OP.subtract), inc=("v", 1))
            for g in range(8):
                bk = 4 + g % 2
                tr = 0
                for i in range(4):
                    fk = 4 * g + i
                    wt = [("v", t_q)] + (self.bank_wait(bk) if i == 0 else [])
                    tr = E("tensor", lambda e: e.transpose(
                        self.pbh[bk][:, i * 128:(i + 1) * 128],
                        l['p_mid'][:, fk * 128:(fk + 1) * 128], l['identb'][:, :]),
                        inc=("t", 1), waits=wt)
                wtc = [("t", tr)]
                if g == 0 and dn_prev:
                    wtc.append(("t", dn_prev))
                gq_done = E("scalar", lambda e: e.copy(
                    out=l['p_gqt'][:, 4 * g * 128:4 * (g + 1) * 128],
                    in_=self.pbh[bk][:, 0:512]), inc=("s", 1), waits=wtc)
                self.bankfree[bk] = ("s", gq_done)
                tr_prev = tr
            for mc in range(8):
                buf = mc % 2
                wt = [("t", self.wd_used[buf])] if self.wd_used[buf] else []
                tl = self.dma_ld(l['p_wd'][:, buf * 4096:buf * 4096 + 4096],
                                 d['wdq'][:, :, mc * 128:(mc + 1) * 128]
                                 .rearrange("k p m -> p k m"), "lwd%d" % buf, waits=wt)
                bk = mc % 2
                mm = 0
                for fk in range(32):
                    wt = []
                    if fk == 0:
                        wt = [("lwd%d" % buf, tl), ("s", gq_done)] + self.bank_wait(bk)
                    mm = E("tensor", lambda e: e.matmul(
                        pb[bk][:, 0:128],
                        l['p_wd'][:, buf * 4096 + fk * 128:buf * 4096 + (fk + 1) * 128],
                        l['p_gqt'][:, fk * 128:(fk + 1) * 128],
                        start=(fk == 0), stop=(fk == 31)), inc=("t", 1), waits=wt)
                self.wd_used[buf] = mm
                dn_prev = mm
                cbuf = l['p_chunk'] if mc % 2 == 0 else l['p_chunk2']
                t_c = E("vector", lambda e: e.tensor_tensor(
                    cbuf[:, 0:128], pb[bk][:, 0:128],
                    l['p_s2rep'][:, tt * 128:(tt + 1) * 128], op=OP.mult),
                    inc=("v", 1), waits=[("t", mm), ("v", self.s2rep_ready)])
                t_d = E("vector", lambda e: e.tensor_tensor(
                    l['p_B'][:, mc * 512 + tt * 128:mc * 512 + (tt + 1) * 128],
                    cbuf[:, 0:128],
                    l['p_D'][:, mc * 512 + tt * 128:mc * 512 + (tt + 1) * 128],
                    op=OP.add), inc=("v", 1), waits=[("v", t_c)])
                self.bankfree[bk] = ("v", t_d)
        self.mid_q_free = w.c["t"]
        self.mid_free = w.c["t"]

    def reset_sched_state(self):
        self.bankfree = [("v", 0) for _ in range(8)]
        self.rs_ready = {}
        self.rs_done = {}
        self.srow_rd = 0
        self.wq_done = 0
        self.ffw2_cast = [0, 0]
        self.wd_used = [0, 0]
        self.xbt_free = [("t", 0), ("t", 0)]
        self.stg_free = [0, 0]
        self.u_free = ("v", 0)
        self.sz_free = ("v", 0)
        self.dt_free = ("v", 0)
        self.cv_free = ("s", 0)
        self.xdbc_free = ("sa", 0)
        self.dtin_free = ("t", 0)
        self.ysz_free = ("t", 0)
        self.xp_free = 0
        self.rsy_free = 0
        self.mid_free = 0
        self.mid_q_free = 0
        self.out_st = 0
        self.pstg_free = [0, 0]
        self.qst_free = [0, 0]
        self.ps_marks = [0, 0]
        self.gu_st = [0, 0]
        self.sg_last = 0
        self.red_last = 0
        self.dbg_xbt_st = [0, 0]
        self.dbg_u_st = 0
        self.dbg_xib_st = 0

    def schedule(self):
        w = self.w
        self.ph_consts()
        self.ph_wquant(self.d['wgT'], self.d['wgq'], 8, FF, 0)
        self.ph_wquant(self.d['wuT'], self.d['wuq'], 8, FF, 1)
        self.ph_wquant(self.d['wdT'], self.d['wdq'], 32, DM, 2)
        self.wq_done = w.c["sq"]
        self.barrier()
        for dirn in range(2):
            self.ph_sweep(dirn)
            self.barrier()
        self.ph_post()
        self.barrier()
        for name, (tgt, src) in self.dumps.items():
            self.dma_st(tgt, src, "sd", waits=[("g", w.c["g"])])
        w.E("gpsimd", lambda e: e.nop(),
            waits=[(s, w.c[s]) for s in ST_SEMS if w.c[s] > 0])

    def build(self):
        nc = self.nc
        from contextlib import ExitStack
        with ExitStack() as ctx:
            sems = {n: ctx.enter_context(nc.semaphore(n))
                    for n in ("v", "s", "t", "g") + LD_SEMS + ST_SEMS}
            block = ctx.enter_context(nc.Block())

            def run(name):
                def body(eng):
                    self.w = Walk(name, eng, sems)
                    self.reset_sched_state()
                    self.schedule()
                return body

            block.sync(run("sync"))
            block.vector(run("vector"))
            block.scalar(run("scalar"))
            block.tensor(run("tensor"))
            block.gpsimd(run("gpsimd"))
        return nc


def _stage_inputs(inputs):
    f = lambda k: np.ascontiguousarray(np.asarray(inputs[k], dtype=np.float32))
    x = f('L0_attn_x')
    ident = np.eye(128, dtype=np.float32)
    toks = np.concatenate([np.arange(j * SC, j * SC + QT) for j in range(NSC)])

    pvec = np.zeros((128, 64), np.float32)
    ffb1 = f('L0_attn_ff_b1')
    ffb2 = f('L0_attn_ff_b2')
    for mc in range(16):
        pvec[:, mc] = ffb1[mc * 128:(mc + 1) * 128]
    for mc in range(8):
        pvec[:, 16 + mc] = ffb2[mc * 128:(mc + 1) * 128]
        pvec[:, 24 + mc] = f('L0_attn_fwn_w')[mc * 128:(mc + 1) * 128]
        pvec[:, 32 + mc] = f('L0_attn_fwn_b')[mc * 128:(mc + 1) * 128]
        pvec[:, 40 + mc] = f('L0_attn_fin_w')[mc * 128:(mc + 1) * 128]
        pvec[:, 48 + mc] = f('L0_attn_fin_b')[mc * 128:(mc + 1) * 128]

    com = {'ident': ident, 'pvec': pvec,
           'wgT': np.ascontiguousarray(f('L0_ffn_w_gate').T),
           'wuT': np.ascontiguousarray(f('L0_ffn_w_up').T),
           'wdT': np.ascontiguousarray(f('L0_ffn_w_down').T),
           'ffw1': f('L0_attn_ff_W1'), 'ffw2': f('L0_attn_ff_W2')}

    in_maps = []
    for b in range(2):
        xT = np.ascontiguousarray(x[b].T)
        xTf = np.ascontiguousarray(x[b, ::-1].T)
        for q in range(4):
            im = dict(com)
            im['xT'] = xT
            im['xTf'] = xTf
            im['xpost'] = np.ascontiguousarray(x[b].T[:, toks + q * QT])
            for dirn, pfx in enumerate(('fw', 'bw')):
                dn = "fb"[dirn]
                Win = f(f'L0_attn_{pfx}_Win')
                im[f'win_{dn}'] = np.ascontiguousarray(np.concatenate(
                    [Win[:, q * DQ:(q + 1) * DQ],
                     Win[:, DI + q * DQ:DI + (q + 1) * DQ]], axis=1))
                im[f'wx_{dn}'] = np.ascontiguousarray(
                    f(f'L0_attn_{pfx}_Wx')[q * DQ:(q + 1) * DQ, :])
                im[f'wdt_{dn}'] = np.ascontiguousarray(
                    f(f'L0_attn_{pfx}_Wdt')[:, q * DQ:(q + 1) * DQ])
                im[f'wout_{dn}'] = np.ascontiguousarray(
                    f(f'L0_attn_{pfx}_Wout')[q * DQ:(q + 1) * DQ, :])
                vec = np.zeros((128, 92), np.float32)
                cvw = f(f'L0_attn_{pfx}_convw')[q * DQ:(q + 1) * DQ]
                cvb = f(f'L0_attn_{pfx}_convb')[q * DQ:(q + 1) * DQ]
                bdt = f(f'L0_attn_{pfx}_bdt')[q * DQ:(q + 1) * DQ]
                Dp = f(f'L0_attn_{pfx}_D')[q * DQ:(q + 1) * DQ]
                Alog = f(f'L0_attn_{pfx}_Alog')[q * DQ:(q + 1) * DQ]
                for dc in range(NDC):
                    sl = slice(dc * 128, (dc + 1) * 128)
                    vec[:, dc * 4:(dc + 1) * 4] = cvw[sl]
                    vec[:, 16 + dc] = cvb[sl]
                    vec[:, 20 + dc] = bdt[sl]
                    vec[:, 24 + dc] = Dp[sl]
                    vec[:, 28 + dc * 16:28 + (dc + 1) * 16] = Alog[sl]
                im[f'vec_{dn}'] = vec
            in_maps.append(im)
    return in_maps, toks


_NC_CACHE = {}


def kernel(**inputs):
    if 'nc' not in _NC_CACHE:
        _NC_CACHE['nc'] = MK().build()
    nc = _NC_CACHE['nc']
    in_maps, toks = _stage_inputs(inputs)
    res = run_bass_kernel_spmd(nc, in_maps, core_ids=list(range(8)))
    out = np.zeros((2, L, DM), np.float32)
    for b in range(2):
        for q in range(4):
            out[b, toks + q * QT, :] = res.results[b * 4 + q]['out']
    return out


if __name__ == "__main__":
    mk = MK()
    nc = mk.build()
    ni = sum(1 for _ in nc.all_instructions())
    print("built, instructions:", ni)
    print("mamba peak", mk.MB_PEAK, "post peak", mk.PO_PEAK,
          "quant peak", mk.QU_PEAK, "arena", mk.A.n)


# version 13
# speedup vs baseline: 1.0276x; 1.0276x over previous
import numpy as np
import ml_dtypes
import concourse.bass as bass
import concourse.mybir as mybir
from concourse.bass_utils import run_bass_kernel_spmd

F32 = mybir.dt.float32
BF16 = mybir.dt.bfloat16
AF = mybir.ActivationFunctionType
OP = mybir.AluOpType
AX = mybir.AxisListType.X

L = 8192
DM = 1024
DI = 2048
DQ = 512
NDC = 4
DS = 16
RK = 64
SC = 1024
NSC = 8
TPC = 2048
QT = 256
FF = 4096
MAGIC = float(np.float32(12582912.0))
WCLIP = float(np.float32(1.49999988))
EPS = 1e-5
RG = [[0, 1, 2, 3], [4, 5, 6, 7]]

DEBUG_TAPS = {}
DEBUG_DUMP = []
POST_LIMIT = None
DEBUG_SWEEP = []

LD_SEMS = ("lw", "l0", "l1", "l2", "l3", "lb0", "lb1", "lc0", "lc1", "ldt",
           "lxp", "lry", "lg0", "lg1", "lu0", "lu1", "lgu", "lwd0", "lwd1")
ST_SEMS = ("sa", "sp0", "sp1", "sg", "so", "sq", "sd")
DMA_SEMS = frozenset(LD_SEMS) | frozenset(ST_SEMS)


class Walk:

    def __init__(self, name, eng, sems):
        self.name = name
        self.eng = eng
        self.sems = sems
        self.c = {k: 0 for k in sems}

    def E(self, engname, fn, inc=None, waits=()):
        if engname == self.name:
            for s, v in waits:
                if s in DMA_SEMS and v > 0:
                    v = self.c[s]
                if v > 0:
                    self.eng.wait_ge(self.sems[s], v)
            inst = fn(self.eng)
        else:
            inst = None
        if inc is not None:
            s, a = inc
            self.c[s] += a
            if inst is not None:
                inst.then_inc(self.sems[s], a)
            return self.c[s]
        return None


class Arena:
    def __init__(self, ap):
        self.base = ap
        self.n = ap.shape[1]
        self.off = 0
        self.marks = []
        self.peak = 0

    def alloc(self, cols, tag=""):
        cols = (cols + 15) // 16 * 16
        a = self.base[:, self.off:self.off + cols]
        self.off += cols
        self.peak = max(self.peak, self.off)
        assert self.off <= self.n, f"arena overflow at {tag}: {self.off}>{self.n}"
        return a

    def alloc_bf16(self, cols_bf16, tag=""):
        return self.alloc((cols_bf16 + 1) // 2, tag).bitcast(BF16)

    def push(self):
        self.marks.append(self.off)

    def pop(self):
        self.off = self.marks.pop()


class MK:
    def __init__(self):
        nc = bass.Bass("TRN2", target_bir_lowering=False)
        self.nc = nc
        d = self.d = {}
        d['xT'] = nc.dram_tensor("xT", [DM, L], F32, kind="ExternalInput").ap()
        d['xTf'] = nc.dram_tensor("xTf", [DM, L], F32, kind="ExternalInput").ap()
        for i, dn in enumerate("fb"):
            d[f'win{i}'] = nc.dram_tensor(f"win_{dn}", [DM, 2 * DQ], F32, kind="ExternalInput").ap()
            d[f'wx{i}'] = nc.dram_tensor(f"wx_{dn}", [DQ, 96], F32, kind="ExternalInput").ap()
            d[f'wdt{i}'] = nc.dram_tensor(f"wdt_{dn}", [RK, DQ], F32, kind="ExternalInput").ap()
            d[f'wout{i}'] = nc.dram_tensor(f"wout_{dn}", [DQ, DM], F32, kind="ExternalInput").ap()
            d[f'vec{i}'] = nc.dram_tensor(f"vec_{dn}", [128, 92], F32, kind="ExternalInput").ap()
        d['ident'] = nc.dram_tensor("ident", [128, 128], F32, kind="ExternalInput").ap()
        d['ffw1'] = nc.dram_tensor("ffw1", [DM, DI], F32, kind="ExternalInput").ap()
        d['ffw2'] = nc.dram_tensor("ffw2", [DI, DM], F32, kind="ExternalInput").ap()
        d['pvec'] = nc.dram_tensor("pvec", [128, 64], F32, kind="ExternalInput").ap()
        d['wgT'] = nc.dram_tensor("wgT", [DM, FF], F32, kind="ExternalInput").ap()
        d['wuT'] = nc.dram_tensor("wuT", [DM, FF], F32, kind="ExternalInput").ap()
        d['wdT'] = nc.dram_tensor("wdT", [FF, DM], F32, kind="ExternalInput").ap()
        d['xpost'] = nc.dram_tensor("xpost", [DM, TPC], F32, kind="ExternalInput").ap()
        d['out'] = nc.dram_tensor("out", [TPC, DM], F32, kind="ExternalOutput").ap()
        d['arin'] = nc.dram_tensor("arin", [2, NSC, 96, SC], F32, kind="Internal").ap()
        d['arout'] = nc.dram_tensor("arout", [2, NSC, 96, SC], F32, kind="Internal").ap()
        d['rsin'] = nc.dram_tensor("rsin", [NSC, 4, 2, 8, 128, QT], F32, kind="Internal").ap()
        d['rsout'] = nc.dram_tensor("rsout", [NSC, 2, 8, 128, QT], F32, kind="Internal").ap()
        d['wgq'] = nc.dram_tensor("wgq", [8, 128, FF], BF16, kind="Internal").ap()
        d['wuq'] = nc.dram_tensor("wuq", [8, 128, FF], BF16, kind="Internal").ap()
        d['wdq'] = nc.dram_tensor("wdq", [32, 128, DM], BF16, kind="Internal").ap()
        d['gud'] = nc.dram_tensor("gud", [4, 4, 8, 128, 512], F32, kind="Internal").ap()
        d['x1d'] = nc.dram_tensor("x1d", [4, 8, 128, 512], F32, kind="Internal").ap()
        d['attnd'] = nc.dram_tensor("attnd", [4, 8, 128, 512], F32, kind="Internal").ap()
        d['yd'] = nc.dram_tensor("yd", [4, 8, 128, 512], F32, kind="Internal").ap()
        d['qtmd'] = nc.dram_tensor("qtmd", [4, 4, 128, 1024], BF16, kind="Internal").ap()
        d['xqd'] = nc.dram_tensor("xqd", [4, 8, 128, 512], BF16, kind="Internal").ap()
        d['mxd'] = nc.dram_tensor("mxd", [4, 128, 64], F32, kind="Internal").ap()
        d['srow'] = nc.dram_tensor("srow", [128, 8], F32, kind="Internal").ap()
        if 'u' in DEBUG_SWEEP:
            d['ud'] = nc.dram_tensor("ud", [2, NSC, 128, 4096], F32,
                                     kind="ExternalOutput").ap()
        if 'xib' in DEBUG_SWEEP:
            d['xibd'] = nc.dram_tensor("xibd", [2, NSC, 128, 4160], F32,
                                       kind="ExternalOutput").ap()
        if 'xbt' in DEBUG_SWEEP:
            d['xbtd'] = nc.dram_tensor("xbtd", [2, NSC, 2, 128, 4096], BF16,
                                       kind="ExternalOutput").ap()
        self.taps = {}
        for name, shape in DEBUG_TAPS.items():
            self.taps[name] = nc.dram_tensor("tap_" + name, list(shape), F32,
                                             kind="ExternalOutput").ap()
        self.dumps = {}
        for name, key, idx in DEBUG_DUMP:
            src = d[key] if idx is None else d[key][idx]
            dt_ = src.dtype
            self.dumps[name] = (nc.dram_tensor("dump_" + name, list(src.shape), dt_,
                                               kind="ExternalOutput").ap(), src)

        ARN = 53184
        sbuf = nc.alloc_sbuf_tensor("arena", [128, ARN], F32).ap()
        A = self.A = Arena(sbuf)
        y = self.l = {}
        y['identf'] = A.alloc(128, 'identf')
        y['identb'] = A.alloc_bf16(128, 'identb')
        mis = A.alloc(16, 'mis')
        y['onescol'] = mis[:, 0:1]
        y['c127'] = mis[:, 1:2]
        orow = A.alloc(128, 'onesrow')
        y['onesrow128'] = orow[0:1, 0:128]
        y['vecs'] = A.alloc(2 * 96, 'vecs')
        y['pvec'] = A.alloc(64, 'pvec')
        y['carry'] = A.alloc(64, 'carry')
        y['qsc'] = A.alloc(16, 'qsc')
        y['rowt'] = A.alloc(16, 'rowt')

        A.push()
        y['stage'] = A.alloc(2 * 1024, 'stage')
        y['wzb'] = A.alloc_bf16(8 * 1024, 'wzb')
        y['woutb'] = A.alloc_bf16(4 * 1024, 'woutb')
        y['wxf'] = A.alloc(4 * 96, 'wxf')
        y['wdtf'] = A.alloc(4 * 128, 'wdtf')
        y['xbt'] = A.alloc_bf16(2 * 8 * 512, 'xbt')
        y['xib'] = A.alloc(NDC * 1040, 'xib')
        y['ubuf'] = A.alloc(NDC * 1024, 'ubuf')
        y['szbuf'] = A.alloc(NDC * 1024, 'szbuf')
        y['dtbuf'] = A.alloc(NDC * 1024, 'dtbuf')
        y['dtubuf'] = A.alloc(NDC * 1024, 'dtubuf')
        y['cvtmp'] = A.alloc(1024, 'cvtmp')
        y['dttmp'] = A.alloc(512, 'dttmp')
        y['abuf'] = A.alloc(2 * 1024, 'abuf')
        y['bbuf'] = A.alloc(2 * 1024, 'bbuf')
        y['hbuf'] = A.alloc(2 * 1024, 'hbuf')
        y['brep'] = A.alloc(2 * 1024, 'brep')
        y['crep'] = A.alloc(2 * 1024, 'crep')
        y['zbuf'] = A.alloc_bf16(2 * 1024, 'zbuf')
        y['y1buf'] = y['cvtmp']
        y['yszb'] = A.alloc_bf16(NDC * 1024, 'yszb')
        y['dtin'] = A.alloc(1024, 'dtin')
        y['xdbcv'] = A.alloc(1024, 'xdbcv')
        y['pstage'] = A.alloc(2 * 512, 'pstage')
        self.MB_PEAK = A.off
        A.pop()

        A.push()
        y['p_rsy'] = A.alloc(8 * 512, 'p_rsy')
        y['p_xp'] = A.alloc(8 * 512, 'p_xp')
        y['p_B'] = A.alloc(8 * 512, 'p_B')
        y['p_C'] = A.alloc(8 * 512, 'p_C')
        y['p_D'] = A.alloc(8 * 512, 'p_D')
        yb_base = A.alloc(2048, 'p_ybf')
        y['p_ybf'] = yb_base.bitcast(BF16)
        y['p_qtm'] = yb_base[:, 0:512].bitcast(BF16)
        y['p_s2rep'] = yb_base[:, 512:1024]
        y['p_outst'] = yb_base[:, 1024:2048]
        y['p_mid'] = A.alloc_bf16(16 * 512, 'p_mid')
        y['p_xq'] = A.alloc_bf16(8 * 512, 'p_xq')
        y['p_wt'] = A.alloc_bf16(2 * 1024, 'p_wt')
        y['p_ws'] = A.alloc_bf16(2 * 8 * 512, 'p_ws')
        y['p_ws2'] = A.alloc_bf16(2 * 8 * 512, 'p_ws2')
        y['p_stg'] = A.alloc(2 * 1024, 'p_stg')
        y['p_gu'] = A.alloc(2048, 'p_gu')
        y['p_gqt'] = A.alloc_bf16(32 * 128, 'p_gqt')
        y['p_wd'] = A.alloc_bf16(2 * 32 * 128, 'p_wd')
        y['p_chunk'] = A.alloc(512, 'p_chunk')
        y['p_chunk2'] = A.alloc(512, 'p_chunk2')
        y['p_rows'] = A.alloc(3 * 512, 'p_rows')
        y['p_mx'] = A.alloc(64, 'p_mx')
        self.PO_PEAK = A.off
        A.pop()

        A.push()
        y['q_st'] = A.alloc(2 * 4096, 'q_st')
        y['q_bf'] = A.alloc_bf16(2 * 4096, 'q_bf')
        y['q_acc'] = A.alloc(48, 'q_acc')
        self.QU_PEAK = A.off
        A.pop()

        self.pb = [nc.alloc_psum_tensor(f"pb{i}", [128, 512], F32).ap() for i in range(8)]
        self.pbh = [p.bitcast(BF16) for p in self.pb]

    def dma_ld(self, out, in_, sem, waits=()):
        return self.w.E("sync", lambda e: e.dma_start(out=out, in_=in_),
                        inc=(sem, 16), waits=waits)

    def dma_st(self, out, in_, sem, waits=()):
        return self.w.E("gpsimd", lambda e: e.dma_start(out=out, in_=in_),
                        inc=(sem, 16), waits=waits)

    def vgap(self):
        self.w.E("vector", lambda e: e.nop(cycle_cnt=640))

    def bank_wait(self, i):
        return [self.bankfree[i]] if self.bankfree[i][1] > 0 else []

    def barrier(self):
        w = self.w
        marks = dict(w.c)
        dma_wt = [(sn, marks[sn]) for sn in LD_SEMS + ST_SEMS if marks[sn] > 0]
        for eng in ("vector", "scalar", "tensor", "gpsimd", "sync"):
            wt = [(sn, marks[sn]) for sn in ("v", "s", "t", "g")]
            sem = {"vector": "v", "scalar": "s", "tensor": "t",
                   "gpsimd": "g", "sync": "g"}[eng]
            wt = [x for x in wt if x[0] != sem or eng in ("sync",)]
            w.E(eng, lambda e: e.nop(),
                inc=({"vector": "v", "scalar": "s", "tensor": "t"}.get(eng, "g"), 1)
                if eng in ("vector", "scalar", "tensor") else None,
                waits=wt + dma_wt)

    def tap(self, name, src):
        if name in self.taps:
            tgt = self.taps[name]
            w = self.w
            self.dma_st(tgt[0:src.shape[0], :], src, "sd",
                        waits=[("v", w.c["v"]), ("s", w.c["s"]), ("t", w.c["t"])])

    def ph_consts(self):
        w, l, d = self.w, self.l, self.d
        E = w.E
        t_id = self.dma_ld(l['identf'][:, :], d['ident'][:, :], "lw")
        t_v0 = self.dma_ld(l['vecs'][:, 0:92], d['vec0'][:, :], "lw")
        t_v1 = self.dma_ld(l['vecs'][:, 96:188], d['vec1'][:, :], "lw")
        self.t_pv = self.dma_ld(l['pvec'][:, :], d['pvec'][:, :], "lw")
        E("vector", lambda e: e.memset(l['onescol'], 1.0))
        E("vector", lambda e: e.memset(l['c127'], 127.0))
        E("vector", lambda e: e.memset(l['onesrow128'], 1.0))
        E("vector", lambda e: e.tensor_copy(l['identb'][:, :], l['identf'][:, :]),
          waits=[("lw", t_id)])
        for dirn in range(2):
            vv = l['vecs'][:, 96 * dirn + 28:96 * dirn + 92]
            t_e = E("scalar", lambda e: e.activation(out=vv, in_=vv, func=AF.Exp),
                    inc=("s", 1), waits=[("lw", max(t_v0, t_v1))])
            E("vector", lambda e: e.tensor_scalar(vv, vv, -1.0, None, op0=OP.mult),
              waits=[("s", t_e)])
        E("vector", lambda e: e.nop(), inc=("v", 1))

    def ph_wquant(self, dsrc, ddst, nchunk, width, slot):
        w, l = self.w, self.l
        E = w.E
        qst, qbf, qacc = l['q_st'], l['q_bf'], l['q_acc']
        abs_marks = []
        for c in range(nchunk):
            stg = qst[:, (c % 2) * 4096:(c % 2) * 4096 + width]
            wt = [("s", abs_marks[c - 2])] if c >= 2 else []
            if self.qst_free[c % 2]:
                wt = wt + [("v", self.qst_free[c % 2])]
            tl = self.dma_ld(stg, dsrc[c * 128:(c + 1) * 128, :], "l%d" % (c % 2),
                             waits=wt)
            m = E("scalar", lambda e: e.activation(
                out=stg, in_=stg, func=AF.Abs, accum_out=qacc[:, c:c + 1]),
                inc=("s", 1), waits=[("l%d" % (c % 2), tl)])
            abs_marks.append(m)
        t_sum = E("vector", lambda e: e.tensor_reduce(
            qacc[:, 36:37], qacc[:, 0:nchunk], axis=AX, op=OP.add),
            inc=("v", 1), waits=[("s", abs_marks[-1])])
        t_ps = E("tensor", lambda e: e.matmul(self.pb[6][0:1, 0:1], l['onescol'],
                                              qacc[:, 36:37], start=True, stop=True),
                 inc=("t", 1), waits=[("v", t_sum)] + self.bank_wait(6))
        t_mn = E("vector", lambda e: e.tensor_scalar(
            qacc[0:1, 37:38], self.pb[6][0:1, 0:1], 1.0 / (nchunk * 128 * width), 1e-5,
            op0=OP.mult, op1=OP.max), inc=("v", 1), waits=[("t", t_ps)])
        t_rep = E("tensor", lambda e: e.matmul(self.pb[6][:, 1:2], l['onesrow128'],
                                               qacc[0:1, 37:38], start=True, stop=True),
                  inc=("t", 1), waits=[("v", t_mn)])
        t_sinv = E("vector", lambda e: e.tensor_copy(l['qsc'][:, slot:slot + 1],
                                                     self.pb[6][:, 1:2]),
                   inc=("v", 1), waits=[("t", t_rep)])
        self.bankfree[6] = ("v", t_sinv)
        self.vgap()
        t_s = E("vector", lambda e: e.reciprocal(
            l['qsc'][:, 3 + slot:4 + slot], l['qsc'][:, slot:slot + 1]),
            inc=("v", 1), waits=[("v", t_sinv)])
        self.vgap()
        st_marks = []
        tq_marks = [0, 0]
        for c in range(nchunk):
            stg = qst[:, (c % 2) * 4096:(c % 2) * 4096 + width]
            qb = qbf[:, (c % 2) * 4096:(c % 2) * 4096 + width]
            wt = [("s", abs_marks[-1])]
            if c >= 2:
                wt = [("sq", st_marks[c - 2]), ("v", tq_marks[c % 2])]
            tl = self.dma_ld(stg, dsrc[c * 128:(c + 1) * 128, :], "l%d" % (c % 2),
                             waits=wt)
            f1 = E("vector", lambda e: e.tensor_scalar(
                stg, stg, l['qsc'][:, 3 + slot:4 + slot], WCLIP, op0=OP.mult, op1=OP.min),
                inc=("v", 1), waits=[("l%d" % (c % 2), tl), ("v", t_s)])
            f2 = E("vector", lambda e: e.tensor_scalar(
                stg, stg, -WCLIP, MAGIC, op0=OP.max, op1=OP.add), inc=("v", 1),
                waits=[("v", f1)])
            t_q = E("vector", lambda e: e.tensor_scalar(
                qb, stg, MAGIC, None, op0=OP.subtract), inc=("v", 1), waits=[("v", f2)])
            tq_marks[c % 2] = t_q
            sm = self.dma_st(ddst[c, :, :], qb, "sq", waits=[("v", t_q)])
            st_marks.append(sm)
        self.qst_free = list(tq_marks)

    def ph_sweep(self, dirn):
        w, l, d = self.w, self.l, self.d
        E = w.E
        xT = d['xT'] if dirn == 0 else d['xTf']
        stage = l['stage']

        cast_marks = []

        def ldcast(dst_bf, src_ap):
            buf = len(cast_marks) % 2
            stg = stage[:, buf * 1024:buf * 1024 + 1024]
            wt = [("v", self.stg_free[buf])] if self.stg_free[buf] else []
            tl = self.dma_ld(stg, src_ap, "l%d" % buf, waits=wt)
            m = E("vector", lambda e: e.tensor_copy(dst_bf, stg), inc=("v", 1),
                  waits=[("l%d" % buf, tl)])
            self.stg_free[buf] = m
            cast_marks.append(m)

        for kc in range(8):
            ldcast(l['wzb'][:, kc * 1024:(kc + 1) * 1024],
                   d[f'win{dirn}'][kc * 128:(kc + 1) * 128, :])
        for dc in range(NDC):
            ldcast(l['woutb'][:, dc * 1024:(dc + 1) * 1024],
                   d[f'wout{dirn}'][dc * 128:(dc + 1) * 128, :])
        t_wx = self.dma_ld(l['wxf'][:, 0:4 * 96],
                           d[f'wx{dirn}'][:, :].rearrange("(dc p) r -> p dc r", p=128),
                           "lw")
        t_wdt = self.dma_ld(l['wdtf'][0:RK, :], d[f'wdt{dirn}'][:, :], "lw")
        self.st_sweep = dict(t_wcast=cast_marks[-1], t_wx=t_wx, t_wdt=t_wdt)
        self.stg4_free = [self.stg_free[0], self.stg_free[0],
                          self.stg_free[1], self.stg_free[1]]

        E("vector", lambda e: e.memset(l['carry'][:, 0:64], 0.0))
        for dc in range(NDC):
            E("vector", lambda e: e.memset(
                l['xib'][:, dc * 1040 + 13:dc * 1040 + 16], 0.0))
        self.sweep_v0 = E("vector", lambda e: e.nop(), inc=("v", 1))

        self.ph_A_load(dirn, 0, xT)
        for j in range(NSC):
            self.ph_A(dirn, j, xT)
            if j + 1 < NSC:
                self.ph_A_load(dirn, j + 1, xT)
            self.ph_B(dirn, j)

    def ph_A_load(self, dirn, j, xT):
        w, l, d = self.w, self.l, self.d
        E = w.E
        t0 = j * SC
        for ts in range(2):
            tbuf = ts % 2
            xc_marks = []
            for kc in range(8):
                q4 = kc % 4
                stg = l['stage'][:, q4 * 512:q4 * 512 + 512]
                wt = [self.xbt_free[tbuf]] if kc < 2 and self.xbt_free[tbuf][1] > 0 else []
                if kc < 2 and self.dbg_xbt_st[tbuf]:
                    wt = wt + [("sd", self.dbg_xbt_st[tbuf])]
                if self.stg4_free[q4]:
                    wt = wt + [("v", self.stg4_free[q4])]
                tl = self.dma_ld(stg, xT[kc * 128:(kc + 1) * 128,
                                         t0 + ts * 512:t0 + (ts + 1) * 512],
                                 "l%d" % q4, waits=wt)
                m = E("vector", lambda e: e.tensor_copy(
                    l['xbt'][:, (tbuf * 8 + kc) * 512:(tbuf * 8 + kc + 1) * 512], stg),
                    inc=("v", 1), waits=[("l%d" % q4, tl)])
                self.stg4_free[q4] = m
                self.stg_free[q4 // 2] = m
                xc_marks.append(m)
            self.xcast[tbuf] = xc_marks[-1]
            if 'xbt' in DEBUG_SWEEP:
                self.dbg_xbt_st[tbuf] = self.dma_st(
                    d['xbtd'][dirn, j, ts],
                    l['xbt'][:, tbuf * 4096:(tbuf + 1) * 4096], "sd",
                    waits=[("v", xc_marks[-1])])

    def ph_A(self, dirn, j, xT):
        w, l, d, pb = self.w, self.l, self.d, self.pb
        E = w.E
        vo = 96 * dirn
        t0 = j * SC
        st = self.st_sweep
        u_marks = []
        sz_marks = []
        for ts in range(2):
            tbuf = ts % 2
            t_xcast = self.xcast[tbuf]
            mm_last = 0
            for mchunk in range(8):
                bk = mchunk % 2
                for kc in range(8):
                    wt = []
                    if kc == 0:
                        wt = [("v", t_xcast), ("v", st['t_wcast'])] + self.bank_wait(bk)
                    mm_last = E("tensor", lambda e: e.matmul(
                        pb[bk][:, :],
                        l['wzb'][:, kc * 1024 + mchunk * 128:kc * 1024 + (mchunk + 1) * 128],
                        l['xbt'][:, (tbuf * 8 + kc) * 512:(tbuf * 8 + kc + 1) * 512],
                        start=(kc == 0), stop=(kc == 7)), inc=("t", 1), waits=wt)
                if mchunk < 4:
                    dc = mchunk
                    wtx = [("t", mm_last)]
                    if ts == 0 and mchunk == 0 and self.dbg_xib_st:
                        wtx.append(("sd", self.dbg_xib_st))
                    m = E("vector", lambda e: e.tensor_copy(
                        l['xib'][:, dc * 1040 + 16 + ts * 512:dc * 1040 + 16 + (ts + 1) * 512],
                        pb[bk][:, :]), inc=("v", 1), waits=wtx)
                    self.bankfree[bk] = ("v", m)
                else:
                    dc = mchunk - 4
                    wt = [("t", mm_last)]
                    if self.sz_free[1] > 0:
                        wt.append(self.sz_free)
                    m = E("scalar", lambda e: e.activation(
                        out=l['szbuf'][:, dc * 1024 + ts * 512:dc * 1024 + (ts + 1) * 512],
                        in_=pb[bk][:, :], func=AF.Silu), inc=("s", 1), waits=wt)
                    sz_marks.append(m)
                    self.bankfree[bk] = ("s", m)
            self.xbt_free[tbuf] = ("t", mm_last)

        cw = l['vecs']
        for dc in range(NDC):
            base = dc * 1040 + 16
            wt0 = []
            if dc > 0:
                wt0 = [("s", u_marks[dc - 1])]
            elif self.cv_free[1] > 0:
                wt0 = [self.cv_free]
            t_cv = E("vector", lambda e: e.tensor_scalar(
                l['cvtmp'][:, 0:1024], l['xib'][:, base - 3:base - 3 + 1024],
                cw[:, vo + dc * 4 + 0:vo + dc * 4 + 1],
                cw[:, vo + 16 + dc:vo + 16 + dc + 1], op0=OP.mult, op1=OP.add),
                inc=("v", 1), waits=wt0)
            for k in range(1, 4):
                t_cv = E("vector", lambda e: e.scalar_tensor_tensor(
                    l['cvtmp'][:, 0:1024], l['xib'][:, base - (3 - k):base - (3 - k) + 1024],
                    cw[:, vo + dc * 4 + k:vo + dc * 4 + k + 1],
                    l['cvtmp'][:, 0:1024], op0=OP.mult, op1=OP.add), inc=("v", 1))
            wtu = [("v", t_cv)]
            if self.u_free[1] > 0 and dc == 0:
                wtu.append(self.u_free)
            if dc == 0 and self.dbg_u_st:
                wtu.append(("sd", self.dbg_u_st))
            m = E("scalar", lambda e: e.activation(
                out=l['ubuf'][:, dc * 1024:(dc + 1) * 1024], in_=l['cvtmp'][:, 0:1024],
                func=AF.Silu), inc=("s", 1), waits=wtu)
            u_marks.append(m)
            E("vector", lambda e: e.tensor_copy(
                l['xib'][:, dc * 1040 + 13:dc * 1040 + 16],
                l['xib'][:, base + 1021:base + 1024]))
        self.cv_free = ("s", u_marks[-1])
        self.u_ready = u_marks
        self.sz_ready = sz_marks[-1]
        if 'u' in DEBUG_SWEEP:
            self.dbg_u_st = self.dma_st(d['ud'][dirn, j], l['ubuf'][:, 0:4096],
                                        "sd", waits=[("s", u_marks[-1])])
        if 'xib' in DEBUG_SWEEP:
            self.dbg_xib_st = self.dma_st(d['xibd'][dirn, j], l['xib'][:, 0:4160],
                                          "sd", waits=[("s", u_marks[-1])])

        ev_marks = []
        for ts in range(2):
            mm = 0
            for dc in range(NDC):
                wt = [("s", u_marks[dc])]
                if dc == 0:
                    wt += self.bank_wait(2) + [("lw", st['t_wx'])]
                mm = E("tensor", lambda e: e.matmul(
                    pb[2][0:96, :], l['wxf'][:, dc * 96:(dc + 1) * 96],
                    l['ubuf'][:, dc * 1024 + ts * 512:dc * 1024 + (ts + 1) * 512],
                    start=(dc == 0), stop=(dc == 3)), inc=("t", 1), waits=wt)
            wt = [("t", mm)]
            if self.xdbc_free[1] > 0 and ts == 0:
                wt.append(self.xdbc_free)
            m = E("vector", lambda e: e.tensor_copy(
                l['xdbcv'][0:96, ts * 512:(ts + 1) * 512], pb[2][0:96, :]),
                inc=("v", 1), waits=wt)
            self.bankfree[2] = ("v", m)
            ev_marks.append(m)
        t_st = self.dma_st(d['arin'][dirn, j], l['xdbcv'][0:96, 0:1024], "sa",
                           waits=[("v", ev_marks[-1])])
        self.xdbc_free = ("sa", t_st)
        t_ar = E("gpsimd", lambda e: e.collective_compute(
            "AllReduce", OP.add, replica_groups=RG,
            ins=[d['arin'][dirn, j]], outs=[d['arout'][dirn, j]]),
            inc=("g", 1), waits=[("sa", t_st)])
        self.ar_done = t_ar

    def ph_B(self, dirn, j):
        w, l, d, pb = self.w, self.l, self.d, self.pb
        E = w.E
        vo = 96 * dirn
        t_ar = self.ar_done
        st = self.st_sweep

        wt = [("g", t_ar)]
        if self.dtin_free[1] > 0:
            wt.append(self.dtin_free)
        t_dtin = self.dma_ld(l['dtin'][0:RK, 0:1024], d['arout'][dirn, j][0:RK, :],
                             "ldt", waits=wt)
        dt_marks = {}
        last_mm = 0
        for dc in range(NDC):
            for ts in range(2):
                wt = [("ldt", t_dtin), ("lw", st['t_wdt'])] + self.bank_wait(3)
                if self.dt_free[1] > 0 and dc == 0 and ts == 0:
                    wt.append(self.dt_free)
                last_mm = E("tensor", lambda e: e.matmul(
                    pb[3][:, :], l['wdtf'][0:RK, dc * 128:(dc + 1) * 128],
                    l['dtin'][0:RK, ts * 512:(ts + 1) * 512],
                    start=True, stop=True), inc=("t", 1), waits=wt)
                t_e = E("scalar", lambda e: e.activation(
                    out=l['dttmp'][:, 0:512], in_=pb[3][:, :], func=AF.Exp,
                    bias=l['vecs'][:, vo + 20 + dc:vo + 20 + dc + 1]),
                    inc=("s", 1), waits=[("t", last_mm)])
                self.bankfree[3] = ("s", t_e)
                m = E("scalar", lambda e: e.activation(
                    out=l['dtbuf'][:, dc * 1024 + ts * 512:dc * 1024 + (ts + 1) * 512],
                    in_=l['dttmp'][:, 0:512], func=AF.Ln, bias=1.0), inc=("s", 1))
            dt_marks[dc] = m
        self.dtin_free = ("t", last_mm)

        dtu_marks = {}
        for dc in range(NDC):
            wt = [("s", dt_marks[dc]), ("s", self.u_ready[dc])]
            if dc == 0 and self.scan_gp_last:
                wt.append(("g", self.scan_gp_last))
            m = E("vector", lambda e: e.tensor_tensor(
                l['dtubuf'][:, dc * 1024:(dc + 1) * 1024],
                l['dtbuf'][:, dc * 1024:(dc + 1) * 1024],
                l['ubuf'][:, dc * 1024:(dc + 1) * 1024], op=OP.mult),
                inc=("v", 1), waits=wt)
            dtu_marks[dc] = m

        scan_slots = {}
        bfree = {}
        cfree = {}
        last_y1 = 0
        ysz_marks = {}
        for dcp in range(2):
            dcs = [dcp * 2, dcp * 2 + 1]
            for s in range(DS):
                slot = s % 2
                wtb = [("g", t_ar)]
                if (dcp, s - 2) in bfree:
                    wtb += bfree[(dcp, s - 2)]
                elif dcp > 0 and (0, s + DS - 2) in bfree:
                    wtb += bfree[(0, s + DS - 2)]
                t_bb = self.dma_ld(l['brep'][:, slot * 1024:(slot + 1) * 1024],
                                   d['arout'][dirn, j][RK + s:RK + s + 1, :]
                                   .partition_broadcast(128).squeeze(1),
                                   "lb%d" % slot, waits=wtb)
                wtc = []
                if (dcp, s - 2) in cfree:
                    wtc += cfree[(dcp, s - 2)]
                elif dcp > 0 and (0, s + DS - 2) in cfree:
                    wtc += cfree[(0, s + DS - 2)]
                t_cb = self.dma_ld(l['crep'][:, slot * 1024:(slot + 1) * 1024],
                                   d['arout'][dirn, j][RK + DS + s:RK + DS + s + 1, :]
                                   .partition_broadcast(128).squeeze(1),
                                   "lc%d" % slot, waits=wtc)
                tb_marks = []
                tz_marks = []
                for di, dc in enumerate(dcs):
                    aslot = di
                    cidx = s * 4 + dc
                    prev = scan_slots.get(aslot, (0, 0, 0))
                    t_a = E("scalar", lambda e: e.activation(
                        out=l['abuf'][:, aslot * 1024:(aslot + 1) * 1024],
                        in_=l['dtbuf'][:, dc * 1024:(dc + 1) * 1024], func=AF.Exp,
                        scale=l['vecs'][:, vo + 28 + dc * 16 + s:vo + 28 + dc * 16 + s + 1]),
                        inc=("s", 1),
                        waits=[("s", dt_marks[dc])] +
                              ([("v", prev[0])] if prev[0] else []))
                    t_b = E("vector", lambda e: e.tensor_tensor(
                        l['bbuf'][:, aslot * 1024:(aslot + 1) * 1024],
                        l['dtubuf'][:, dc * 1024:(dc + 1) * 1024],
                        l['brep'][:, slot * 1024:(slot + 1) * 1024], op=OP.mult),
                        inc=("v", 1),
                        waits=[("lb%d" % slot, t_bb), ("v", dtu_marks[dc])])
                    t_h = E("vector", lambda e: e.tensor_tensor_scan(
                        l['hbuf'][:, aslot * 1024:(aslot + 1) * 1024],
                        l['abuf'][:, aslot * 1024:(aslot + 1) * 1024],
                        l['bbuf'][:, aslot * 1024:(aslot + 1) * 1024],
                        l['carry'][:, cidx:cidx + 1], op0=OP.mult, op1=OP.add),
                        inc=("v", 1), waits=[("s", t_a)])
                    t_z = E("vector", lambda e: e.tensor_tensor(
                        l['zbuf'][:, aslot * 1024:(aslot + 1) * 1024],
                        l['hbuf'][:, aslot * 1024:(aslot + 1) * 1024],
                        l['crep'][:, slot * 1024:(slot + 1) * 1024], op=OP.mult),
                        inc=("v", 1),
                        waits=[("lc%d" % slot, t_cb)] +
                              ([("t", prev[2])] if prev[2] else []))
                    E("vector", lambda e: e.tensor_copy(
                        l['carry'][:, cidx:cidx + 1],
                        l['hbuf'][:, aslot * 1024 + 1023:aslot * 1024 + 1024]))
                    tb_marks.append(("v", t_b))
                    tz_marks.append(("v", t_z))
                    t_id = 0
                    for half in range(2):
                        bk = 4 + di * 2 + half
                        wt3 = [("v", t_z)]
                        if s == 0:
                            wt3 += self.bank_wait(bk)
                        t_id = E("tensor", lambda e: e.matmul(
                            pb[bk][:, :], l['identb'][:, :],
                            l['zbuf'][:, aslot * 1024 + half * 512:aslot * 1024 + (half + 1) * 512],
                            start=(s == 0), stop=(s == DS - 1),
                            skip_group_check=True), inc=("t", 1), waits=wt3)
                    scan_slots[aslot] = (t_h, t_z, t_id)
                bfree[(dcp, s)] = tb_marks
                cfree[(dcp, s)] = tz_marks
            t_idm = w.c["t"]
            for di, dc in enumerate(dcs):
                for half in range(2):
                    bk = 4 + di * 2 + half
                    wty = [("t", t_idm)]
                    if dcp == 0 and di == 0 and half == 0 and self.cv_free[1] > 0:
                        wty.append(self.cv_free)
                    last_y1 = E("vector", lambda e: e.scalar_tensor_tensor(
                        l['y1buf'][:, half * 512:(half + 1) * 512],
                        l['ubuf'][:, dc * 1024 + half * 512:dc * 1024 + (half + 1) * 512],
                        l['vecs'][:, vo + 24 + dc:vo + 24 + dc + 1],
                        pb[bk][:, :], op0=OP.mult, op1=OP.add),
                        inc=("v", 1), waits=wty)
                    self.bankfree[bk] = ("v", last_y1)
                if dirn == 0:
                    outap = l['yszb'][:, dc * 1024:(dc + 1) * 1024]
                else:
                    outap = l['yszb'][:, dc * 1024:(dc + 1) * 1024][:, ::-1]
                wt = [("s", self.sz_ready)]
                if self.ysz_free[1] > 0 and dcp == 0 and di == 0:
                    wt.append(self.ysz_free)
                ysz_marks[dc] = E("vector", lambda e: e.tensor_tensor(
                    outap, l['y1buf'][:, 0:1024],
                    l['szbuf'][:, dc * 1024:(dc + 1) * 1024], op=OP.mult),
                    inc=("v", 1), waits=wt)
        self.u_free = ("v", last_y1)
        self.sz_free = ("v", max(ysz_marks.values()))
        self.dt_free = ("v", max(dtu_marks.values()))
        self.cv_free = ("v", max(ysz_marks.values()))
        t_scan_done = max(ysz_marks.values())

        jj = j if dirn == 0 else NSC - 1 - j
        self.rs_ready.setdefault(jj, [])
        ps_marks = list(self.ps_marks)
        last_wout = 0
        for half in range(2):
            for mc in range(8):
                bk = mc % 2
                mm = 0
                for dc in range(NDC):
                    wt = []
                    if dc == 0:
                        wt = [("v", t_scan_done)] + self.bank_wait(bk)
                    mm = E("tensor", lambda e: e.matmul(
                        pb[bk][:, :],
                        l['woutb'][:, dc * 1024 + mc * 128:dc * 1024 + (mc + 1) * 128],
                        l['yszb'][:, dc * 1024 + half * 512:dc * 1024 + (half + 1) * 512],
                        start=(dc == 0), stop=(dc == 3)), inc=("t", 1), waits=wt)
                last_wout = mm
                pslot = mc % 2
                wtp = [("sp%d" % pslot, ps_marks[pslot])] if ps_marks[pslot] > 0 else []
                t_ev = E("scalar", lambda e: e.copy(
                    out=l['pstage'][:, pslot * 512:(pslot + 1) * 512], in_=pb[bk][:, :]),
                    inc=("s", 1), waits=[("t", mm)] + wtp)
                self.bankfree[bk] = ("s", t_ev)
                t_ps = self.dma_st(
                    d['rsin'][jj, 2 * half:2 * half + 2, dirn, mc, :, :]
                    .rearrange("q p i -> p q i"),
                    l['pstage'][:, pslot * 512:(pslot + 1) * 512]
                    .rearrange("p (q i) -> p q i", q=2),
                    "sp%d" % pslot, waits=[("s", t_ev)])
                ps_marks[pslot] = t_ps
        self.ps_marks = ps_marks
        self.ysz_free = ("t", last_wout)

        if dirn == 1:
            t_rs = E("gpsimd", lambda e: e.collective_compute(
                "ReduceScatter", OP.add, replica_groups=RG,
                ins=[d['rsin'][jj]], outs=[d['rsout'][jj]]),
                inc=("g", 1),
                waits=[("sp0", w.c["sp0"]), ("sp1", w.c["sp1"])])
            self.rs_done[jj] = t_rs

    def ln(self, src, dst, wcol=None, bcol=None, ready=()):
        w, l, pb = self.w, self.l, self.pb
        E = w.E
        mm_q = 0
        for mc in range(8):
            wt = list(ready) if mc == 0 else [("t", mm_q)]
            t_sq = E("scalar", lambda e: e.activation(
                out=l['p_chunk'][:, :], in_=src(mc), func=AF.Square), inc=("s", 1),
                waits=wt)
            wts = (list(ready) + self.bank_wait(2) + self.bank_wait(3)) if mc == 0 else []
            mm_s = E("tensor", lambda e: e.matmul(
                pb[2][0:1, :], l['onescol'], src(mc), start=(mc == 0), stop=(mc == 7),
                skip_group_check=True), inc=("t", 1), waits=wts)
            mm_q = E("tensor", lambda e: e.matmul(
                pb[3][0:1, :], l['onescol'], l['p_chunk'][:, :], start=(mc == 0),
                stop=(mc == 7), skip_group_check=True), inc=("t", 1), waits=[("s", t_sq)])
        r = l['p_rows']
        r_mu, r_m2, r_t = (r[0:1, i * 512:(i + 1) * 512] for i in range(3))
        r_r = r_m2
        t1 = E("vector", lambda e: e.tensor_scalar(r_mu, pb[2][0:1, :], 1.0 / DM, None,
                                                   op0=OP.mult), inc=("v", 1),
               waits=[("t", mm_q)])
        E("vector", lambda e: e.tensor_scalar(r_m2, pb[3][0:1, :], 1.0 / DM, None,
                                              op0=OP.mult))
        E("vector", lambda e: e.tensor_tensor(r_t, r_mu, r_mu, op=OP.mult))
        E("vector", lambda e: e.tensor_tensor(r_m2, r_m2, r_t, op=OP.subtract))
        t3 = E("vector", lambda e: e.tensor_scalar(r_m2, r_m2, EPS, None, op0=OP.add),
               inc=("v", 1))
        E("scalar", lambda e: e.activation(out=r_t, in_=r_m2, func=AF.Ln),
          waits=[("v", t3)])
        t5 = E("scalar", lambda e: e.activation(out=r_r, in_=r_t, func=AF.Exp, scale=-0.5),
               inc=("s", 1))
        mrep = E("tensor", lambda e: e.matmul(pb[2][:, :], l['onesrow128'], r_mu,
                                              start=True, stop=True), inc=("t", 1),
                 waits=[("v", t1)])
        rrep = E("tensor", lambda e: e.matmul(pb[3][:, :], l['onesrow128'], r_r,
                                              start=True, stop=True), inc=("t", 1),
                 waits=[("s", t5)])
        last = 0
        for mc in range(8):
            E("vector", lambda e: e.tensor_tensor(
                l['p_chunk'][:, :], src(mc), pb[2][:, :], op=OP.subtract),
              waits=([("t", rrep)] if mc == 0 else []))
            E("vector", lambda e: e.tensor_tensor(
                l['p_chunk'][:, :], l['p_chunk'][:, :], pb[3][:, :], op=OP.mult))
            if wcol is not None:
                last = E("vector", lambda e: e.tensor_scalar(
                    dst(mc), l['p_chunk'][:, :], wcol(mc), bcol(mc),
                    op0=OP.mult, op1=OP.add), inc=("v", 1))
            else:
                last = E("vector", lambda e: e.tensor_copy(dst(mc), l['p_chunk'][:, :]),
                         inc=("v", 1))
        self.bankfree[2] = ("v", last)
        self.bankfree[3] = ("v", last)
        return last

    def ph_post(self):
        for ts in range(4):
            self.post_ts(ts)

    def post_ts(self, ts):
        w, l, d, pb = self.w, self.l, self.d, self.pb
        E = w.E
        pv = l['pvec']
        CH = lambda buf, mc: l[buf][:, mc * 512:(mc + 1) * 512]
        j0, j1 = 2 * ts, 2 * ts + 1
        t_xp = self.dma_ld(l['p_xp'][:, :],
                           d['xpost'][:, ts * 512:(ts + 1) * 512]
                           .rearrange("(kc p) t -> p kc t", p=128),
                           "lxp", waits=[("v", self.xp_free)] if self.xp_free else [])
        t_sum = 0
        for dirn in range(2):
            ld_m = 0
            for mc in range(8):
                for jb, j in enumerate((j0, j1)):
                    wt = []
                    if mc == 0 and jb == 0:
                        wt = [("g", self.rs_done[j0]), ("g", self.rs_done[j1])]
                        if self.rsy_free:
                            wt.append(("v", self.rsy_free))
                    ld_m = self.dma_ld(
                        l['p_rsy'][:, mc * 512 + jb * 256:mc * 512 + (jb + 1) * 256],
                        d['rsout'][j, dirn, mc, :, :], "lry", waits=wt)
            add_m = 0
            for mc in range(8):
                wt = [("lry", ld_m), ("lxp", t_xp)] if mc == 0 else []
                add_m = E("vector", lambda e: e.tensor_tensor(
                    CH('p_B', mc), CH('p_rsy', mc), CH('p_xp', mc), op=OP.add),
                    inc=("v", 1), waits=wt)
            self.rsy_free = add_m
            dstbuf = 'p_C' if dirn == 0 else 'p_B'
            self.ln(lambda mc: CH('p_B', mc), lambda mc: CH(dstbuf, mc),
                    wcol=lambda mc: pv[:, 24 + mc:25 + mc],
                    bcol=lambda mc: pv[:, 32 + mc:33 + mc],
                    ready=[("v", add_m), ("lw", self.t_pv)])
            if dirn == 1:
                for mc in range(8):
                    t_sum = E("vector", lambda e: e.tensor_tensor(
                        CH('p_C', mc), CH('p_C', mc), CH('p_B', mc), op=OP.add),
                        inc=("v", 1))
                if DEBUG_DUMP:
                    for mc in range(8):
                        self.dma_st(d['yd'][ts, mc], CH('p_C', mc), "sd",
                                    waits=[("v", t_sum)])

        cast_m = 0
        for mc in range(8):
            wt = []
            if mc == 0:
                wt = [("v", t_sum)]
                if self.out_st:
                    wt.append(("so", self.out_st))
                if self.mid_q_free:
                    wt.append(("t", self.mid_q_free))
            cast_m = E("vector", lambda e: e.tensor_copy(
                l['p_ybf'][:, mc * 512:(mc + 1) * 512], CH('p_C', mc)), inc=("v", 1),
                waits=wt)
        mid_m = 0
        for mc in range(16):
            buf = mc % 2
            stg = l['p_stg'][:, buf * 1024:(buf + 1) * 1024]
            wt = [("v", self.pstg_free[buf])] if self.pstg_free[buf] else []
            tl = self.dma_ld(stg, d['ffw1'][:, mc * 128:(mc + 1) * 128]
                             .rearrange("(kc p) m -> p kc m", p=128),
                             "l%d" % buf, waits=wt)
            cst = E("vector", lambda e: e.tensor_copy(
                l['p_wt'][:, buf * 1024:(buf + 1) * 1024], stg), inc=("v", 1),
                waits=[("l%d" % buf, tl)])
            self.pstg_free[buf] = cst
            bk = mc % 2
            mm = 0
            for kc in range(8):
                wt = []
                if kc == 0:
                    wt = [("v", cast_m), ("v", cst)] + self.bank_wait(bk)
                mm = E("tensor", lambda e: e.matmul(
                    pb[bk][:, :],
                    l['p_wt'][:, buf * 1024 + kc * 128:buf * 1024 + (kc + 1) * 128],
                    l['p_ybf'][:, kc * 512:(kc + 1) * 512],
                    start=(kc == 0), stop=(kc == 7)), inc=("t", 1), waits=wt)
            wt = [("t", mm), ("lw", self.t_pv)]
            if self.mid_free and mc < 2:
                wt.append(("t", self.mid_free))
            mid_m = E("scalar", lambda e: e.activation(
                out=l['p_mid'][:, mc * 512:(mc + 1) * 512], in_=pb[bk][:, :],
                func=AF.Relu, bias=pv[:, mc:mc + 1]), inc=("s", 1), waits=wt)
            self.bankfree[bk] = ("s", mid_m)
        ff2_last = 0
        for mc in range(8):
            buf = mc % 2
            cst = 0
            for hf in range(2):
                stg = l['p_stg'][:, buf * 1024:(buf + 1) * 1024]
                wt = [("v", self.pstg_free[buf])] if self.pstg_free[buf] else []
                tl = self.dma_ld(stg, d['ffw2'][hf * 1024:(hf + 1) * 1024,
                                                mc * 128:(mc + 1) * 128]
                                 .rearrange("(kc p) m -> p kc m", p=128),
                                 "l%d" % buf, waits=wt)
                cst = E("vector", lambda e: e.tensor_copy(
                    l['p_ws2'][:, buf * 4096 + hf * 1024:buf * 4096 + (hf + 1) * 1024],
                    stg), inc=("v", 1), waits=[("l%d" % buf, tl)])
                self.pstg_free[buf] = cst
                self.ffw2_cast[buf] = cst
            bk = 2 + mc % 2
            mm = 0
            for kc in range(16):
                wt = []
                if kc == 0:
                    wt = [("v", cst), ("s", mid_m)] + self.bank_wait(bk)
                mm = E("tensor", lambda e: e.matmul(
                    pb[bk][:, :],
                    l['p_ws2'][:, buf * 4096 + kc * 128:buf * 4096 + (kc + 1) * 128],
                    l['p_mid'][:, kc * 512:(kc + 1) * 512],
                    start=(kc == 0), stop=(kc == 15)), inc=("t", 1), waits=wt)
            ff2_last = E("vector", lambda e: e.scalar_tensor_tensor(
                CH('p_B', mc), pb[bk][:, :], pv[:, 16 + mc:17 + mc], CH('p_C', mc),
                op0=OP.add, op1=OP.add), inc=("v", 1), waits=[("t", mm)])
            self.bankfree[bk] = ("v", ff2_last)
        self.mid_free = w.c["t"]
        t_fin = self.ln(lambda mc: CH('p_B', mc), lambda mc: CH('p_B', mc),
                        wcol=lambda mc: pv[:, 40 + mc:41 + mc],
                        bcol=lambda mc: pv[:, 48 + mc:49 + mc],
                        ready=[("v", ff2_last)])
        if DEBUG_DUMP:
            for mc in range(8):
                self.dma_st(d['attnd'][ts, mc], CH('p_B', mc), "sd",
                            waits=[("v", t_fin)])
        t_add = 0
        for mc in range(8):
            t_add = E("vector", lambda e: e.tensor_tensor(
                CH('p_D', mc), CH('p_xp', mc), CH('p_B', mc), op=OP.add), inc=("v", 1))
        self.xp_free = t_add
        self.x1_ready = self.ln(lambda mc: CH('p_D', mc), lambda mc: CH('p_D', mc),
                                ready=[("v", t_add)])
        if DEBUG_DUMP:
            for mc in range(8):
                self.dma_st(d['x1d'][ts, mc], CH('p_D', mc), "sd",
                            waits=[("v", self.x1_ready)])
            self.barrier()

        if POST_LIMIT == 'ff':
            for mc in range(8):
                E("vector", lambda e: e.tensor_copy(CH('p_B', mc), CH('p_D', mc)))
            E("vector", lambda e: e.nop(), inc=("v", 1))
        else:
            self.post_ffn(ts)

        t_fln = self.ln(lambda mc: CH('p_B', mc), lambda mc: CH('p_B', mc),
                        ready=[("v", w.c["v"])])
        for tt in range(4):
            cp = 0
            for g in range(2):
                bk = 4 + g
                tr = 0
                for i in range(4):
                    mc = 4 * g + i
                    wt = self.bank_wait(bk) if i == 0 else []
                    if tt == 0 and g == 0 and i == 0:
                        wt = wt + [("v", t_fln)]
                    tr = E("tensor", lambda e: e.transpose(
                        pb[bk][:, i * 128:(i + 1) * 128],
                        l['p_B'][:, mc * 512 + tt * 128:mc * 512 + tt * 128 + 128],
                        l['identf'][:, :]), inc=("t", 1), waits=wt)
                wtc = [("t", tr)]
                if g == 0 and self.out_st:
                    wtc.append(("so", self.out_st))
                cp = E("vector", lambda e: e.tensor_copy(
                    l['p_outst'][:, g * 512:(g + 1) * 512], pb[bk][:, :]),
                    inc=("v", 1), waits=wtc)
                self.bankfree[bk] = ("v", cp)
            self.out_st = self.dma_st(
                d['out'][ts * 512 + tt * 128:ts * 512 + (tt + 1) * 128, :],
                l['p_outst'][:, :], "so", waits=[("v", cp)])

    def post_ffn(self, ts):
        w, l, d, pb = self.w, self.l, self.d, self.pb
        E = w.E
        mx = l['p_mx']
        xq_m = 0
        for tt in range(4):
            trm = 0
            for mc in range(8):
                bk = 4 + mc // 4
                wt = self.bank_wait(bk) if mc % 4 == 0 else []
                if tt == 0 and mc == 0:
                    wt = wt + [("v", self.x1_ready)]
                trm = E("tensor", lambda e: e.transpose(
                    pb[bk][:, (mc % 4) * 128:(mc % 4 + 1) * 128],
                    l['p_D'][:, mc * 512 + tt * 128:mc * 512 + tt * 128 + 128],
                    l['identf'][:, :]), inc=("t", 1), waits=wt)
            c1, c2 = mx[:, 20:21], mx[:, 21:22]
            mxc, rc, c2t, sc = mx[:, 16:17], mx[:, 17:18], mx[:, 18:19], mx[:, 19:20]
            E("vector", lambda e: e.tensor_reduce(c1, pb[4][:, :], axis=AX, op=OP.max,
                                                  apply_absolute_value=True),
              waits=[("t", trm)])
            r2 = E("vector", lambda e: e.tensor_reduce(c2, pb[5][:, :], axis=AX,
                                                       op=OP.max,
                                                       apply_absolute_value=True),
                   inc=("v", 1))
            m1 = E("vector", lambda e: e.scalar_tensor_tensor(
                mxc, c1, 1e-5, c2, op0=OP.max, op1=OP.max), inc=("v", 1),
                waits=[("v", r2)])
            rc1 = E("vector", lambda e: e.reciprocal(rc, mxc), inc=("v", 1),
                    waits=[("v", m1)])
            ct1 = E("vector", lambda e: e.tensor_scalar(
                c2t, mxc, float(np.float32(1.0 / 127.0)), None, op0=OP.mult),
                inc=("v", 1))
            sc1 = E("vector", lambda e: e.tensor_scalar(sc, rc, 127.0, None,
                                                        op0=OP.mult), inc=("v", 1),
                    waits=[("v", rc1)])
            E("vector", lambda e: e.tensor_tensor(
                mx[:, tt:tt + 1], c2t, l['qsc'][:, 0:1], op=OP.mult),
              waits=[("v", ct1)])
            t_sc = E("vector", lambda e: e.tensor_tensor(
                mx[:, 4 + tt:5 + tt], c2t, l['qsc'][:, 1:2], op=OP.mult), inc=("v", 1))
            qm = 0
            for hb, bk in enumerate((4, 5)):
                E("vector", lambda e: e.tensor_scalar(
                    l['p_chunk'][:, :], pb[bk][:, :], sc, MAGIC, op0=OP.mult, op1=OP.add),
                  waits=[("v", t_sc), ("v", sc1)])
                qm = E("vector", lambda e: e.tensor_scalar(
                    l['p_qtm'][:, hb * 512:(hb + 1) * 512], l['p_chunk'][:, :],
                    MAGIC, None, op0=OP.subtract), inc=("v", 1))
            self.bankfree[4] = ("v", qm)
            self.bankfree[5] = ("v", qm)
            if DEBUG_DUMP:
                dm_ = self.dma_st(d['qtmd'][ts, tt], l['p_qtm'][:, :], waits=[("v", qm)])
                self.barrier()
            for g in range(2):
                bk = 6 + g
                tr2 = 0
                for i in range(4):
                    mc = 4 * g + i
                    wt = [("v", qm)] + (self.bank_wait(bk) if i == 0 else [])
                    tr2 = E("tensor", lambda e: e.transpose(
                        self.pbh[bk][:, i * 128:(i + 1) * 128],
                        l['p_qtm'][:, mc * 128:(mc + 1) * 128], l['identb'][:, :]),
                        inc=("t", 1), waits=wt)
                for i in range(4):
                    mc = 4 * g + i
                    xq_m = E("scalar", lambda e: e.copy(
                        out=l['p_xq'][:, mc * 512 + tt * 128:mc * 512 + tt * 128 + 128],
                        in_=self.pbh[bk][:, i * 128:(i + 1) * 128]), inc=("s", 1),
                        waits=[("t", tr2)] if i == 0 else [])
                self.bankfree[bk] = ("s", xq_m)

        self.sc_ready = w.c["v"]
        if POST_LIMIT == 'quant':
            for mc in range(8):
                E("vector", lambda e: e.tensor_copy(
                    l['p_B'][:, mc * 512:(mc + 1) * 512],
                    l['p_D'][:, mc * 512:(mc + 1) * 512]), inc=("v", 1),
                    waits=[("s", xq_m)])
            return
        if DEBUG_DUMP:
            for mc in range(8):
                self.dma_st(d['xqd'][ts, mc],
                            l['p_xq'][:, mc * 512:(mc + 1) * 512], "sd",
                            waits=[("s", xq_m)])
            self.dma_st(d['mxd'][ts], mx[:, :], "sd", waits=[("s", xq_m)])
            self.barrier()
        gu_st = self.gu_st
        gu_prev = 0
        red_m = 0
        for mcg in range(8):
            buf = mcg % 2
            wt = [("sq", self.wq_done)]
            if gu_st[buf]:
                wt.append(("sg", gu_st[buf]))
            tg = self.dma_ld(l['p_ws'][:, buf * 4096:buf * 4096 + 4096],
                             d['wgq'][:, :, mcg * 512:(mcg + 1) * 512]
                             .rearrange("k p m -> p k m"), "lg%d" % buf, waits=wt)
            wtu = [("v", self.ffw2_cast[buf])] if self.ffw2_cast[buf] else []
            if gu_st[buf]:
                wtu.append(("sg", gu_st[buf]))
            tu = self.dma_ld(l['p_ws2'][:, buf * 4096:buf * 4096 + 4096],
                             d['wuq'][:, :, mcg * 512:(mcg + 1) * 512]
                             .rearrange("k p m -> p k m"), "lu%d" % buf, waits=wtu)
            if mcg < 2:
                self.ffw2_cast[buf] = 0
            for tt in range(4):
                mmg = 0
                for kc in range(8):
                    wt = []
                    if kc == 0:
                        wt = [("lg%d" % buf, tg), ("s", xq_m)] + self.bank_wait(0)
                    mmg = E("tensor", lambda e: e.matmul(
                        pb[0][:, :],
                        l['p_xq'][:, kc * 512 + tt * 128:kc * 512 + tt * 128 + 128],
                        l['p_ws'][:, buf * 4096 + kc * 512:buf * 4096 + (kc + 1) * 512],
                        start=(kc == 0), stop=(kc == 7)), inc=("t", 1), waits=wt)
                wtg = [("t", mmg)]
                if gu_prev:
                    wtg.append(("v", gu_prev))
                wtg.append(("v", self.sc_ready))
                t_g = E("scalar", lambda e: e.activation(
                    out=l['p_chunk'][:, :], in_=pb[0][:, :], func=AF.Silu,
                    scale=mx[:, tt:tt + 1]), inc=("s", 1), waits=wtg)
                self.bankfree[0] = ("s", t_g)
                mmu = 0
                for kc in range(8):
                    wt = []
                    if kc == 0:
                        wt = [("lu%d" % buf, tu)] + self.bank_wait(1)
                    mmu = E("tensor", lambda e: e.matmul(
                        pb[1][:, :],
                        l['p_xq'][:, kc * 512 + tt * 128:kc * 512 + tt * 128 + 128],
                        l['p_ws2'][:, buf * 4096 + kc * 512:buf * 4096 + (kc + 1) * 512],
                        start=(kc == 0), stop=(kc == 7)), inc=("t", 1), waits=wt)
                t_u = E("scalar", lambda e: e.activation(
                    out=l['p_chunk2'][:, :], in_=pb[1][:, :], func=AF.Identity,
                    scale=mx[:, 4 + tt:5 + tt]), inc=("s", 1),
                    waits=[("t", mmu), ("v", self.sc_ready)])
                self.bankfree[1] = ("s", t_u)
                wt = [("s", t_g), ("s", t_u)]
                if self.sg_last:
                    wt.append(("sg", self.sg_last))
                t_gu = E("vector", lambda e: e.tensor_tensor(
                    l['p_gu'][:, 0:512], l['p_chunk'][:, :], l['p_chunk2'][:, :],
                    op=OP.mult), inc=("v", 1), waits=wt)
                gu_prev = t_gu
                gu_st[buf] = self.dma_st(d['gud'][ts, tt, mcg, :, :],
                                         l['p_gu'][:, 0:512], "sg",
                                         waits=[("v", t_gu)])
                self.sg_last = gu_st[buf]
                red_m = E("vector", lambda e: e.tensor_reduce(
                    mx[:, 32 + tt * 8 + mcg:33 + tt * 8 + mcg], l['p_gu'][:, 0:512],
                    axis=AX, op=OP.max, apply_absolute_value=True), inc=("v", 1))
        self.red_last = red_m
        cm = 0
        for tt in range(4):
            cm = E("vector", lambda e: e.tensor_reduce(
                mx[:, 8 + tt:9 + tt], mx[:, 32 + tt * 8:32 + tt * 8 + 8],
                axis=AX, op=OP.max), inc=("v", 1),
                waits=[("v", red_m)] if tt == 0 else [])
        km = 0
        for tt in range(4):
            km = E("vector", lambda e: e.tensor_scalar(
                mx[:, 24 + tt:25 + tt], mx[:, 8 + tt:9 + tt], 1e-5, None, op0=OP.max),
                inc=("v", 1), waits=[("v", cm)] if tt == 0 else [])
        rm = 0
        for tt in range(4):
            rm = E("vector", lambda e: e.reciprocal(mx[:, 28 + tt:29 + tt],
                                                    mx[:, 24 + tt:25 + tt]),
                   inc=("v", 1), waits=[("v", km)] if tt == 0 else [])
        gm = 0
        for tt in range(4):
            gm = E("vector", lambda e: e.tensor_scalar(
                mx[:, 16 + tt:17 + tt], mx[:, 24 + tt:25 + tt],
                float(np.float32(1.0 / 127.0)), None, op0=OP.mult), inc=("v", 1))
        for tt in range(4):
            E("vector", lambda e: e.tensor_scalar(
                mx[:, 12 + tt:13 + tt], mx[:, 28 + tt:29 + tt], 127.0, None,
                op0=OP.mult), waits=[("v", rm)] if tt == 0 else [])
        for tt in range(4):
            srow_m = E("vector", lambda e: e.tensor_tensor(
                mx[:, 20 + tt:21 + tt], mx[:, 16 + tt:17 + tt], l['qsc'][:, 2:3],
                op=OP.mult), inc=("v", 1), waits=[("v", gm)] if tt == 0 else [])
        if POST_LIMIT == 'gateup':
            for mc in range(8):
                E("vector", lambda e: e.tensor_copy(
                    l['p_B'][:, mc * 512:(mc + 1) * 512],
                    l['p_D'][:, mc * 512:(mc + 1) * 512]), inc=("v", 1),
                    waits=[("v", red_m), ("sg", self.sg_last)])
            return
        t_tr = 0
        for tt in range(4):
            wt = [("v", srow_m)] + (self.bank_wait(6) if tt == 0 else [])
            t_tr = E("tensor", lambda e: e.transpose(
                pb[6][0:1, tt * 128:(tt + 1) * 128], mx[:, 20 + tt:21 + tt],
                l['identf'][:, :]), inc=("t", 1), waits=wt)
        rr4 = l['p_rows'][0:1, 1024:1536]
        t_cp = E("vector", lambda e: e.tensor_copy(rr4, pb[6][0:1, :]),
                 inc=("v", 1), waits=[("t", t_tr)])
        self.bankfree[6] = ("v", t_cp)
        t_bc = 0
        for tt in range(4):
            rr = l['p_rows'][0:1, 1024 + tt * 128:1024 + (tt + 1) * 128]
            wt2 = [("v", t_cp)] + (self.bank_wait(7) if tt == 0 else [])
            t_bc = E("tensor", lambda e: e.matmul(
                pb[7][:, tt * 128:(tt + 1) * 128], l['onesrow128'], rr,
                start=True, stop=True, skip_group_check=True),
                inc=("t", 1), waits=wt2)
        t_srep = E("vector", lambda e: e.tensor_copy(l['p_s2rep'][:, :], pb[7][:, :]),
                   inc=("v", 1), waits=[("t", t_bc)])
        self.bankfree[7] = ("v", t_srep)
        self.s2rep_ready = t_srep

        t_q = 0
        dn_prev = 0
        tr_prev = 0
        for tt in range(4):
            gq_done = 0
            for sub in range(2):
                wt = ([("sg", self.sg_last), ("v", self.red_last)]
                      if tt == 0 and sub == 0 else [])
                if t_q:
                    wt.append(("v", t_q))
                tl = self.dma_ld(l['p_gu'][:, :],
                                 d['gud'][ts, tt, 4 * sub:4 * sub + 4]
                                 .rearrange("m p i -> p m i"), "lgu", waits=wt)
                wt2 = [("lgu", tl), ("v", srow_m)]
                if tt == 0 and sub == 0 and self.mid_q_free:
                    wt2.append(("t", self.mid_q_free))
                if tr_prev:
                    wt2.append(("t", tr_prev))
                E("vector", lambda e: e.tensor_scalar(
                    l['p_gu'][:, :], l['p_gu'][:, :], mx[:, 12 + tt:13 + tt], MAGIC,
                    op0=OP.mult, op1=OP.add), waits=wt2)
                t_q = E("vector", lambda e: e.tensor_scalar(
                    l['p_mid'][:, sub * 2048:sub * 2048 + 2048], l['p_gu'][:, :],
                    MAGIC, None, op0=OP.subtract), inc=("v", 1))
            for g in range(8):
                bk = 4 + g % 2
                tr = 0
                for i in range(4):
                    fk = 4 * g + i
                    wt = [("v", t_q)] + (self.bank_wait(bk) if i == 0 else [])
                    tr = E("tensor", lambda e: e.transpose(
                        self.pbh[bk][:, i * 128:(i + 1) * 128],
                        l['p_mid'][:, fk * 128:(fk + 1) * 128], l['identb'][:, :]),
                        inc=("t", 1), waits=wt)
                wtc = [("t", tr)]
                if g == 0 and dn_prev:
                    wtc.append(("t", dn_prev))
                gq_done = E("scalar", lambda e: e.copy(
                    out=l['p_gqt'][:, 4 * g * 128:4 * (g + 1) * 128],
                    in_=self.pbh[bk][:, 0:512]), inc=("s", 1), waits=wtc)
                self.bankfree[bk] = ("s", gq_done)
                tr_prev = tr
            for mc in range(8):
                buf = mc % 2
                wt = [("t", self.wd_used[buf])] if self.wd_used[buf] else []
                tl = self.dma_ld(l['p_wd'][:, buf * 4096:buf * 4096 + 4096],
                                 d['wdq'][:, :, mc * 128:(mc + 1) * 128]
                                 .rearrange("k p m -> p k m"), "lwd%d" % buf, waits=wt)
                bk = mc % 2
                mm = 0
                for fk in range(32):
                    wt = []
                    if fk == 0:
                        wt = [("lwd%d" % buf, tl), ("s", gq_done)] + self.bank_wait(bk)
                    mm = E("tensor", lambda e: e.matmul(
                        pb[bk][:, 0:128],
                        l['p_wd'][:, buf * 4096 + fk * 128:buf * 4096 + (fk + 1) * 128],
                        l['p_gqt'][:, fk * 128:(fk + 1) * 128],
                        start=(fk == 0), stop=(fk == 31)), inc=("t", 1), waits=wt)
                self.wd_used[buf] = mm
                dn_prev = mm
                cbuf = l['p_chunk'] if mc % 2 == 0 else l['p_chunk2']
                t_c = E("vector", lambda e: e.tensor_tensor(
                    cbuf[:, 0:128], pb[bk][:, 0:128],
                    l['p_s2rep'][:, tt * 128:(tt + 1) * 128], op=OP.mult),
                    inc=("v", 1), waits=[("t", mm), ("v", self.s2rep_ready)])
                t_d = E("vector", lambda e: e.tensor_tensor(
                    l['p_B'][:, mc * 512 + tt * 128:mc * 512 + (tt + 1) * 128],
                    cbuf[:, 0:128],
                    l['p_D'][:, mc * 512 + tt * 128:mc * 512 + (tt + 1) * 128],
                    op=OP.add), inc=("v", 1), waits=[("v", t_c)])
                self.bankfree[bk] = ("v", t_d)
        self.mid_q_free = w.c["t"]
        self.mid_free = w.c["t"]

    def reset_sched_state(self):
        self.bankfree = [("v", 0) for _ in range(8)]
        self.rs_ready = {}
        self.rs_done = {}
        self.srow_rd = 0
        self.wq_done = 0
        self.ffw2_cast = [0, 0]
        self.wd_used = [0, 0]
        self.xbt_free = [("t", 0), ("t", 0)]
        self.stg_free = [0, 0]
        self.u_free = ("v", 0)
        self.sz_free = ("v", 0)
        self.dt_free = ("v", 0)
        self.cv_free = ("s", 0)
        self.xdbc_free = ("sa", 0)
        self.dtin_free = ("t", 0)
        self.ysz_free = ("t", 0)
        self.xp_free = 0
        self.rsy_free = 0
        self.mid_free = 0
        self.mid_q_free = 0
        self.out_st = 0
        self.pstg_free = [0, 0]
        self.qst_free = [0, 0]
        self.ps_marks = [0, 0]
        self.stg4_free = [0, 0, 0, 0]
        self.xcast = [0, 0]
        self.scan_gp_last = 0
        self.sweep_v0 = 0
        self.gu_st = [0, 0]
        self.sg_last = 0
        self.red_last = 0
        self.dbg_xbt_st = [0, 0]
        self.dbg_u_st = 0
        self.dbg_xib_st = 0

    def schedule(self):
        w = self.w
        self.ph_consts()
        self.ph_wquant(self.d['wgT'], self.d['wgq'], 8, FF, 0)
        self.ph_wquant(self.d['wuT'], self.d['wuq'], 8, FF, 1)
        self.ph_wquant(self.d['wdT'], self.d['wdq'], 32, DM, 2)
        self.wq_done = w.c["sq"]
        self.barrier()
        for dirn in range(2):
            self.ph_sweep(dirn)
            self.barrier()
        self.ph_post()
        self.barrier()
        for name, (tgt, src) in self.dumps.items():
            self.dma_st(tgt, src, "sd", waits=[("g", w.c["g"])])
        w.E("gpsimd", lambda e: e.nop(),
            waits=[(s, w.c[s]) for s in ST_SEMS if w.c[s] > 0])

    def build(self):
        nc = self.nc
        from contextlib import ExitStack
        with ExitStack() as ctx:
            sems = {n: ctx.enter_context(nc.semaphore(n))
                    for n in ("v", "s", "t", "g") + LD_SEMS + ST_SEMS}
            block = ctx.enter_context(nc.Block())

            def run(name):
                def body(eng):
                    self.w = Walk(name, eng, sems)
                    self.reset_sched_state()
                    self.schedule()
                return body

            block.sync(run("sync"))
            block.vector(run("vector"))
            block.scalar(run("scalar"))
            block.tensor(run("tensor"))
            block.gpsimd(run("gpsimd"))
        return nc


def _stage_inputs(inputs):
    f = lambda k: np.ascontiguousarray(np.asarray(inputs[k], dtype=np.float32))
    x = f('L0_attn_x')
    ident = np.eye(128, dtype=np.float32)
    toks = np.concatenate([np.arange(j * SC, j * SC + QT) for j in range(NSC)])

    pvec = np.zeros((128, 64), np.float32)
    ffb1 = f('L0_attn_ff_b1')
    ffb2 = f('L0_attn_ff_b2')
    for mc in range(16):
        pvec[:, mc] = ffb1[mc * 128:(mc + 1) * 128]
    for mc in range(8):
        pvec[:, 16 + mc] = ffb2[mc * 128:(mc + 1) * 128]
        pvec[:, 24 + mc] = f('L0_attn_fwn_w')[mc * 128:(mc + 1) * 128]
        pvec[:, 32 + mc] = f('L0_attn_fwn_b')[mc * 128:(mc + 1) * 128]
        pvec[:, 40 + mc] = f('L0_attn_fin_w')[mc * 128:(mc + 1) * 128]
        pvec[:, 48 + mc] = f('L0_attn_fin_b')[mc * 128:(mc + 1) * 128]

    com = {'ident': ident, 'pvec': pvec,
           'wgT': np.ascontiguousarray(f('L0_ffn_w_gate').T),
           'wuT': np.ascontiguousarray(f('L0_ffn_w_up').T),
           'wdT': np.ascontiguousarray(f('L0_ffn_w_down').T),
           'ffw1': f('L0_attn_ff_W1'), 'ffw2': f('L0_attn_ff_W2')}

    in_maps = []
    for b in range(2):
        xT = np.ascontiguousarray(x[b].T)
        xTf = np.ascontiguousarray(x[b, ::-1].T)
        for q in range(4):
            im = dict(com)
            im['xT'] = xT
            im['xTf'] = xTf
            im['xpost'] = np.ascontiguousarray(x[b].T[:, toks + q * QT])
            for dirn, pfx in enumerate(('fw', 'bw')):
                dn = "fb"[dirn]
                Win = f(f'L0_attn_{pfx}_Win')
                im[f'win_{dn}'] = np.ascontiguousarray(np.concatenate(
                    [Win[:, q * DQ:(q + 1) * DQ],
                     Win[:, DI + q * DQ:DI + (q + 1) * DQ]], axis=1))
                im[f'wx_{dn}'] = np.ascontiguousarray(
                    f(f'L0_attn_{pfx}_Wx')[q * DQ:(q + 1) * DQ, :])
                im[f'wdt_{dn}'] = np.ascontiguousarray(
                    f(f'L0_attn_{pfx}_Wdt')[:, q * DQ:(q + 1) * DQ])
                im[f'wout_{dn}'] = np.ascontiguousarray(
                    f(f'L0_attn_{pfx}_Wout')[q * DQ:(q + 1) * DQ, :])
                vec = np.zeros((128, 92), np.float32)
                cvw = f(f'L0_attn_{pfx}_convw')[q * DQ:(q + 1) * DQ]
                cvb = f(f'L0_attn_{pfx}_convb')[q * DQ:(q + 1) * DQ]
                bdt = f(f'L0_attn_{pfx}_bdt')[q * DQ:(q + 1) * DQ]
                Dp = f(f'L0_attn_{pfx}_D')[q * DQ:(q + 1) * DQ]
                Alog = f(f'L0_attn_{pfx}_Alog')[q * DQ:(q + 1) * DQ]
                for dc in range(NDC):
                    sl = slice(dc * 128, (dc + 1) * 128)
                    vec[:, dc * 4:(dc + 1) * 4] = cvw[sl]
                    vec[:, 16 + dc] = cvb[sl]
                    vec[:, 20 + dc] = bdt[sl]
                    vec[:, 24 + dc] = Dp[sl]
                    vec[:, 28 + dc * 16:28 + (dc + 1) * 16] = Alog[sl]
                im[f'vec_{dn}'] = vec
            in_maps.append(im)
    return in_maps, toks


_NC_CACHE = {}


def kernel(**inputs):
    if 'nc' not in _NC_CACHE:
        _NC_CACHE['nc'] = MK().build()
    nc = _NC_CACHE['nc']
    in_maps, toks = _stage_inputs(inputs)
    res = run_bass_kernel_spmd(nc, in_maps, core_ids=list(range(8)))
    out = np.zeros((2, L, DM), np.float32)
    for b in range(2):
        for q in range(4):
            out[b, toks + q * QT, :] = res.results[b * 4 + q]['out']
    return out


if __name__ == "__main__":
    mk = MK()
    nc = mk.build()
    ni = sum(1 for _ in nc.all_instructions())
    print("built, instructions:", ni)
    print("mamba peak", mk.MB_PEAK, "post peak", mk.PO_PEAK,
          "quant peak", mk.QU_PEAK, "arena", mk.A.n)


# version 14
# speedup vs baseline: 1.0305x; 1.0027x over previous
import numpy as np
import ml_dtypes
import concourse.bass as bass
import concourse.mybir as mybir
from concourse.bass_utils import run_bass_kernel_spmd

F32 = mybir.dt.float32
BF16 = mybir.dt.bfloat16
AF = mybir.ActivationFunctionType
OP = mybir.AluOpType
AX = mybir.AxisListType.X

L = 8192
DM = 1024
DI = 2048
DQ = 512
NDC = 4
DS = 16
RK = 64
SC = 1024
NSC = 8
TPC = 2048
QT = 256
FF = 4096
MAGIC = float(np.float32(12582912.0))
WCLIP = float(np.float32(1.49999988))
EPS = 1e-5
RG = [[0, 1, 2, 3], [4, 5, 6, 7]]

DEBUG_TAPS = {}
DEBUG_DUMP = []
POST_LIMIT = None
DEBUG_SWEEP = []

LD_SEMS = ("lw", "l0", "l1", "l2", "l3", "lb0", "lb1", "lc0", "lc1", "ldt",
           "lxp", "lry", "lg0", "lg1", "lu0", "lu1", "lgu", "lwd0", "lwd1")
ST_SEMS = ("sa", "sp0", "sp1", "sg", "so", "sq", "sd")
DMA_SEMS = frozenset(LD_SEMS) | frozenset(ST_SEMS)


class Walk:

    def __init__(self, name, eng, sems):
        self.name = name
        self.eng = eng
        self.sems = sems
        self.c = {k: 0 for k in sems}

    def E(self, engname, fn, inc=None, waits=()):
        if engname == self.name:
            for s, v in waits:
                if s in DMA_SEMS and v > 0:
                    v = self.c[s]
                if v > 0:
                    self.eng.wait_ge(self.sems[s], v)
            inst = fn(self.eng)
        else:
            inst = None
        if inc is not None:
            s, a = inc
            self.c[s] += a
            if inst is not None:
                inst.then_inc(self.sems[s], a)
            return self.c[s]
        return None


class Arena:
    def __init__(self, ap):
        self.base = ap
        self.n = ap.shape[1]
        self.off = 0
        self.marks = []
        self.peak = 0

    def alloc(self, cols, tag=""):
        cols = (cols + 15) // 16 * 16
        a = self.base[:, self.off:self.off + cols]
        self.off += cols
        self.peak = max(self.peak, self.off)
        assert self.off <= self.n, f"arena overflow at {tag}: {self.off}>{self.n}"
        return a

    def alloc_bf16(self, cols_bf16, tag=""):
        return self.alloc((cols_bf16 + 1) // 2, tag).bitcast(BF16)

    def push(self):
        self.marks.append(self.off)

    def pop(self):
        self.off = self.marks.pop()


class MK:
    def __init__(self):
        nc = bass.Bass("TRN2", target_bir_lowering=False)
        self.nc = nc
        d = self.d = {}
        d['xT'] = nc.dram_tensor("xT", [DM, L], F32, kind="ExternalInput").ap()
        d['xTf'] = nc.dram_tensor("xTf", [DM, L], F32, kind="ExternalInput").ap()
        for i, dn in enumerate("fb"):
            d[f'win{i}'] = nc.dram_tensor(f"win_{dn}", [DM, 2 * DQ], F32, kind="ExternalInput").ap()
            d[f'wx{i}'] = nc.dram_tensor(f"wx_{dn}", [DQ, 96], F32, kind="ExternalInput").ap()
            d[f'wdt{i}'] = nc.dram_tensor(f"wdt_{dn}", [RK, DQ], F32, kind="ExternalInput").ap()
            d[f'wout{i}'] = nc.dram_tensor(f"wout_{dn}", [DQ, DM], F32, kind="ExternalInput").ap()
            d[f'vec{i}'] = nc.dram_tensor(f"vec_{dn}", [128, 92], F32, kind="ExternalInput").ap()
        d['ident'] = nc.dram_tensor("ident", [128, 128], F32, kind="ExternalInput").ap()
        d['ffw1'] = nc.dram_tensor("ffw1", [DM, DI], F32, kind="ExternalInput").ap()
        d['ffw2'] = nc.dram_tensor("ffw2", [DI, DM], F32, kind="ExternalInput").ap()
        d['pvec'] = nc.dram_tensor("pvec", [128, 64], F32, kind="ExternalInput").ap()
        d['wgT'] = nc.dram_tensor("wgT", [DM, FF], F32, kind="ExternalInput").ap()
        d['wuT'] = nc.dram_tensor("wuT", [DM, FF], F32, kind="ExternalInput").ap()
        d['wdT'] = nc.dram_tensor("wdT", [FF, DM], F32, kind="ExternalInput").ap()
        d['xpost'] = nc.dram_tensor("xpost", [DM, TPC], F32, kind="ExternalInput").ap()
        d['out'] = nc.dram_tensor("out", [TPC, DM], F32, kind="ExternalOutput").ap()
        d['arin'] = nc.dram_tensor("arin", [2, NSC, 96, SC], F32, kind="Internal").ap()
        d['arout'] = nc.dram_tensor("arout", [2, NSC, 96, SC], F32, kind="Internal").ap()
        d['rsin'] = nc.dram_tensor("rsin", [NSC, 4, 2, 8, 128, QT], F32, kind="Internal").ap()
        d['rsout'] = nc.dram_tensor("rsout", [NSC, 2, 8, 128, QT], F32, kind="Internal").ap()
        d['wgq'] = nc.dram_tensor("wgq", [8, 128, FF], BF16, kind="Internal").ap()
        d['wuq'] = nc.dram_tensor("wuq", [8, 128, FF], BF16, kind="Internal").ap()
        d['wdq'] = nc.dram_tensor("wdq", [32, 128, DM], BF16, kind="Internal").ap()
        d['gud'] = nc.dram_tensor("gud", [4, 4, 8, 128, 512], F32, kind="Internal").ap()
        d['x1d'] = nc.dram_tensor("x1d", [4, 8, 128, 512], F32, kind="Internal").ap()
        d['attnd'] = nc.dram_tensor("attnd", [4, 8, 128, 512], F32, kind="Internal").ap()
        d['yd'] = nc.dram_tensor("yd", [4, 8, 128, 512], F32, kind="Internal").ap()
        d['qtmd'] = nc.dram_tensor("qtmd", [4, 4, 128, 1024], BF16, kind="Internal").ap()
        d['xqd'] = nc.dram_tensor("xqd", [4, 8, 128, 512], BF16, kind="Internal").ap()
        d['mxd'] = nc.dram_tensor("mxd", [4, 128, 64], F32, kind="Internal").ap()
        d['srow'] = nc.dram_tensor("srow", [128, 8], F32, kind="Internal").ap()
        if 'u' in DEBUG_SWEEP:
            d['ud'] = nc.dram_tensor("ud", [2, NSC, 128, 4096], F32,
                                     kind="ExternalOutput").ap()
        if 'xib' in DEBUG_SWEEP:
            d['xibd'] = nc.dram_tensor("xibd", [2, NSC, 128, 4160], F32,
                                       kind="ExternalOutput").ap()
        if 'xbt' in DEBUG_SWEEP:
            d['xbtd'] = nc.dram_tensor("xbtd", [2, NSC, 2, 128, 4096], BF16,
                                       kind="ExternalOutput").ap()
        self.taps = {}
        for name, shape in DEBUG_TAPS.items():
            self.taps[name] = nc.dram_tensor("tap_" + name, list(shape), F32,
                                             kind="ExternalOutput").ap()
        self.dumps = {}
        for name, key, idx in DEBUG_DUMP:
            src = d[key] if idx is None else d[key][idx]
            dt_ = src.dtype
            self.dumps[name] = (nc.dram_tensor("dump_" + name, list(src.shape), dt_,
                                               kind="ExternalOutput").ap(), src)

        ARN = 53184
        sbuf = nc.alloc_sbuf_tensor("arena", [128, ARN], F32).ap()
        A = self.A = Arena(sbuf)
        y = self.l = {}
        y['identf'] = A.alloc(128, 'identf')
        y['identb'] = A.alloc_bf16(128, 'identb')
        mis = A.alloc(16, 'mis')
        y['onescol'] = mis[:, 0:1]
        y['c127'] = mis[:, 1:2]
        orow = A.alloc(128, 'onesrow')
        y['onesrow128'] = orow[0:1, 0:128]
        y['vecs'] = A.alloc(2 * 96, 'vecs')
        y['pvec'] = A.alloc(64, 'pvec')
        y['carry'] = A.alloc(64, 'carry')
        y['qsc'] = A.alloc(16, 'qsc')
        y['rowt'] = A.alloc(16, 'rowt')

        A.push()
        y['stage'] = A.alloc(2 * 1024, 'stage')
        y['wzb'] = A.alloc_bf16(8 * 1024, 'wzb')
        y['woutb'] = A.alloc_bf16(4 * 1024, 'woutb')
        y['wxf'] = A.alloc(4 * 96, 'wxf')
        y['wdtf'] = A.alloc(4 * 128, 'wdtf')
        y['xbt'] = A.alloc_bf16(2 * 8 * 512, 'xbt')
        y['xib'] = A.alloc(NDC * 1040, 'xib')
        y['ubuf'] = A.alloc(NDC * 1024, 'ubuf')
        y['szbuf'] = A.alloc(NDC * 1024, 'szbuf')
        y['dtbuf'] = A.alloc(NDC * 1024, 'dtbuf')
        y['dtubuf'] = A.alloc(NDC * 1024, 'dtubuf')
        y['cvtmp'] = A.alloc(1024, 'cvtmp')
        y['dttmp'] = A.alloc(512, 'dttmp')
        y['abuf'] = A.alloc(2 * 1024, 'abuf')
        y['bbuf'] = A.alloc(2 * 1024, 'bbuf')
        y['hbuf'] = A.alloc(2 * 1024, 'hbuf')
        y['brep'] = A.alloc(2 * 1024, 'brep')
        y['crep'] = A.alloc(2 * 1024, 'crep')
        y['zbuf'] = A.alloc_bf16(2 * 1024, 'zbuf')
        y['y1buf'] = y['cvtmp']
        y['yszb'] = A.alloc_bf16(NDC * 1024, 'yszb')
        y['dtin'] = A.alloc(1024, 'dtin')
        y['xdbcv'] = A.alloc(1024, 'xdbcv')
        y['pstage'] = A.alloc(2 * 512, 'pstage')
        self.MB_PEAK = A.off
        A.pop()

        A.push()
        y['p_rsy'] = A.alloc(8 * 512, 'p_rsy')
        y['p_xp'] = A.alloc(8 * 512, 'p_xp')
        y['p_B'] = A.alloc(8 * 512, 'p_B')
        y['p_C'] = A.alloc(8 * 512, 'p_C')
        y['p_D'] = A.alloc(8 * 512, 'p_D')
        yb_base = A.alloc(2048, 'p_ybf')
        y['p_ybf'] = yb_base.bitcast(BF16)
        y['p_qtm'] = yb_base[:, 0:512].bitcast(BF16)
        y['p_s2rep'] = yb_base[:, 512:1024]
        y['p_outst'] = yb_base[:, 1024:2048]
        y['p_mid'] = A.alloc_bf16(16 * 512, 'p_mid')
        y['p_xq'] = A.alloc_bf16(8 * 512, 'p_xq')
        y['p_wt'] = A.alloc_bf16(2 * 1024, 'p_wt')
        y['p_ws'] = A.alloc_bf16(2 * 8 * 512, 'p_ws')
        y['p_ws2'] = A.alloc_bf16(2 * 8 * 512, 'p_ws2')
        y['p_stg'] = A.alloc(2 * 1024, 'p_stg')
        y['p_gu'] = A.alloc(2048, 'p_gu')
        y['p_gqt'] = A.alloc_bf16(32 * 128, 'p_gqt')
        y['p_wd'] = A.alloc_bf16(2 * 32 * 128, 'p_wd')
        y['p_chunk'] = A.alloc(512, 'p_chunk')
        y['p_chunk2'] = A.alloc(512, 'p_chunk2')
        y['p_rows'] = A.alloc(3 * 512, 'p_rows')
        y['p_mx'] = A.alloc(64, 'p_mx')
        self.PO_PEAK = A.off
        A.pop()

        A.push()
        y['q_st'] = A.alloc(2 * 4096, 'q_st')
        y['q_bf'] = A.alloc_bf16(2 * 4096, 'q_bf')
        y['q_acc'] = A.alloc(48, 'q_acc')
        self.QU_PEAK = A.off
        A.pop()

        self.pb = [nc.alloc_psum_tensor(f"pb{i}", [128, 512], F32).ap() for i in range(8)]
        self.pbh = [p.bitcast(BF16) for p in self.pb]

    def dma_ld(self, out, in_, sem, waits=()):
        return self.w.E("sync", lambda e: e.dma_start(out=out, in_=in_),
                        inc=(sem, 16), waits=waits)

    def dma_st(self, out, in_, sem, waits=()):
        return self.w.E("gpsimd", lambda e: e.dma_start(out=out, in_=in_),
                        inc=(sem, 16), waits=waits)

    def vgap(self):
        self.w.E("vector", lambda e: e.nop(cycle_cnt=640))

    def bank_wait(self, i):
        return [self.bankfree[i]] if self.bankfree[i][1] > 0 else []

    def barrier(self):
        w = self.w
        marks = dict(w.c)
        dma_wt = [(sn, marks[sn]) for sn in LD_SEMS + ST_SEMS if marks[sn] > 0]
        for eng in ("vector", "scalar", "tensor", "gpsimd", "sync"):
            wt = [(sn, marks[sn]) for sn in ("v", "s", "t", "g")]
            sem = {"vector": "v", "scalar": "s", "tensor": "t",
                   "gpsimd": "g", "sync": "g"}[eng]
            wt = [x for x in wt if x[0] != sem or eng in ("sync",)]
            w.E(eng, lambda e: e.nop(),
                inc=({"vector": "v", "scalar": "s", "tensor": "t"}.get(eng, "g"), 1)
                if eng in ("vector", "scalar", "tensor") else None,
                waits=wt + dma_wt)

    def tap(self, name, src):
        if name in self.taps:
            tgt = self.taps[name]
            w = self.w
            self.dma_st(tgt[0:src.shape[0], :], src, "sd",
                        waits=[("v", w.c["v"]), ("s", w.c["s"]), ("t", w.c["t"])])

    def ph_consts(self):
        w, l, d = self.w, self.l, self.d
        E = w.E
        t_id = self.dma_ld(l['identf'][:, :], d['ident'][:, :], "lw")
        t_v0 = self.dma_ld(l['vecs'][:, 0:92], d['vec0'][:, :], "lw")
        t_v1 = self.dma_ld(l['vecs'][:, 96:188], d['vec1'][:, :], "lw")
        self.t_pv = self.dma_ld(l['pvec'][:, :], d['pvec'][:, :], "lw")
        E("vector", lambda e: e.memset(l['onescol'], 1.0))
        E("vector", lambda e: e.memset(l['c127'], 127.0))
        E("vector", lambda e: e.memset(l['onesrow128'], 1.0))
        E("vector", lambda e: e.tensor_copy(l['identb'][:, :], l['identf'][:, :]),
          waits=[("lw", t_id)])
        for dirn in range(2):
            vv = l['vecs'][:, 96 * dirn + 28:96 * dirn + 92]
            t_e = E("scalar", lambda e: e.activation(out=vv, in_=vv, func=AF.Exp),
                    inc=("s", 1), waits=[("lw", max(t_v0, t_v1))])
            E("vector", lambda e: e.tensor_scalar(vv, vv, -1.0, None, op0=OP.mult),
              waits=[("s", t_e)])
        E("vector", lambda e: e.nop(), inc=("v", 1))

    def ph_wquant(self, dsrc, ddst, nchunk, width, slot):
        w, l = self.w, self.l
        E = w.E
        qst, qbf, qacc = l['q_st'], l['q_bf'], l['q_acc']
        abs_marks = []
        for c in range(nchunk):
            stg = qst[:, (c % 2) * 4096:(c % 2) * 4096 + width]
            wt = [("s", abs_marks[c - 2])] if c >= 2 else []
            if self.qst_free[c % 2]:
                wt = wt + [("v", self.qst_free[c % 2])]
            tl = self.dma_ld(stg, dsrc[c * 128:(c + 1) * 128, :], "l%d" % (c % 2),
                             waits=wt)
            m = E("scalar", lambda e: e.activation(
                out=stg, in_=stg, func=AF.Abs, accum_out=qacc[:, c:c + 1]),
                inc=("s", 1), waits=[("l%d" % (c % 2), tl)])
            abs_marks.append(m)
        t_sum = E("vector", lambda e: e.tensor_reduce(
            qacc[:, 36:37], qacc[:, 0:nchunk], axis=AX, op=OP.add),
            inc=("v", 1), waits=[("s", abs_marks[-1])])
        t_ps = E("tensor", lambda e: e.matmul(self.pb[6][0:1, 0:1], l['onescol'],
                                              qacc[:, 36:37], start=True, stop=True),
                 inc=("t", 1), waits=[("v", t_sum)] + self.bank_wait(6))
        t_mn = E("vector", lambda e: e.tensor_scalar(
            qacc[0:1, 37:38], self.pb[6][0:1, 0:1], 1.0 / (nchunk * 128 * width), 1e-5,
            op0=OP.mult, op1=OP.max), inc=("v", 1), waits=[("t", t_ps)])
        t_rep = E("tensor", lambda e: e.matmul(self.pb[6][:, 1:2], l['onesrow128'],
                                               qacc[0:1, 37:38], start=True, stop=True),
                  inc=("t", 1), waits=[("v", t_mn)])
        t_sinv = E("vector", lambda e: e.tensor_copy(l['qsc'][:, slot:slot + 1],
                                                     self.pb[6][:, 1:2]),
                   inc=("v", 1), waits=[("t", t_rep)])
        self.bankfree[6] = ("v", t_sinv)
        self.vgap()
        t_s = E("vector", lambda e: e.reciprocal(
            l['qsc'][:, 3 + slot:4 + slot], l['qsc'][:, slot:slot + 1]),
            inc=("v", 1), waits=[("v", t_sinv)])
        self.vgap()
        st_marks = []
        tq_marks = [0, 0]
        for c in range(nchunk):
            stg = qst[:, (c % 2) * 4096:(c % 2) * 4096 + width]
            qb = qbf[:, (c % 2) * 4096:(c % 2) * 4096 + width]
            wt = [("s", abs_marks[-1])]
            if c >= 2:
                wt = [("sq", st_marks[c - 2]), ("v", tq_marks[c % 2])]
            tl = self.dma_ld(stg, dsrc[c * 128:(c + 1) * 128, :], "l%d" % (c % 2),
                             waits=wt)
            f1 = E("vector", lambda e: e.tensor_scalar(
                stg, stg, l['qsc'][:, 3 + slot:4 + slot], WCLIP, op0=OP.mult, op1=OP.min),
                inc=("v", 1), waits=[("l%d" % (c % 2), tl), ("v", t_s)])
            f2 = E("vector", lambda e: e.tensor_scalar(
                stg, stg, -WCLIP, MAGIC, op0=OP.max, op1=OP.add), inc=("v", 1),
                waits=[("v", f1)])
            t_q = E("vector", lambda e: e.tensor_scalar(
                qb, stg, MAGIC, None, op0=OP.subtract), inc=("v", 1), waits=[("v", f2)])
            tq_marks[c % 2] = t_q
            sm = self.dma_st(ddst[c, :, :], qb, "sq", waits=[("v", t_q)])
            st_marks.append(sm)
        self.qst_free = list(tq_marks)

    def ph_sweep(self, dirn):
        w, l, d = self.w, self.l, self.d
        E = w.E
        xT = d['xT'] if dirn == 0 else d['xTf']
        stage = l['stage']

        cast_marks = []

        def ldcast(dst_bf, src_ap):
            buf = len(cast_marks) % 2
            stg = stage[:, buf * 1024:buf * 1024 + 1024]
            wt = [("v", self.stg_free[buf])] if self.stg_free[buf] else []
            tl = self.dma_ld(stg, src_ap, "l%d" % buf, waits=wt)
            m = E("vector", lambda e: e.tensor_copy(dst_bf, stg), inc=("v", 1),
                  waits=[("l%d" % buf, tl)])
            self.stg_free[buf] = m
            cast_marks.append(m)

        for kc in range(8):
            ldcast(l['wzb'][:, kc * 1024:(kc + 1) * 1024],
                   d[f'win{dirn}'][kc * 128:(kc + 1) * 128, :])
        for dc in range(NDC):
            ldcast(l['woutb'][:, dc * 1024:(dc + 1) * 1024],
                   d[f'wout{dirn}'][dc * 128:(dc + 1) * 128, :])
        t_wx = self.dma_ld(l['wxf'][:, 0:4 * 96],
                           d[f'wx{dirn}'][:, :].rearrange("(dc p) r -> p dc r", p=128),
                           "lw")
        t_wdt = self.dma_ld(l['wdtf'][0:RK, :], d[f'wdt{dirn}'][:, :], "lw")
        self.st_sweep = dict(t_wcast=cast_marks[-1], t_wx=t_wx, t_wdt=t_wdt)
        self.stg4_free = [self.stg_free[0], self.stg_free[0],
                          self.stg_free[1], self.stg_free[1]]

        E("vector", lambda e: e.memset(l['carry'][:, 0:64], 0.0))
        for dc in range(NDC):
            E("vector", lambda e: e.memset(
                l['xib'][:, dc * 1040 + 13:dc * 1040 + 16], 0.0))
        self.sweep_v0 = E("vector", lambda e: e.nop(), inc=("v", 1))

        self.ph_A_load(dirn, 0, xT)
        self.ph_A_xzx(dirn, 0)
        self.ph_A(dirn, 0, xT)
        for j in range(NSC):
            if j + 1 < NSC:
                self.ph_A_load(dirn, j + 1, xT)
                self.ph_A_xzx(dirn, j + 1)
            self.ph_B(dirn, j)
            if j + 1 < NSC:
                self.ph_A(dirn, j + 1, xT)

    def ph_A_load(self, dirn, j, xT):
        w, l, d = self.w, self.l, self.d
        E = w.E
        t0 = j * SC
        for ts in range(2):
            tbuf = ts % 2
            xc_marks = []
            for kc in range(8):
                q4 = kc % 4
                stg = l['stage'][:, q4 * 512:q4 * 512 + 512]
                wt = [self.xbt_free[tbuf]] if kc < 2 and self.xbt_free[tbuf][1] > 0 else []
                if kc < 2 and self.dbg_xbt_st[tbuf]:
                    wt = wt + [("sd", self.dbg_xbt_st[tbuf])]
                if self.stg4_free[q4]:
                    wt = wt + [("v", self.stg4_free[q4])]
                tl = self.dma_ld(stg, xT[kc * 128:(kc + 1) * 128,
                                         t0 + ts * 512:t0 + (ts + 1) * 512],
                                 "l%d" % q4, waits=wt)
                m = E("vector", lambda e: e.tensor_copy(
                    l['xbt'][:, (tbuf * 8 + kc) * 512:(tbuf * 8 + kc + 1) * 512], stg),
                    inc=("v", 1), waits=[("l%d" % q4, tl)])
                self.stg4_free[q4] = m
                self.stg_free[q4 // 2] = m
                xc_marks.append(m)
            self.xcast[tbuf] = xc_marks[-1]
            if 'xbt' in DEBUG_SWEEP:
                self.dbg_xbt_st[tbuf] = self.dma_st(
                    d['xbtd'][dirn, j, ts],
                    l['xbt'][:, tbuf * 4096:(tbuf + 1) * 4096], "sd",
                    waits=[("v", xc_marks[-1])])

    def ph_A_xzx(self, dirn, j):
        w, l, pb = self.w, self.l, self.pb
        E = w.E
        st = self.st_sweep
        for ts in range(2):
            tbuf = ts % 2
            t_xcast = self.xcast[tbuf]
            mm_last = 0
            for mchunk in range(4):
                bk = mchunk % 2
                for kc in range(8):
                    wt = []
                    if kc == 0:
                        wt = [("v", t_xcast), ("v", st['t_wcast'])] + self.bank_wait(bk)
                    mm_last = E("tensor", lambda e: e.matmul(
                        pb[bk][:, :],
                        l['wzb'][:, kc * 1024 + mchunk * 128:kc * 1024 + (mchunk + 1) * 128],
                        l['xbt'][:, (tbuf * 8 + kc) * 512:(tbuf * 8 + kc + 1) * 512],
                        start=(kc == 0), stop=(kc == 7)), inc=("t", 1), waits=wt)
                dc = mchunk
                wtx = [("t", mm_last)]
                if ts == 0 and mchunk == 0 and self.dbg_xib_st:
                    wtx.append(("sd", self.dbg_xib_st))
                m = E("vector", lambda e: e.tensor_copy(
                    l['xib'][:, dc * 1040 + 16 + ts * 512:dc * 1040 + 16 + (ts + 1) * 512],
                    pb[bk][:, :]), inc=("v", 1), waits=wtx)
                self.bankfree[bk] = ("v", m)

    def ph_A(self, dirn, j, xT):
        w, l, d, pb = self.w, self.l, self.d, self.pb
        E = w.E
        vo = 96 * dirn
        t0 = j * SC
        st = self.st_sweep
        u_marks = []
        sz_marks = []
        for ts in range(2):
            tbuf = ts % 2
            t_xcast = self.xcast[tbuf]
            mm_last = 0
            for mchunk in range(4, 8):
                bk = mchunk % 2
                for kc in range(8):
                    wt = []
                    if kc == 0:
                        wt = [("v", t_xcast), ("v", st['t_wcast'])] + self.bank_wait(bk)
                    mm_last = E("tensor", lambda e: e.matmul(
                        pb[bk][:, :],
                        l['wzb'][:, kc * 1024 + mchunk * 128:kc * 1024 + (mchunk + 1) * 128],
                        l['xbt'][:, (tbuf * 8 + kc) * 512:(tbuf * 8 + kc + 1) * 512],
                        start=(kc == 0), stop=(kc == 7)), inc=("t", 1), waits=wt)
                dc = mchunk - 4
                wt = [("t", mm_last)]
                if self.sz_free[1] > 0:
                    wt.append(self.sz_free)
                m = E("scalar", lambda e: e.activation(
                    out=l['szbuf'][:, dc * 1024 + ts * 512:dc * 1024 + (ts + 1) * 512],
                    in_=pb[bk][:, :], func=AF.Silu), inc=("s", 1), waits=wt)
                sz_marks.append(m)
                self.bankfree[bk] = ("s", m)
            self.xbt_free[tbuf] = ("t", mm_last)

        cw = l['vecs']
        for dc in range(NDC):
            base = dc * 1040 + 16
            wt0 = []
            if dc > 0:
                wt0 = [("s", u_marks[dc - 1])]
            elif self.cv_free[1] > 0:
                wt0 = [self.cv_free]
            t_cv = E("vector", lambda e: e.tensor_scalar(
                l['cvtmp'][:, 0:1024], l['xib'][:, base - 3:base - 3 + 1024],
                cw[:, vo + dc * 4 + 0:vo + dc * 4 + 1],
                cw[:, vo + 16 + dc:vo + 16 + dc + 1], op0=OP.mult, op1=OP.add),
                inc=("v", 1), waits=wt0)
            for k in range(1, 4):
                t_cv = E("vector", lambda e: e.scalar_tensor_tensor(
                    l['cvtmp'][:, 0:1024], l['xib'][:, base - (3 - k):base - (3 - k) + 1024],
                    cw[:, vo + dc * 4 + k:vo + dc * 4 + k + 1],
                    l['cvtmp'][:, 0:1024], op0=OP.mult, op1=OP.add), inc=("v", 1))
            wtu = [("v", t_cv)]
            if self.u_free[1] > 0 and dc == 0:
                wtu.append(self.u_free)
            if dc == 0 and self.dbg_u_st:
                wtu.append(("sd", self.dbg_u_st))
            m = E("scalar", lambda e: e.activation(
                out=l['ubuf'][:, dc * 1024:(dc + 1) * 1024], in_=l['cvtmp'][:, 0:1024],
                func=AF.Silu), inc=("s", 1), waits=wtu)
            u_marks.append(m)
            E("vector", lambda e: e.tensor_copy(
                l['xib'][:, dc * 1040 + 13:dc * 1040 + 16],
                l['xib'][:, base + 1021:base + 1024]))
        self.cv_free = ("s", u_marks[-1])
        self.u_ready = u_marks
        self.sz_ready = sz_marks[-1]
        if 'u' in DEBUG_SWEEP:
            self.dbg_u_st = self.dma_st(d['ud'][dirn, j], l['ubuf'][:, 0:4096],
                                        "sd", waits=[("s", u_marks[-1])])
        if 'xib' in DEBUG_SWEEP:
            self.dbg_xib_st = self.dma_st(d['xibd'][dirn, j], l['xib'][:, 0:4160],
                                          "sd", waits=[("s", u_marks[-1])])

        ev_marks = []
        for ts in range(2):
            mm = 0
            for dc in range(NDC):
                wt = [("s", u_marks[dc])]
                if dc == 0:
                    wt += self.bank_wait(2) + [("lw", st['t_wx'])]
                mm = E("tensor", lambda e: e.matmul(
                    pb[2][0:96, :], l['wxf'][:, dc * 96:(dc + 1) * 96],
                    l['ubuf'][:, dc * 1024 + ts * 512:dc * 1024 + (ts + 1) * 512],
                    start=(dc == 0), stop=(dc == 3)), inc=("t", 1), waits=wt)
            wt = [("t", mm)]
            if self.xdbc_free[1] > 0 and ts == 0:
                wt.append(self.xdbc_free)
            m = E("vector", lambda e: e.tensor_copy(
                l['xdbcv'][0:96, ts * 512:(ts + 1) * 512], pb[2][0:96, :]),
                inc=("v", 1), waits=wt)
            self.bankfree[2] = ("v", m)
            ev_marks.append(m)
        t_st = self.dma_st(d['arin'][dirn, j], l['xdbcv'][0:96, 0:1024], "sa",
                           waits=[("v", ev_marks[-1])])
        self.xdbc_free = ("sa", t_st)
        t_ar = E("gpsimd", lambda e: e.collective_compute(
            "AllReduce", OP.add, replica_groups=RG,
            ins=[d['arin'][dirn, j]], outs=[d['arout'][dirn, j]]),
            inc=("g", 1), waits=[("sa", t_st)])
        self.ar_done = t_ar

    def ph_B(self, dirn, j):
        w, l, d, pb = self.w, self.l, self.d, self.pb
        E = w.E
        vo = 96 * dirn
        t_ar = self.ar_done
        st = self.st_sweep

        wt = [("g", t_ar)]
        if self.dtin_free[1] > 0:
            wt.append(self.dtin_free)
        t_dtin = self.dma_ld(l['dtin'][0:RK, 0:1024], d['arout'][dirn, j][0:RK, :],
                             "ldt", waits=wt)
        dt_marks = {}
        last_mm = 0
        for dc in range(NDC):
            for ts in range(2):
                wt = [("ldt", t_dtin), ("lw", st['t_wdt'])] + self.bank_wait(3)
                if self.dt_free[1] > 0 and dc == 0 and ts == 0:
                    wt.append(self.dt_free)
                last_mm = E("tensor", lambda e: e.matmul(
                    pb[3][:, :], l['wdtf'][0:RK, dc * 128:(dc + 1) * 128],
                    l['dtin'][0:RK, ts * 512:(ts + 1) * 512],
                    start=True, stop=True), inc=("t", 1), waits=wt)
                t_e = E("scalar", lambda e: e.activation(
                    out=l['dttmp'][:, 0:512], in_=pb[3][:, :], func=AF.Exp,
                    bias=l['vecs'][:, vo + 20 + dc:vo + 20 + dc + 1]),
                    inc=("s", 1), waits=[("t", last_mm)])
                self.bankfree[3] = ("s", t_e)
                m = E("scalar", lambda e: e.activation(
                    out=l['dtbuf'][:, dc * 1024 + ts * 512:dc * 1024 + (ts + 1) * 512],
                    in_=l['dttmp'][:, 0:512], func=AF.Ln, bias=1.0), inc=("s", 1))
            dt_marks[dc] = m
        self.dtin_free = ("t", last_mm)

        dtu_marks = {}
        for dc in range(NDC):
            wt = [("s", dt_marks[dc]), ("s", self.u_ready[dc])]
            if dc == 0 and self.scan_gp_last:
                wt.append(("g", self.scan_gp_last))
            m = E("vector", lambda e: e.tensor_tensor(
                l['dtubuf'][:, dc * 1024:(dc + 1) * 1024],
                l['dtbuf'][:, dc * 1024:(dc + 1) * 1024],
                l['ubuf'][:, dc * 1024:(dc + 1) * 1024], op=OP.mult),
                inc=("v", 1), waits=wt)
            dtu_marks[dc] = m

        scan_slots = {}
        bfree = {}
        cfree = {}
        last_y1 = 0
        ysz_marks = {}
        for dcp in range(2):
            dcs = [dcp * 2, dcp * 2 + 1]
            for s in range(DS):
                slot = s % 2
                wtb = [("g", t_ar)]
                if (dcp, s - 2) in bfree:
                    wtb += bfree[(dcp, s - 2)]
                elif dcp > 0 and (0, s + DS - 2) in bfree:
                    wtb += bfree[(0, s + DS - 2)]
                t_bb = self.dma_ld(l['brep'][:, slot * 1024:(slot + 1) * 1024],
                                   d['arout'][dirn, j][RK + s:RK + s + 1, :]
                                   .partition_broadcast(128).squeeze(1),
                                   "lb%d" % slot, waits=wtb)
                wtc = []
                if (dcp, s - 2) in cfree:
                    wtc += cfree[(dcp, s - 2)]
                elif dcp > 0 and (0, s + DS - 2) in cfree:
                    wtc += cfree[(0, s + DS - 2)]
                t_cb = self.dma_ld(l['crep'][:, slot * 1024:(slot + 1) * 1024],
                                   d['arout'][dirn, j][RK + DS + s:RK + DS + s + 1, :]
                                   .partition_broadcast(128).squeeze(1),
                                   "lc%d" % slot, waits=wtc)
                tb_marks = []
                tz_marks = []
                for di, dc in enumerate(dcs):
                    aslot = di
                    cidx = s * 4 + dc
                    prev = scan_slots.get(aslot, (0, 0, 0))
                    t_a = E("scalar", lambda e: e.activation(
                        out=l['abuf'][:, aslot * 1024:(aslot + 1) * 1024],
                        in_=l['dtbuf'][:, dc * 1024:(dc + 1) * 1024], func=AF.Exp,
                        scale=l['vecs'][:, vo + 28 + dc * 16 + s:vo + 28 + dc * 16 + s + 1]),
                        inc=("s", 1),
                        waits=[("s", dt_marks[dc])] +
                              ([("v", prev[0])] if prev[0] else []))
                    t_b = E("vector", lambda e: e.tensor_tensor(
                        l['bbuf'][:, aslot * 1024:(aslot + 1) * 1024],
                        l['dtubuf'][:, dc * 1024:(dc + 1) * 1024],
                        l['brep'][:, slot * 1024:(slot + 1) * 1024], op=OP.mult),
                        inc=("v", 1),
                        waits=[("lb%d" % slot, t_bb), ("v", dtu_marks[dc])])
                    t_h = E("vector", lambda e: e.tensor_tensor_scan(
                        l['hbuf'][:, aslot * 1024:(aslot + 1) * 1024],
                        l['abuf'][:, aslot * 1024:(aslot + 1) * 1024],
                        l['bbuf'][:, aslot * 1024:(aslot + 1) * 1024],
                        l['carry'][:, cidx:cidx + 1], op0=OP.mult, op1=OP.add),
                        inc=("v", 1), waits=[("s", t_a)])
                    t_z = E("vector", lambda e: e.tensor_tensor(
                        l['zbuf'][:, aslot * 1024:(aslot + 1) * 1024],
                        l['hbuf'][:, aslot * 1024:(aslot + 1) * 1024],
                        l['crep'][:, slot * 1024:(slot + 1) * 1024], op=OP.mult),
                        inc=("v", 1),
                        waits=[("lc%d" % slot, t_cb)] +
                              ([("t", prev[2])] if prev[2] else []))
                    E("vector", lambda e: e.tensor_copy(
                        l['carry'][:, cidx:cidx + 1],
                        l['hbuf'][:, aslot * 1024 + 1023:aslot * 1024 + 1024]))
                    tb_marks.append(("v", t_b))
                    tz_marks.append(("v", t_z))
                    t_id = 0
                    for half in range(2):
                        bk = 4 + di * 2 + half
                        wt3 = [("v", t_z)]
                        if s == 0:
                            wt3 += self.bank_wait(bk)
                        t_id = E("tensor", lambda e: e.matmul(
                            pb[bk][:, :], l['identb'][:, :],
                            l['zbuf'][:, aslot * 1024 + half * 512:aslot * 1024 + (half + 1) * 512],
                            start=(s == 0), stop=(s == DS - 1),
                            skip_group_check=True), inc=("t", 1), waits=wt3)
                    scan_slots[aslot] = (t_h, t_z, t_id)
                bfree[(dcp, s)] = tb_marks
                cfree[(dcp, s)] = tz_marks
            t_idm = w.c["t"]
            for di, dc in enumerate(dcs):
                for half in range(2):
                    bk = 4 + di * 2 + half
                    wty = [("t", t_idm)]
                    if dcp == 0 and di == 0 and half == 0 and self.cv_free[1] > 0:
                        wty.append(self.cv_free)
                    last_y1 = E("vector", lambda e: e.scalar_tensor_tensor(
                        l['y1buf'][:, half * 512:(half + 1) * 512],
                        l['ubuf'][:, dc * 1024 + half * 512:dc * 1024 + (half + 1) * 512],
                        l['vecs'][:, vo + 24 + dc:vo + 24 + dc + 1],
                        pb[bk][:, :], op0=OP.mult, op1=OP.add),
                        inc=("v", 1), waits=wty)
                    self.bankfree[bk] = ("v", last_y1)
                if dirn == 0:
                    outap = l['yszb'][:, dc * 1024:(dc + 1) * 1024]
                else:
                    outap = l['yszb'][:, dc * 1024:(dc + 1) * 1024][:, ::-1]
                wt = [("s", self.sz_ready)]
                if self.ysz_free[1] > 0 and dcp == 0 and di == 0:
                    wt.append(self.ysz_free)
                ysz_marks[dc] = E("vector", lambda e: e.tensor_tensor(
                    outap, l['y1buf'][:, 0:1024],
                    l['szbuf'][:, dc * 1024:(dc + 1) * 1024], op=OP.mult),
                    inc=("v", 1), waits=wt)
        self.u_free = ("v", last_y1)
        self.sz_free = ("v", max(ysz_marks.values()))
        self.dt_free = ("v", max(dtu_marks.values()))
        self.cv_free = ("v", max(ysz_marks.values()))
        t_scan_done = max(ysz_marks.values())

        jj = j if dirn == 0 else NSC - 1 - j
        self.rs_ready.setdefault(jj, [])
        ps_marks = list(self.ps_marks)
        last_wout = 0
        for half in range(2):
            for mc in range(8):
                bk = mc % 2
                mm = 0
                for dc in range(NDC):
                    wt = []
                    if dc == 0:
                        wt = [("v", t_scan_done)] + self.bank_wait(bk)
                    mm = E("tensor", lambda e: e.matmul(
                        pb[bk][:, :],
                        l['woutb'][:, dc * 1024 + mc * 128:dc * 1024 + (mc + 1) * 128],
                        l['yszb'][:, dc * 1024 + half * 512:dc * 1024 + (half + 1) * 512],
                        start=(dc == 0), stop=(dc == 3)), inc=("t", 1), waits=wt)
                last_wout = mm
                pslot = mc % 2
                wtp = [("sp%d" % pslot, ps_marks[pslot])] if ps_marks[pslot] > 0 else []
                t_ev = E("scalar", lambda e: e.copy(
                    out=l['pstage'][:, pslot * 512:(pslot + 1) * 512], in_=pb[bk][:, :]),
                    inc=("s", 1), waits=[("t", mm)] + wtp)
                self.bankfree[bk] = ("s", t_ev)
                t_ps = self.dma_st(
                    d['rsin'][jj, 2 * half:2 * half + 2, dirn, mc, :, :]
                    .rearrange("q p i -> p q i"),
                    l['pstage'][:, pslot * 512:(pslot + 1) * 512]
                    .rearrange("p (q i) -> p q i", q=2),
                    "sp%d" % pslot, waits=[("s", t_ev)])
                ps_marks[pslot] = t_ps
        self.ps_marks = ps_marks
        self.ysz_free = ("t", last_wout)

        if dirn == 1:
            t_rs = E("gpsimd", lambda e: e.collective_compute(
                "ReduceScatter", OP.add, replica_groups=RG,
                ins=[d['rsin'][jj]], outs=[d['rsout'][jj]]),
                inc=("g", 1),
                waits=[("sp0", w.c["sp0"]), ("sp1", w.c["sp1"])])
            self.rs_done[jj] = t_rs

    def ln(self, src, dst, wcol=None, bcol=None, ready=()):
        w, l, pb = self.w, self.l, self.pb
        E = w.E
        mm_q = 0
        for mc in range(8):
            wt = list(ready) if mc == 0 else [("t", mm_q)]
            t_sq = E("scalar", lambda e: e.activation(
                out=l['p_chunk'][:, :], in_=src(mc), func=AF.Square), inc=("s", 1),
                waits=wt)
            wts = (list(ready) + self.bank_wait(2) + self.bank_wait(3)) if mc == 0 else []
            mm_s = E("tensor", lambda e: e.matmul(
                pb[2][0:1, :], l['onescol'], src(mc), start=(mc == 0), stop=(mc == 7),
                skip_group_check=True), inc=("t", 1), waits=wts)
            mm_q = E("tensor", lambda e: e.matmul(
                pb[3][0:1, :], l['onescol'], l['p_chunk'][:, :], start=(mc == 0),
                stop=(mc == 7), skip_group_check=True), inc=("t", 1), waits=[("s", t_sq)])
        r = l['p_rows']
        r_mu, r_m2, r_t = (r[0:1, i * 512:(i + 1) * 512] for i in range(3))
        r_r = r_m2
        t1 = E("vector", lambda e: e.tensor_scalar(r_mu, pb[2][0:1, :], 1.0 / DM, None,
                                                   op0=OP.mult), inc=("v", 1),
               waits=[("t", mm_q)])
        E("vector", lambda e: e.tensor_scalar(r_m2, pb[3][0:1, :], 1.0 / DM, None,
                                              op0=OP.mult))
        E("vector", lambda e: e.tensor_tensor(r_t, r_mu, r_mu, op=OP.mult))
        E("vector", lambda e: e.tensor_tensor(r_m2, r_m2, r_t, op=OP.subtract))
        t3 = E("vector", lambda e: e.tensor_scalar(r_m2, r_m2, EPS, None, op0=OP.add),
               inc=("v", 1))
        E("scalar", lambda e: e.activation(out=r_t, in_=r_m2, func=AF.Ln),
          waits=[("v", t3)])
        t5 = E("scalar", lambda e: e.activation(out=r_r, in_=r_t, func=AF.Exp, scale=-0.5),
               inc=("s", 1))
        mrep = E("tensor", lambda e: e.matmul(pb[2][:, :], l['onesrow128'], r_mu,
                                              start=True, stop=True), inc=("t", 1),
                 waits=[("v", t1)])
        rrep = E("tensor", lambda e: e.matmul(pb[3][:, :], l['onesrow128'], r_r,
                                              start=True, stop=True), inc=("t", 1),
                 waits=[("s", t5)])
        last = 0
        for mc in range(8):
            E("vector", lambda e: e.tensor_tensor(
                l['p_chunk'][:, :], src(mc), pb[2][:, :], op=OP.subtract),
              waits=([("t", rrep)] if mc == 0 else []))
            E("vector", lambda e: e.tensor_tensor(
                l['p_chunk'][:, :], l['p_chunk'][:, :], pb[3][:, :], op=OP.mult))
            if wcol is not None:
                last = E("vector", lambda e: e.tensor_scalar(
                    dst(mc), l['p_chunk'][:, :], wcol(mc), bcol(mc),
                    op0=OP.mult, op1=OP.add), inc=("v", 1))
            else:
                last = E("vector", lambda e: e.tensor_copy(dst(mc), l['p_chunk'][:, :]),
                         inc=("v", 1))
        self.bankfree[2] = ("v", last)
        self.bankfree[3] = ("v", last)
        return last

    def ph_post(self):
        for ts in range(4):
            self.post_ts(ts)

    def post_ts(self, ts):
        w, l, d, pb = self.w, self.l, self.d, self.pb
        E = w.E
        pv = l['pvec']
        CH = lambda buf, mc: l[buf][:, mc * 512:(mc + 1) * 512]
        j0, j1 = 2 * ts, 2 * ts + 1
        t_xp = self.dma_ld(l['p_xp'][:, :],
                           d['xpost'][:, ts * 512:(ts + 1) * 512]
                           .rearrange("(kc p) t -> p kc t", p=128),
                           "lxp", waits=[("v", self.xp_free)] if self.xp_free else [])
        t_sum = 0
        for dirn in range(2):
            ld_m = 0
            for mc in range(8):
                for jb, j in enumerate((j0, j1)):
                    wt = []
                    if mc == 0 and jb == 0:
                        wt = [("g", self.rs_done[j0]), ("g", self.rs_done[j1])]
                        if self.rsy_free:
                            wt.append(("v", self.rsy_free))
                    ld_m = self.dma_ld(
                        l['p_rsy'][:, mc * 512 + jb * 256:mc * 512 + (jb + 1) * 256],
                        d['rsout'][j, dirn, mc, :, :], "lry", waits=wt)
            add_m = 0
            for mc in range(8):
                wt = [("lry", ld_m), ("lxp", t_xp)] if mc == 0 else []
                add_m = E("vector", lambda e: e.tensor_tensor(
                    CH('p_B', mc), CH('p_rsy', mc), CH('p_xp', mc), op=OP.add),
                    inc=("v", 1), waits=wt)
            self.rsy_free = add_m
            dstbuf = 'p_C' if dirn == 0 else 'p_B'
            self.ln(lambda mc: CH('p_B', mc), lambda mc: CH(dstbuf, mc),
                    wcol=lambda mc: pv[:, 24 + mc:25 + mc],
                    bcol=lambda mc: pv[:, 32 + mc:33 + mc],
                    ready=[("v", add_m), ("lw", self.t_pv)])
            if dirn == 1:
                for mc in range(8):
                    t_sum = E("vector", lambda e: e.tensor_tensor(
                        CH('p_C', mc), CH('p_C', mc), CH('p_B', mc), op=OP.add),
                        inc=("v", 1))
                if DEBUG_DUMP:
                    for mc in range(8):
                        self.dma_st(d['yd'][ts, mc], CH('p_C', mc), "sd",
                                    waits=[("v", t_sum)])

        cast_m = 0
        for mc in range(8):
            wt = []
            if mc == 0:
                wt = [("v", t_sum)]
                if self.out_st:
                    wt.append(("so", self.out_st))
                if self.mid_q_free:
                    wt.append(("t", self.mid_q_free))
            cast_m = E("vector", lambda e: e.tensor_copy(
                l['p_ybf'][:, mc * 512:(mc + 1) * 512], CH('p_C', mc)), inc=("v", 1),
                waits=wt)
        mid_m = 0
        for mc in range(16):
            buf = mc % 2
            stg = l['p_stg'][:, buf * 1024:(buf + 1) * 1024]
            wt = [("v", self.pstg_free[buf])] if self.pstg_free[buf] else []
            tl = self.dma_ld(stg, d['ffw1'][:, mc * 128:(mc + 1) * 128]
                             .rearrange("(kc p) m -> p kc m", p=128),
                             "l%d" % buf, waits=wt)
            cst = E("vector", lambda e: e.tensor_copy(
                l['p_wt'][:, buf * 1024:(buf + 1) * 1024], stg), inc=("v", 1),
                waits=[("l%d" % buf, tl)])
            self.pstg_free[buf] = cst
            bk = mc % 2
            mm = 0
            for kc in range(8):
                wt = []
                if kc == 0:
                    wt = [("v", cast_m), ("v", cst)] + self.bank_wait(bk)
                mm = E("tensor", lambda e: e.matmul(
                    pb[bk][:, :],
                    l['p_wt'][:, buf * 1024 + kc * 128:buf * 1024 + (kc + 1) * 128],
                    l['p_ybf'][:, kc * 512:(kc + 1) * 512],
                    start=(kc == 0), stop=(kc == 7)), inc=("t", 1), waits=wt)
            wt = [("t", mm), ("lw", self.t_pv)]
            if self.mid_free and mc < 2:
                wt.append(("t", self.mid_free))
            mid_m = E("scalar", lambda e: e.activation(
                out=l['p_mid'][:, mc * 512:(mc + 1) * 512], in_=pb[bk][:, :],
                func=AF.Relu, bias=pv[:, mc:mc + 1]), inc=("s", 1), waits=wt)
            self.bankfree[bk] = ("s", mid_m)
        ff2_last = 0
        for mc in range(8):
            buf = mc % 2
            cst = 0
            for hf in range(2):
                stg = l['p_stg'][:, buf * 1024:(buf + 1) * 1024]
                wt = [("v", self.pstg_free[buf])] if self.pstg_free[buf] else []
                tl = self.dma_ld(stg, d['ffw2'][hf * 1024:(hf + 1) * 1024,
                                                mc * 128:(mc + 1) * 128]
                                 .rearrange("(kc p) m -> p kc m", p=128),
                                 "l%d" % buf, waits=wt)
                cst = E("vector", lambda e: e.tensor_copy(
                    l['p_ws2'][:, buf * 4096 + hf * 1024:buf * 4096 + (hf + 1) * 1024],
                    stg), inc=("v", 1), waits=[("l%d" % buf, tl)])
                self.pstg_free[buf] = cst
                self.ffw2_cast[buf] = cst
            bk = 2 + mc % 2
            mm = 0
            for kc in range(16):
                wt = []
                if kc == 0:
                    wt = [("v", cst), ("s", mid_m)] + self.bank_wait(bk)
                mm = E("tensor", lambda e: e.matmul(
                    pb[bk][:, :],
                    l['p_ws2'][:, buf * 4096 + kc * 128:buf * 4096 + (kc + 1) * 128],
                    l['p_mid'][:, kc * 512:(kc + 1) * 512],
                    start=(kc == 0), stop=(kc == 15)), inc=("t", 1), waits=wt)
            ff2_last = E("vector", lambda e: e.scalar_tensor_tensor(
                CH('p_B', mc), pb[bk][:, :], pv[:, 16 + mc:17 + mc], CH('p_C', mc),
                op0=OP.add, op1=OP.add), inc=("v", 1), waits=[("t", mm)])
            self.bankfree[bk] = ("v", ff2_last)
        self.mid_free = w.c["t"]
        t_fin = self.ln(lambda mc: CH('p_B', mc), lambda mc: CH('p_B', mc),
                        wcol=lambda mc: pv[:, 40 + mc:41 + mc],
                        bcol=lambda mc: pv[:, 48 + mc:49 + mc],
                        ready=[("v", ff2_last)])
        if DEBUG_DUMP:
            for mc in range(8):
                self.dma_st(d['attnd'][ts, mc], CH('p_B', mc), "sd",
                            waits=[("v", t_fin)])
        t_add = 0
        for mc in range(8):
            t_add = E("vector", lambda e: e.tensor_tensor(
                CH('p_D', mc), CH('p_xp', mc), CH('p_B', mc), op=OP.add), inc=("v", 1))
        self.xp_free = t_add
        self.x1_ready = self.ln(lambda mc: CH('p_D', mc), lambda mc: CH('p_D', mc),
                                ready=[("v", t_add)])
        if DEBUG_DUMP:
            for mc in range(8):
                self.dma_st(d['x1d'][ts, mc], CH('p_D', mc), "sd",
                            waits=[("v", self.x1_ready)])
            self.barrier()

        if POST_LIMIT == 'ff':
            for mc in range(8):
                E("vector", lambda e: e.tensor_copy(CH('p_B', mc), CH('p_D', mc)))
            E("vector", lambda e: e.nop(), inc=("v", 1))
        else:
            self.post_ffn(ts)

        t_fln = self.ln(lambda mc: CH('p_B', mc), lambda mc: CH('p_B', mc),
                        ready=[("v", w.c["v"])])
        for tt in range(4):
            cp = 0
            for g in range(2):
                bk = 4 + g
                tr = 0
                for i in range(4):
                    mc = 4 * g + i
                    wt = self.bank_wait(bk) if i == 0 else []
                    if tt == 0 and g == 0 and i == 0:
                        wt = wt + [("v", t_fln)]
                    tr = E("tensor", lambda e: e.transpose(
                        pb[bk][:, i * 128:(i + 1) * 128],
                        l['p_B'][:, mc * 512 + tt * 128:mc * 512 + tt * 128 + 128],
                        l['identf'][:, :]), inc=("t", 1), waits=wt)
                wtc = [("t", tr)]
                if g == 0 and self.out_st:
                    wtc.append(("so", self.out_st))
                cp = E("vector", lambda e: e.tensor_copy(
                    l['p_outst'][:, g * 512:(g + 1) * 512], pb[bk][:, :]),
                    inc=("v", 1), waits=wtc)
                self.bankfree[bk] = ("v", cp)
            self.out_st = self.dma_st(
                d['out'][ts * 512 + tt * 128:ts * 512 + (tt + 1) * 128, :],
                l['p_outst'][:, :], "so", waits=[("v", cp)])

    def post_ffn(self, ts):
        w, l, d, pb = self.w, self.l, self.d, self.pb
        E = w.E
        mx = l['p_mx']
        xq_m = 0
        for tt in range(4):
            trm = 0
            for mc in range(8):
                bk = 4 + mc // 4
                wt = self.bank_wait(bk) if mc % 4 == 0 else []
                if tt == 0 and mc == 0:
                    wt = wt + [("v", self.x1_ready)]
                trm = E("tensor", lambda e: e.transpose(
                    pb[bk][:, (mc % 4) * 128:(mc % 4 + 1) * 128],
                    l['p_D'][:, mc * 512 + tt * 128:mc * 512 + tt * 128 + 128],
                    l['identf'][:, :]), inc=("t", 1), waits=wt)
            c1, c2 = mx[:, 20:21], mx[:, 21:22]
            mxc, rc, c2t, sc = mx[:, 16:17], mx[:, 17:18], mx[:, 18:19], mx[:, 19:20]
            E("vector", lambda e: e.tensor_reduce(c1, pb[4][:, :], axis=AX, op=OP.max,
                                                  apply_absolute_value=True),
              waits=[("t", trm)])
            r2 = E("vector", lambda e: e.tensor_reduce(c2, pb[5][:, :], axis=AX,
                                                       op=OP.max,
                                                       apply_absolute_value=True),
                   inc=("v", 1))
            m1 = E("vector", lambda e: e.scalar_tensor_tensor(
                mxc, c1, 1e-5, c2, op0=OP.max, op1=OP.max), inc=("v", 1),
                waits=[("v", r2)])
            rc1 = E("vector", lambda e: e.reciprocal(rc, mxc), inc=("v", 1),
                    waits=[("v", m1)])
            ct1 = E("vector", lambda e: e.tensor_scalar(
                c2t, mxc, float(np.float32(1.0 / 127.0)), None, op0=OP.mult),
                inc=("v", 1))
            sc1 = E("vector", lambda e: e.tensor_scalar(sc, rc, 127.0, None,
                                                        op0=OP.mult), inc=("v", 1),
                    waits=[("v", rc1)])
            E("vector", lambda e: e.tensor_tensor(
                mx[:, tt:tt + 1], c2t, l['qsc'][:, 0:1], op=OP.mult),
              waits=[("v", ct1)])
            t_sc = E("vector", lambda e: e.tensor_tensor(
                mx[:, 4 + tt:5 + tt], c2t, l['qsc'][:, 1:2], op=OP.mult), inc=("v", 1))
            qm = 0
            for hb, bk in enumerate((4, 5)):
                E("vector", lambda e: e.tensor_scalar(
                    l['p_chunk'][:, :], pb[bk][:, :], sc, MAGIC, op0=OP.mult, op1=OP.add),
                  waits=[("v", t_sc), ("v", sc1)])
                qm = E("vector", lambda e: e.tensor_scalar(
                    l['p_qtm'][:, hb * 512:(hb + 1) * 512], l['p_chunk'][:, :],
                    MAGIC, None, op0=OP.subtract), inc=("v", 1))
            self.bankfree[4] = ("v", qm)
            self.bankfree[5] = ("v", qm)
            if DEBUG_DUMP:
                dm_ = self.dma_st(d['qtmd'][ts, tt], l['p_qtm'][:, :], waits=[("v", qm)])
                self.barrier()
            for g in range(2):
                bk = 6 + g
                tr2 = 0
                for i in range(4):
                    mc = 4 * g + i
                    wt = [("v", qm)] + (self.bank_wait(bk) if i == 0 else [])
                    tr2 = E("tensor", lambda e: e.transpose(
                        self.pbh[bk][:, i * 128:(i + 1) * 128],
                        l['p_qtm'][:, mc * 128:(mc + 1) * 128], l['identb'][:, :]),
                        inc=("t", 1), waits=wt)
                for i in range(4):
                    mc = 4 * g + i
                    xq_m = E("scalar", lambda e: e.copy(
                        out=l['p_xq'][:, mc * 512 + tt * 128:mc * 512 + tt * 128 + 128],
                        in_=self.pbh[bk][:, i * 128:(i + 1) * 128]), inc=("s", 1),
                        waits=[("t", tr2)] if i == 0 else [])
                self.bankfree[bk] = ("s", xq_m)

        self.sc_ready = w.c["v"]
        if POST_LIMIT == 'quant':
            for mc in range(8):
                E("vector", lambda e: e.tensor_copy(
                    l['p_B'][:, mc * 512:(mc + 1) * 512],
                    l['p_D'][:, mc * 512:(mc + 1) * 512]), inc=("v", 1),
                    waits=[("s", xq_m)])
            return
        if DEBUG_DUMP:
            for mc in range(8):
                self.dma_st(d['xqd'][ts, mc],
                            l['p_xq'][:, mc * 512:(mc + 1) * 512], "sd",
                            waits=[("s", xq_m)])
            self.dma_st(d['mxd'][ts], mx[:, :], "sd", waits=[("s", xq_m)])
            self.barrier()
        gu_st = self.gu_st
        gu_prev = 0
        red_m = 0
        for mcg in range(8):
            buf = mcg % 2
            wt = [("sq", self.wq_done)]
            if gu_st[buf]:
                wt.append(("sg", gu_st[buf]))
            tg = self.dma_ld(l['p_ws'][:, buf * 4096:buf * 4096 + 4096],
                             d['wgq'][:, :, mcg * 512:(mcg + 1) * 512]
                             .rearrange("k p m -> p k m"), "lg%d" % buf, waits=wt)
            wtu = [("v", self.ffw2_cast[buf])] if self.ffw2_cast[buf] else []
            if gu_st[buf]:
                wtu.append(("sg", gu_st[buf]))
            tu = self.dma_ld(l['p_ws2'][:, buf * 4096:buf * 4096 + 4096],
                             d['wuq'][:, :, mcg * 512:(mcg + 1) * 512]
                             .rearrange("k p m -> p k m"), "lu%d" % buf, waits=wtu)
            if mcg < 2:
                self.ffw2_cast[buf] = 0
            for tt in range(4):
                mmg = 0
                for kc in range(8):
                    wt = []
                    if kc == 0:
                        wt = [("lg%d" % buf, tg), ("s", xq_m)] + self.bank_wait(0)
                    mmg = E("tensor", lambda e: e.matmul(
                        pb[0][:, :],
                        l['p_xq'][:, kc * 512 + tt * 128:kc * 512 + tt * 128 + 128],
                        l['p_ws'][:, buf * 4096 + kc * 512:buf * 4096 + (kc + 1) * 512],
                        start=(kc == 0), stop=(kc == 7)), inc=("t", 1), waits=wt)
                wtg = [("t", mmg)]
                if gu_prev:
                    wtg.append(("v", gu_prev))
                wtg.append(("v", self.sc_ready))
                t_g = E("scalar", lambda e: e.activation(
                    out=l['p_chunk'][:, :], in_=pb[0][:, :], func=AF.Silu,
                    scale=mx[:, tt:tt + 1]), inc=("s", 1), waits=wtg)
                self.bankfree[0] = ("s", t_g)
                mmu = 0
                for kc in range(8):
                    wt = []
                    if kc == 0:
                        wt = [("lu%d" % buf, tu)] + self.bank_wait(1)
                    mmu = E("tensor", lambda e: e.matmul(
                        pb[1][:, :],
                        l['p_xq'][:, kc * 512 + tt * 128:kc * 512 + tt * 128 + 128],
                        l['p_ws2'][:, buf * 4096 + kc * 512:buf * 4096 + (kc + 1) * 512],
                        start=(kc == 0), stop=(kc == 7)), inc=("t", 1), waits=wt)
                t_u = E("scalar", lambda e: e.activation(
                    out=l['p_chunk2'][:, :], in_=pb[1][:, :], func=AF.Identity,
                    scale=mx[:, 4 + tt:5 + tt]), inc=("s", 1),
                    waits=[("t", mmu), ("v", self.sc_ready)])
                self.bankfree[1] = ("s", t_u)
                wt = [("s", t_g), ("s", t_u)]
                if self.sg_last:
                    wt.append(("sg", self.sg_last))
                t_gu = E("vector", lambda e: e.tensor_tensor(
                    l['p_gu'][:, 0:512], l['p_chunk'][:, :], l['p_chunk2'][:, :],
                    op=OP.mult), inc=("v", 1), waits=wt)
                gu_prev = t_gu
                gu_st[buf] = self.dma_st(d['gud'][ts, tt, mcg, :, :],
                                         l['p_gu'][:, 0:512], "sg",
                                         waits=[("v", t_gu)])
                self.sg_last = gu_st[buf]
                red_m = E("vector", lambda e: e.tensor_reduce(
                    mx[:, 32 + tt * 8 + mcg:33 + tt * 8 + mcg], l['p_gu'][:, 0:512],
                    axis=AX, op=OP.max, apply_absolute_value=True), inc=("v", 1))
        self.red_last = red_m
        cm = 0
        for tt in range(4):
            cm = E("vector", lambda e: e.tensor_reduce(
                mx[:, 8 + tt:9 + tt], mx[:, 32 + tt * 8:32 + tt * 8 + 8],
                axis=AX, op=OP.max), inc=("v", 1),
                waits=[("v", red_m)] if tt == 0 else [])
        km = 0
        for tt in range(4):
            km = E("vector", lambda e: e.tensor_scalar(
                mx[:, 24 + tt:25 + tt], mx[:, 8 + tt:9 + tt], 1e-5, None, op0=OP.max),
                inc=("v", 1), waits=[("v", cm)] if tt == 0 else [])
        rm = 0
        for tt in range(4):
            rm = E("vector", lambda e: e.reciprocal(mx[:, 28 + tt:29 + tt],
                                                    mx[:, 24 + tt:25 + tt]),
                   inc=("v", 1), waits=[("v", km)] if tt == 0 else [])
        gm = 0
        for tt in range(4):
            gm = E("vector", lambda e: e.tensor_scalar(
                mx[:, 16 + tt:17 + tt], mx[:, 24 + tt:25 + tt],
                float(np.float32(1.0 / 127.0)), None, op0=OP.mult), inc=("v", 1))
        for tt in range(4):
            E("vector", lambda e: e.tensor_scalar(
                mx[:, 12 + tt:13 + tt], mx[:, 28 + tt:29 + tt], 127.0, None,
                op0=OP.mult), waits=[("v", rm)] if tt == 0 else [])
        for tt in range(4):
            srow_m = E("vector", lambda e: e.tensor_tensor(
                mx[:, 20 + tt:21 + tt], mx[:, 16 + tt:17 + tt], l['qsc'][:, 2:3],
                op=OP.mult), inc=("v", 1), waits=[("v", gm)] if tt == 0 else [])
        if POST_LIMIT == 'gateup':
            for mc in range(8):
                E("vector", lambda e: e.tensor_copy(
                    l['p_B'][:, mc * 512:(mc + 1) * 512],
                    l['p_D'][:, mc * 512:(mc + 1) * 512]), inc=("v", 1),
                    waits=[("v", red_m), ("sg", self.sg_last)])
            return
        t_tr = 0
        for tt in range(4):
            wt = [("v", srow_m)] + (self.bank_wait(6) if tt == 0 else [])
            t_tr = E("tensor", lambda e: e.transpose(
                pb[6][0:1, tt * 128:(tt + 1) * 128], mx[:, 20 + tt:21 + tt],
                l['identf'][:, :]), inc=("t", 1), waits=wt)
        rr4 = l['p_rows'][0:1, 1024:1536]
        t_cp = E("vector", lambda e: e.tensor_copy(rr4, pb[6][0:1, :]),
                 inc=("v", 1), waits=[("t", t_tr)])
        self.bankfree[6] = ("v", t_cp)
        t_bc = 0
        for tt in range(4):
            rr = l['p_rows'][0:1, 1024 + tt * 128:1024 + (tt + 1) * 128]
            wt2 = [("v", t_cp)] + (self.bank_wait(7) if tt == 0 else [])
            t_bc = E("tensor", lambda e: e.matmul(
                pb[7][:, tt * 128:(tt + 1) * 128], l['onesrow128'], rr,
                start=True, stop=True, skip_group_check=True),
                inc=("t", 1), waits=wt2)
        t_srep = E("vector", lambda e: e.tensor_copy(l['p_s2rep'][:, :], pb[7][:, :]),
                   inc=("v", 1), waits=[("t", t_bc)])
        self.bankfree[7] = ("v", t_srep)
        self.s2rep_ready = t_srep

        t_q = 0
        dn_prev = 0
        tr_prev = 0
        for tt in range(4):
            gq_done = 0
            for sub in range(2):
                wt = ([("sg", self.sg_last), ("v", self.red_last)]
                      if tt == 0 and sub == 0 else [])
                if t_q:
                    wt.append(("v", t_q))
                tl = self.dma_ld(l['p_gu'][:, :],
                                 d['gud'][ts, tt, 4 * sub:4 * sub + 4]
                                 .rearrange("m p i -> p m i"), "lgu", waits=wt)
                wt2 = [("lgu", tl), ("v", srow_m)]
                if tt == 0 and sub == 0 and self.mid_q_free:
                    wt2.append(("t", self.mid_q_free))
                if tr_prev:
                    wt2.append(("t", tr_prev))
                E("vector", lambda e: e.tensor_scalar(
                    l['p_gu'][:, :], l['p_gu'][:, :], mx[:, 12 + tt:13 + tt], MAGIC,
                    op0=OP.mult, op1=OP.add), waits=wt2)
                t_q = E("vector", lambda e: e.tensor_scalar(
                    l['p_mid'][:, sub * 2048:sub * 2048 + 2048], l['p_gu'][:, :],
                    MAGIC, None, op0=OP.subtract), inc=("v", 1))
            for g in range(8):
                bk = 4 + g % 2
                tr = 0
                for i in range(4):
                    fk = 4 * g + i
                    wt = [("v", t_q)] + (self.bank_wait(bk) if i == 0 else [])
                    tr = E("tensor", lambda e: e.transpose(
                        self.pbh[bk][:, i * 128:(i + 1) * 128],
                        l['p_mid'][:, fk * 128:(fk + 1) * 128], l['identb'][:, :]),
                        inc=("t", 1), waits=wt)
                wtc = [("t", tr)]
                if g == 0 and dn_prev:
                    wtc.append(("t", dn_prev))
                gq_done = E("scalar", lambda e: e.copy(
                    out=l['p_gqt'][:, 4 * g * 128:4 * (g + 1) * 128],
                    in_=self.pbh[bk][:, 0:512]), inc=("s", 1), waits=wtc)
                self.bankfree[bk] = ("s", gq_done)
                tr_prev = tr
            for mc in range(8):
                buf = mc % 2
                wt = [("t", self.wd_used[buf])] if self.wd_used[buf] else []
                tl = self.dma_ld(l['p_wd'][:, buf * 4096:buf * 4096 + 4096],
                                 d['wdq'][:, :, mc * 128:(mc + 1) * 128]
                                 .rearrange("k p m -> p k m"), "lwd%d" % buf, waits=wt)
                bk = mc % 2
                mm = 0
                for fk in range(32):
                    wt = []
                    if fk == 0:
                        wt = [("lwd%d" % buf, tl), ("s", gq_done)] + self.bank_wait(bk)
                    mm = E("tensor", lambda e: e.matmul(
                        pb[bk][:, 0:128],
                        l['p_wd'][:, buf * 4096 + fk * 128:buf * 4096 + (fk + 1) * 128],
                        l['p_gqt'][:, fk * 128:(fk + 1) * 128],
                        start=(fk == 0), stop=(fk == 31)), inc=("t", 1), waits=wt)
                self.wd_used[buf] = mm
                dn_prev = mm
                cbuf = l['p_chunk'] if mc % 2 == 0 else l['p_chunk2']
                t_c = E("vector", lambda e: e.tensor_tensor(
                    cbuf[:, 0:128], pb[bk][:, 0:128],
                    l['p_s2rep'][:, tt * 128:(tt + 1) * 128], op=OP.mult),
                    inc=("v", 1), waits=[("t", mm), ("v", self.s2rep_ready)])
                t_d = E("vector", lambda e: e.tensor_tensor(
                    l['p_B'][:, mc * 512 + tt * 128:mc * 512 + (tt + 1) * 128],
                    cbuf[:, 0:128],
                    l['p_D'][:, mc * 512 + tt * 128:mc * 512 + (tt + 1) * 128],
                    op=OP.add), inc=("v", 1), waits=[("v", t_c)])
                self.bankfree[bk] = ("v", t_d)
        self.mid_q_free = w.c["t"]
        self.mid_free = w.c["t"]

    def reset_sched_state(self):
        self.bankfree = [("v", 0) for _ in range(8)]
        self.rs_ready = {}
        self.rs_done = {}
        self.srow_rd = 0
        self.wq_done = 0
        self.ffw2_cast = [0, 0]
        self.wd_used = [0, 0]
        self.xbt_free = [("t", 0), ("t", 0)]
        self.stg_free = [0, 0]
        self.u_free = ("v", 0)
        self.sz_free = ("v", 0)
        self.dt_free = ("v", 0)
        self.cv_free = ("s", 0)
        self.xdbc_free = ("sa", 0)
        self.dtin_free = ("t", 0)
        self.ysz_free = ("t", 0)
        self.xp_free = 0
        self.rsy_free = 0
        self.mid_free = 0
        self.mid_q_free = 0
        self.out_st = 0
        self.pstg_free = [0, 0]
        self.qst_free = [0, 0]
        self.ps_marks = [0, 0]
        self.stg4_free = [0, 0, 0, 0]
        self.xcast = [0, 0]
        self.scan_gp_last = 0
        self.sweep_v0 = 0
        self.gu_st = [0, 0]
        self.sg_last = 0
        self.red_last = 0
        self.dbg_xbt_st = [0, 0]
        self.dbg_u_st = 0
        self.dbg_xib_st = 0

    def schedule(self):
        w = self.w
        self.ph_consts()
        self.ph_wquant(self.d['wgT'], self.d['wgq'], 8, FF, 0)
        self.ph_wquant(self.d['wuT'], self.d['wuq'], 8, FF, 1)
        self.ph_wquant(self.d['wdT'], self.d['wdq'], 32, DM, 2)
        self.wq_done = w.c["sq"]
        self.barrier()
        for dirn in range(2):
            self.ph_sweep(dirn)
            self.barrier()
        self.ph_post()
        self.barrier()
        for name, (tgt, src) in self.dumps.items():
            self.dma_st(tgt, src, "sd", waits=[("g", w.c["g"])])
        w.E("gpsimd", lambda e: e.nop(),
            waits=[(s, w.c[s]) for s in ST_SEMS if w.c[s] > 0])

    def build(self):
        nc = self.nc
        from contextlib import ExitStack
        with ExitStack() as ctx:
            sems = {n: ctx.enter_context(nc.semaphore(n))
                    for n in ("v", "s", "t", "g") + LD_SEMS + ST_SEMS}
            block = ctx.enter_context(nc.Block())

            def run(name):
                def body(eng):
                    self.w = Walk(name, eng, sems)
                    self.reset_sched_state()
                    self.schedule()
                return body

            block.sync(run("sync"))
            block.vector(run("vector"))
            block.scalar(run("scalar"))
            block.tensor(run("tensor"))
            block.gpsimd(run("gpsimd"))
        return nc


def _stage_inputs(inputs):
    f = lambda k: np.ascontiguousarray(np.asarray(inputs[k], dtype=np.float32))
    x = f('L0_attn_x')
    ident = np.eye(128, dtype=np.float32)
    toks = np.concatenate([np.arange(j * SC, j * SC + QT) for j in range(NSC)])

    pvec = np.zeros((128, 64), np.float32)
    ffb1 = f('L0_attn_ff_b1')
    ffb2 = f('L0_attn_ff_b2')
    for mc in range(16):
        pvec[:, mc] = ffb1[mc * 128:(mc + 1) * 128]
    for mc in range(8):
        pvec[:, 16 + mc] = ffb2[mc * 128:(mc + 1) * 128]
        pvec[:, 24 + mc] = f('L0_attn_fwn_w')[mc * 128:(mc + 1) * 128]
        pvec[:, 32 + mc] = f('L0_attn_fwn_b')[mc * 128:(mc + 1) * 128]
        pvec[:, 40 + mc] = f('L0_attn_fin_w')[mc * 128:(mc + 1) * 128]
        pvec[:, 48 + mc] = f('L0_attn_fin_b')[mc * 128:(mc + 1) * 128]

    com = {'ident': ident, 'pvec': pvec,
           'wgT': np.ascontiguousarray(f('L0_ffn_w_gate').T),
           'wuT': np.ascontiguousarray(f('L0_ffn_w_up').T),
           'wdT': np.ascontiguousarray(f('L0_ffn_w_down').T),
           'ffw1': f('L0_attn_ff_W1'), 'ffw2': f('L0_attn_ff_W2')}

    in_maps = []
    for b in range(2):
        xT = np.ascontiguousarray(x[b].T)
        xTf = np.ascontiguousarray(x[b, ::-1].T)
        for q in range(4):
            im = dict(com)
            im['xT'] = xT
            im['xTf'] = xTf
            im['xpost'] = np.ascontiguousarray(x[b].T[:, toks + q * QT])
            for dirn, pfx in enumerate(('fw', 'bw')):
                dn = "fb"[dirn]
                Win = f(f'L0_attn_{pfx}_Win')
                im[f'win_{dn}'] = np.ascontiguousarray(np.concatenate(
                    [Win[:, q * DQ:(q + 1) * DQ],
                     Win[:, DI + q * DQ:DI + (q + 1) * DQ]], axis=1))
                im[f'wx_{dn}'] = np.ascontiguousarray(
                    f(f'L0_attn_{pfx}_Wx')[q * DQ:(q + 1) * DQ, :])
                im[f'wdt_{dn}'] = np.ascontiguousarray(
                    f(f'L0_attn_{pfx}_Wdt')[:, q * DQ:(q + 1) * DQ])
                im[f'wout_{dn}'] = np.ascontiguousarray(
                    f(f'L0_attn_{pfx}_Wout')[q * DQ:(q + 1) * DQ, :])
                vec = np.zeros((128, 92), np.float32)
                cvw = f(f'L0_attn_{pfx}_convw')[q * DQ:(q + 1) * DQ]
                cvb = f(f'L0_attn_{pfx}_convb')[q * DQ:(q + 1) * DQ]
                bdt = f(f'L0_attn_{pfx}_bdt')[q * DQ:(q + 1) * DQ]
                Dp = f(f'L0_attn_{pfx}_D')[q * DQ:(q + 1) * DQ]
                Alog = f(f'L0_attn_{pfx}_Alog')[q * DQ:(q + 1) * DQ]
                for dc in range(NDC):
                    sl = slice(dc * 128, (dc + 1) * 128)
                    vec[:, dc * 4:(dc + 1) * 4] = cvw[sl]
                    vec[:, 16 + dc] = cvb[sl]
                    vec[:, 20 + dc] = bdt[sl]
                    vec[:, 24 + dc] = Dp[sl]
                    vec[:, 28 + dc * 16:28 + (dc + 1) * 16] = Alog[sl]
                im[f'vec_{dn}'] = vec
            in_maps.append(im)
    return in_maps, toks


_NC_CACHE = {}


def kernel(**inputs):
    if 'nc' not in _NC_CACHE:
        _NC_CACHE['nc'] = MK().build()
    nc = _NC_CACHE['nc']
    in_maps, toks = _stage_inputs(inputs)
    res = run_bass_kernel_spmd(nc, in_maps, core_ids=list(range(8)))
    out = np.zeros((2, L, DM), np.float32)
    for b in range(2):
        for q in range(4):
            out[b, toks + q * QT, :] = res.results[b * 4 + q]['out']
    return out


if __name__ == "__main__":
    mk = MK()
    nc = mk.build()
    ni = sum(1 for _ in nc.all_instructions())
    print("built, instructions:", ni)
    print("mamba peak", mk.MB_PEAK, "post peak", mk.PO_PEAK,
          "quant peak", mk.QU_PEAK, "arena", mk.A.n)
